# version 01
# baseline (speedup 1.0000x reference)
import numpy as np
from contextlib import ExitStack
import concourse.bass as bass
import concourse.mybir as mybir
from concourse.bass_utils import run_bass_kernel_spmd

F32 = mybir.dt.float32
BF16 = mybir.dt.bfloat16
AF = mybir.ActivationFunctionType
OP = mybir.AluOpType
AX = mybir.AxisListType

T = 4096
D = 1024
NT = T // 128
NC = D // 128
EPS_LN = 1e-5

N_LOAD_SLOTS = 12
N_STORE_SLOTS = 4


class Slot:

    def __init__(self, mk, name):
        self.mk = mk
        self.name = name
        self.guard = None
        self.extra_guard = None

    def dma(self, eng, out, in_):
        mk = self.mk
        if self.guard is not None:
            eng.wait_ge(mk.sems[self.guard[0]], self.guard[1])
            self.guard = None
        if self.extra_guard is not None:
            eng.wait_ge(mk.sems[self.extra_guard[0]], self.extra_guard[1])
            self.extra_guard = None
        mk.mark(eng.dma_start(out=out, in_=in_), self.name, 16)
        return mk.cnt[self.name]

    def consumed(self, sem_name):
        self.guard = (sem_name, self.mk.cnt[sem_name])


class MK:
    def __init__(self):
        self.nc = bass.Bass("TRN2", target_bir_lowering=False)
        self.stack = ExitStack()
        self.cnt = {}
        self.sems = {}
        self.block = None
        self.drains = True

    def start(self):
        self.block = self.stack.enter_context(self.nc.Block())
        names = ["CN", "Tm", "Vm", "Sm", "Gm", "PH"]
        names += ["L%d" % i for i in range(N_LOAD_SLOTS)]
        names += ["St%d" % i for i in range(N_STORE_SLOTS)]
        for n in names:
            self.sems[n] = self.stack.enter_context(self.nc.semaphore(n))
            self.cnt[n] = 0
        self.lslots = [Slot(self, "L%d" % i) for i in range(N_LOAD_SLOTS)]
        self.sslots = [Slot(self, "St%d" % i) for i in range(N_STORE_SLOTS)]

    def sb(self, stk, name, shape, dt=F32):
        return stk.enter_context(self.nc.sbuf_tensor(name, shape, dt))

    def psum(self, stk, name, shape, dt=F32):
        return stk.enter_context(self.nc.psum_tensor(name, shape, dt))

    def t(self, fn):
        self.block.tensor(fn)

    def v(self, fn):
        self.block.vector(fn)

    def s(self, fn):
        self.block.scalar(fn)

    def g(self, fn):
        self.block.gpsimd(fn)

    def y(self, fn):
        self.block.sync(fn)

    def mark(self, instr, sem, k=1):
        self.cnt[sem] += k
        instr.then_inc(self.sems[sem], k)
        return self.cnt[sem]

    def w(self, eng, sem, val=None):
        eng.wait_ge(self.sems[sem], self.cnt[sem] if val is None else val)

    def dr(self, eng):
        if self.drains:
            eng.drain()

    def barrier(self):
        mk = self
        tgt = {k: self.cnt[k] for k in
               (["Tm", "Sm", "Gm", "CN"] + ["St%d" % i for i in range(N_STORE_SLOTS)])}

        def vf(eng):
            for k, v in tgt.items():
                if v:
                    eng.wait_ge(mk.sems[k], v)
            mk.mark(eng.engine_nop(), "PH")
        self.v(vf)
        ph = self.cnt["PH"]

        def wf(eng):
            eng.wait_ge(mk.sems["PH"], ph)
        for e in (self.t, self.s, self.g, self.y):
            e(wf)






def emit_ln_stats(mk, stk, tag, x_hbm, sums, sqs, slots, ntiles=NT):
    nbuf = len(slots)
    xb = mk.sb(stk, tag + "_xb", [128, nbuf, D], F32)
    junk = mk.sb(stk, tag + "_junk", [128, D], F32)
    for i in range(ntiles):
        b = i % nbuf
        sl = slots[b]
        qv = [None]

        def yf(eng, i=i, b=b, sl=sl, qv=qv):
            qv[0] = sl.dma(eng, xb[:, b, :], x_hbm[i * 128:(i + 1) * 128, :])
        mk.y(yf)

        def vf(eng, i=i, b=b, sl=sl, qv=qv):
            eng.wait_ge(mk.sems[sl.name], qv[0])
            mk.mark(eng.reduce_sum(sums[:, i:i + 1], xb[:, b, :], axis=AX.X), "Vm")
        mk.v(vf)
        vmark = mk.cnt["Vm"]

        def sf(eng, i=i, b=b, sl=sl, qv=qv):
            eng.wait_ge(mk.sems[sl.name], qv[0])
            mk.dr(eng)
            mk.mark(eng.activation(junk[:], xb[:, b, :], AF.Square,
                                   accum_out=sqs[:, i:i + 1]), "Sm")
        mk.s(sf)
        sl.guard = ("Sm", mk.cnt["Sm"])
        sl.extra_guard = ("Vm", vmark)




def emit_stats_finalize(mk, stk, tag, sums, sqs, n, eps, rstd, negb, ntiles=NT):
    mean = mk.sb(stk, tag + "_mean", [128, ntiles], F32)
    msq = mk.sb(stk, tag + "_msq", [128, ntiles], F32)
    var = mk.sb(stk, tag + "_var", [128, ntiles], F32)

    def vf(eng):
        mk.w(eng, "Sm")
        mk.dr(eng)
        eng.tensor_scalar(mean[:], sums[:], 1.0 / n, None, OP.mult)
        mk.dr(eng)
        eng.tensor_tensor(msq[:], mean[:], mean[:], OP.mult)
        eng.tensor_scalar(var[:], sqs[:], 1.0 / n, eps, OP.mult, OP.add)
        mk.dr(eng)
        eng.tensor_tensor(var[:], var[:], msq[:], OP.subtract)
        mk.dr(eng)
        mk.mark(eng.reciprocal(var[:], var[:]), "Vm")
    mk.v(vf)

    def sf(eng):
        mk.w(eng, "Vm")
        mk.mark(eng.activation(rstd[:], var[:], AF.Sqrt), "Sm")
    mk.s(sf)

    def vf2(eng):
        mk.w(eng, "Sm")
        mk.dr(eng)
        mk.mark(eng.scalar_tensor_tensor(negb[:], mean[:], -1.0, rstd[:],
                                         OP.mult, OP.mult), "Vm")
    mk.v(vf2)


def emit_load_cast(mk, stage, tag, w_hbm, dst_bf, rows, cols, slots):
    nchunk = rows // 128
    for j in range(nchunk):
        b = j % 2
        sl = slots[b]
        qv = [None]

        def yf(eng, j=j, b=b, sl=sl, qv=qv):
            qv[0] = sl.dma(eng, stage[:, b, 0:cols], w_hbm[j * 128:(j + 1) * 128, :])
        mk.y(yf)

        def vf(eng, j=j, b=b, sl=sl, qv=qv):
            eng.wait_ge(mk.sems[sl.name], qv[0])
            mk.mark(eng.tensor_copy(dst_bf[:, j, :], stage[:, b, 0:cols]), "Vm")
        mk.v(vf)
        sl.consumed("Vm")


def emit_residual_update(mk, stk, tag, x_in, mix_hbm, x_out, sums, sqs,
                         lslots, sslots, mix_store_marks):
    rstd = mk.sb(stk, tag + "_rstd", [128, NT], F32)
    negb = mk.sb(stk, tag + "_negb", [128, NT], F32)
    emit_stats_finalize(mk, stk, tag + "_fin", sums, sqs, D, EPS_LN, rstd, negb)

    xb = mk.sb(stk, tag + "_uxb", [128, 3, D], F32)
    mb = mk.sb(stk, tag + "_umb", [128, 3, D], F32)
    ob = mk.sb(stk, tag + "_uob", [128, 3, D], F32)
    lntmp = mk.sb(stk, tag + "_ulnt", [128, 2, D], F32)
    for i in range(NT):
        b = i % 3
        slx, slm, sls = lslots[b], lslots[3 + b], sslots[b]
        qv = [None, None]

        def yf(eng, i=i, b=b, slx=slx, slm=slm, qv=qv):
            ms = mix_store_marks.get(i)
            if ms is not None:
                eng.wait_ge(mk.sems[ms[0]], ms[1])
            qv[0] = slx.dma(eng, xb[:, b, :], x_in[i * 128:(i + 1) * 128, :])
            qv[1] = slm.dma(eng, mb[:, b, :], mix_hbm[i * 128:(i + 1) * 128, :])
        mk.y(yf)

        def sf(eng, i=i, b=b, slm=slm, qv=qv):
            eng.wait_ge(mk.sems[slm.name], qv[1])
            mk.w(eng, "Vm")
            mk.mark(eng.activation(lntmp[:, i % 2, :], mb[:, b, :], AF.Identity,
                                   bias=negb[:, i:i + 1], scale=rstd[:, i:i + 1]), "Sm")
        mk.s(sf)
        slm.consumed("Sm")

        def vf(eng, i=i, b=b, slx=slx, sls=sls, qv=qv):
            eng.wait_ge(mk.sems[slx.name], qv[0])
            mk.w(eng, "Sm")
            if sls.guard is not None:
                eng.wait_ge(mk.sems[sls.name], sls.guard[1])
            mk.mark(eng.tensor_tensor(ob[:, b, :], xb[:, b, :],
                                      lntmp[:, i % 2, :], OP.add), "Vm")
        mk.v(vf)
        slx.consumed("Vm")
        vv = mk.cnt["Vm"]

        def yg(eng, i=i, b=b, sls=sls, vv=vv):
            eng.wait_ge(mk.sems["Vm"], vv)
            mk.mark(eng.dma_start(out=x_out[i * 128:(i + 1) * 128, :],
                                  in_=ob[:, b, :]), sls.name, 16)
        mk.g(yg)
        sls.guard = (sls.name, mk.cnt[sls.name])



def phase_l1ffn(mk, x_in, x_out, W, mixbuf):
    LS = mk.lslots
    SS = mk.sslots
    with ExitStack() as stk:
        wrb = mk.sb(stk, "d_wrb", [128, NC, D], BF16)
        wkb = mk.sb(stk, "d_wkb", [128, NC, D], BF16)
        wvb = mk.sb(stk, "d_wvb", [128, NC, D], BF16)
        mur = mk.sb(stk, "d_mur", [128, NC], F32)
        muk = mk.sb(stk, "d_muk", [128, NC], F32)
        murm = mk.sb(stk, "d_murm", [128, NC], F32)
        mukm = mk.sb(stk, "d_mukm", [128, NC], F32)
        idf = mk.sb(stk, "d_idf", [128, 128], F32)
        wstage = mk.sb(stk, "d_wstg", [128, 2, D], F32)

        lnsum = mk.sb(stk, "d_lnsum", [128, NT], F32)
        lnsq = mk.sb(stk, "d_lnsq", [128, NT], F32)
        rstd = mk.sb(stk, "d_rstd", [128, NT], F32)
        negb = mk.sb(stk, "d_negb", [128, NT], F32)
        mxsum = mk.sb(stk, "d_mxsum", [128, NT], F32)
        mxsq = mk.sb(stk, "d_mxsq", [128, NT], F32)

        def yf(eng):
            mk.mark(eng.dma_start(out=mur[:, :], in_=W["mur_s"][:, :]), "CN", 16)
            mk.mark(eng.dma_start(out=muk[:, :], in_=W["muk_s"][:, :]), "CN", 16)
            mk.mark(eng.dma_start(out=idf[:, :], in_=W["ident"][:, :]), "CN", 16)
        mk.y(yf)
        cn0 = mk.cnt["CN"]

        def vf0(eng):
            eng.wait_ge(mk.sems["CN"], cn0)
            eng.tensor_scalar(murm[:], mur[:], -1.0, 1.0, OP.mult, OP.add)
            mk.mark(eng.tensor_scalar(mukm[:], muk[:], -1.0, 1.0, OP.mult, OP.add), "Vm")
        mk.v(vf0)
        emit_load_cast(mk, wstage, "d_wr", W["wr_t"], wrb, D, D, LS[4:6])
        emit_load_cast(mk, wstage, "d_wk", W["wk_t"], wkb, D, D, LS[4:6])
        emit_load_cast(mk, wstage, "d_wv", W["wv_t"], wvb, D, D, LS[4:6])

        emit_ln_stats(mk, stk, "d_ln", x_in, lnsum, lnsq, LS[0:4])
        emit_stats_finalize(mk, stk, "d_lnf", lnsum, lnsq, D, EPS_LN, rstd, negb)

        mix_store_marks = {}
        with ExitStack() as stk2:
            xin2 = mk.sb(stk2, "d_xin2", [128, 4, D], F32)
            ztmp = mk.sb(stk2, "d_ztmp", [128, 2, D], F32)
            zcm = mk.sb(stk2, "d_zcm", [128, NC, 513], BF16)
            xrk = mk.sb(stk2, "d_xrk", [128, NC, 512], BF16)
            rcm = mk.sb(stk2, "d_rcm", [128, NC, 512], F32)
            kact = mk.sb(stk2, "d_kact", [128, NC, 512], BF16)
            ktmp = mk.sb(stk2, "d_ktmp", [128, 512], F32)
            junk2 = mk.sb(stk2, "d_junk2", [128, D], F32)
            mixcm = mk.sb(stk2, "d_mixcm", [128, NC, 512], F32)
            mixtok = mk.sb(stk2, "d_mixtok", [128, 2, D], F32)
            ptr = mk.psum(stk2, "d_ptr", [128, 2, 512], F32)
            prk = mk.psum(stk2, "d_prk", [128, 4, 512], F32)
            pvo = mk.psum(stk2, "d_pvo", [128, 2, 512], F32)

            def vz(eng):
                mk.mark(eng.memset(zcm[:, :, 0:1], 0.0), "Vm")
            mk.v(vz)

            for g in range(8):
                tiles = list(range(4 * g, 4 * g + 4))
                qvs = {}
                for ii, i in enumerate(tiles):
                    sl = LS[6 + ii]

                    def yf(eng, i=i, ii=ii, sl=sl):
                        qvs[ii] = sl.dma(eng, xin2[:, ii, :], x_in[i * 128:(i + 1) * 128, :])
                    mk.y(yf)

                if g > 0:
                    def vb(eng):
                        mk.dr(eng)
                        mk.mark(eng.tensor_copy(zcm[:, :, 0:1], zcm[:, :, 512:513]), "Vm")
                    mk.v(vb)

                for ii, i in enumerate(tiles):
                    sl = LS[6 + ii]

                    def sf(eng, i=i, ii=ii, sl=sl):
                        eng.wait_ge(mk.sems[sl.name], qvs[ii])
                        mk.w(eng, "Vm")
                        mk.w(eng, "Tm")
                        mk.mark(eng.activation(ztmp[:, ii % 2, :], xin2[:, ii, :],
                                               AF.Identity, bias=negb[:, i:i + 1],
                                               scale=rstd[:, i:i + 1]), "Sm")
                    mk.s(sf)
                    sl.consumed("Sm")

                    def tf(eng, i=i, ii=ii):
                        mk.w(eng, "Sm")
                        mk.w(eng, "Vm")
                        last = None
                        for j in range(NC):
                            last = eng.transpose(
                                ptr[:, j % 2, (j // 2) * 128:(j // 2) * 128 + 128],
                                ztmp[:, ii % 2, j * 128:(j + 1) * 128], idf[:])
                        mk.mark(last, "Tm")
                    mk.t(tf)

                    def vf(eng, i=i, ii=ii):
                        mk.w(eng, "Tm")
                        mk.dr(eng)
                        c0 = 1 + ii * 128
                        last = None
                        for j in range(NC):
                            last = eng.tensor_copy(
                                zcm[:, j, c0:c0 + 128],
                                ptr[:, j % 2, (j // 2) * 128:(j // 2) * 128 + 128])
                        mk.mark(last, "Vm")
                    mk.v(vf)

                def vxr(eng):
                    mk.w(eng, "Tm")
                    last = None
                    for j in range(NC):
                        mk.dr(eng)
                        eng.tensor_scalar(ktmp[:], zcm[:, j, 1:513], mur[:, j:j + 1],
                                          None, OP.mult)
                        mk.dr(eng)
                        last = eng.scalar_tensor_tensor(
                            xrk[:, j, :], zcm[:, j, 0:512], murm[:, j:j + 1],
                            ktmp[:], OP.mult, OP.add)
                    mk.mark(last, "Vm")
                mk.v(vxr)
                xrdone = mk.cnt["Vm"]

                for i in range(NC):
                    def tf(eng, i=i, xrdone=xrdone):
                        if i == 0:
                            mk.w(eng, "Vm", xrdone)
                            mk.w(eng, "Sm")
                        if i >= 2:
                            mk.w(eng, "Sm")
                        last = None
                        for j in range(NC):
                            last = eng.matmul(prk[:, i % 2, :],
                                              wrb[:, j, i * 128:(i + 1) * 128],
                                              xrk[:, j, :],
                                              start=(j == 0), stop=(j == NC - 1))
                        mk.mark(last, "Tm")
                    mk.t(tf)

                    def sf(eng, i=i):
                        mk.w(eng, "Tm")
                        mk.mark(eng.activation(rcm[:, i, :], prk[:, i % 2, :],
                                               AF.Sigmoid), "Sm")
                    mk.s(sf)
                rdone_s = mk.cnt["Sm"]

                def vxk(eng):
                    mk.w(eng, "Tm")
                    last = None
                    for j in range(NC):
                        mk.dr(eng)
                        eng.tensor_scalar(ktmp[:], zcm[:, j, 1:513], muk[:, j:j + 1],
                                          None, OP.mult)
                        mk.dr(eng)
                        last = eng.scalar_tensor_tensor(
                            xrk[:, j, :], zcm[:, j, 0:512], mukm[:, j:j + 1],
                            ktmp[:], OP.mult, OP.add)
                    mk.mark(last, "Vm")
                mk.v(vxk)

                for i in range(NC):
                    def tf(eng, i=i):
                        mk.w(eng, "Vm")
                        last = None
                        for j in range(NC):
                            last = eng.matmul(prk[:, 2 + i % 2, :],
                                              wkb[:, j, i * 128:(i + 1) * 128],
                                              xrk[:, j, :],
                                              start=(j == 0), stop=(j == NC - 1))
                        mk.mark(last, "Tm")
                    mk.t(tf)

                    def vf(eng, i=i):
                        mk.w(eng, "Tm")
                        mk.dr(eng)
                        eng.tensor_scalar(ktmp[:], prk[:, 2 + i % 2, :], 0.0, None, OP.max)
                        mk.dr(eng)
                        mk.mark(eng.tensor_tensor(kact[:, i, :], ktmp[:], ktmp[:],
                                                  OP.mult), "Vm")
                    mk.v(vf)

                for i in range(NC):
                    def tf(eng, i=i):
                        mk.w(eng, "Vm")
                        last = None
                        for j in range(NC):
                            last = eng.matmul(pvo[:, i % 2, :],
                                              wvb[:, j, i * 128:(i + 1) * 128],
                                              kact[:, j, :],
                                              start=(j == 0), stop=(j == NC - 1))
                        mk.mark(last, "Tm")
                    mk.t(tf)

                    def vf(eng, i=i, rdone_s=rdone_s):
                        mk.w(eng, "Tm")
                        eng.wait_ge(mk.sems["Sm"], rdone_s)
                        mk.mark(eng.tensor_tensor(mixcm[:, i, :], rcm[:, i, :],
                                                  pvo[:, i % 2, :], OP.mult), "Vm")
                    mk.v(vf)
                mixdone = mk.cnt["Vm"]

                for ii, i in enumerate(tiles):
                    sls = SS[i % 2]

                    def tf(eng, i=i, ii=ii, mixdone=mixdone):
                        if ii == 0:
                            mk.w(eng, "Vm", mixdone)
                        else:
                            mk.w(eng, "Vm")
                        last = None
                        for j in range(NC):
                            last = eng.transpose(
                                ptr[:, j % 2, (j // 2) * 128:(j // 2) * 128 + 128],
                                mixcm[:, j, ii * 128:(ii + 1) * 128], idf[:])
                        mk.mark(last, "Tm")
                    mk.t(tf)

                    def vf(eng, i=i, ii=ii, sls=sls):
                        mk.w(eng, "Tm")
                        if sls.guard is not None:
                            eng.wait_ge(mk.sems[sls.name], sls.guard[1])
                        if i >= 2:
                            mk.w(eng, "Sm")
                        for j in range(NC):
                            eng.tensor_copy(
                                mixtok[:, ii % 2, j * 128:(j + 1) * 128],
                                ptr[:, j % 2, (j // 2) * 128:(j // 2) * 128 + 128])
                        mk.dr(eng)
                        mk.mark(eng.reduce_sum(mxsum[:, i:i + 1], mixtok[:, ii % 2, :],
                                               axis=AX.X), "Vm")
                    mk.v(vf)
                    vv = mk.cnt["Vm"]

                    def sq(eng, i=i, ii=ii, vv=vv):
                        eng.wait_ge(mk.sems["Vm"], vv)
                        mk.dr(eng)
                        mk.mark(eng.activation(junk2[:], mixtok[:, ii % 2, :], AF.Square,
                                               accum_out=mxsq[:, i:i + 1]), "Sm")
                    mk.s(sq)

                    def yg(eng, i=i, ii=ii, sls=sls, vv=vv):
                        eng.wait_ge(mk.sems["Vm"], vv)
                        mk.mark(eng.dma_start(out=mixbuf[i * 128:(i + 1) * 128, :],
                                              in_=mixtok[:, ii % 2, :]), sls.name, 16)
                    mk.g(yg)
                    sls.guard = (sls.name, mk.cnt[sls.name])
                    mix_store_marks[i] = (sls.name, mk.cnt[sls.name])

        mk.barrier()
        with ExitStack() as stk3:
            emit_residual_update(mk, stk3, "d_up", x_in, mixbuf, x_out,
                                 mxsum, mxsq, LS[0:6], SS[0:3], mix_store_marks)


def prep_l1ffn(W_r, W_k, W_v, mu_r, mu_k):
    return {
        "wr_t": np.ascontiguousarray(W_r.T),
        "wk_t": np.ascontiguousarray(W_k.T),
        "wv_t": np.ascontiguousarray(W_v.T),
        "mur_s": np.ascontiguousarray(mu_r.reshape(NC, 128).T),
        "muk_s": np.ascontiguousarray(mu_k.reshape(NC, 128).T),
    }




def phase_l1attn(mk, x_in, x_out, W, mixbuf, qk_hbm, attn_hbm):
    LS = mk.lslots
    SS = mk.sslots
    H = 16
    with ExitStack() as stk:
        idf = mk.sb(stk, "c_idf", [128, 128], F32)
        ones = mk.sb(stk, "c_ones", [128, 128], F32)
        lnsum = mk.sb(stk, "c_lnsum", [128, NT], F32)
        lnsq = mk.sb(stk, "c_lnsq", [128, NT], F32)
        rstd = mk.sb(stk, "c_rstd", [128, NT], F32)
        negb = mk.sb(stk, "c_negb", [128, NT], F32)
        mxsum = mk.sb(stk, "c_mxsum", [128, NT], F32)
        mxsq = mk.sb(stk, "c_mxsq", [128, NT], F32)
        vres = mk.sb(stk, "c_vres", [128, NT, H, 65], BF16)

        def yf(eng):
            mk.mark(eng.dma_start(out=idf[:, :], in_=W["ident"][:, :]), "CN", 16)
            mk.mark(eng.dma_start(out=ones[:, :], in_=W["ones"][:, :]), "CN", 16)
        mk.y(yf)
        cn0 = mk.cnt["CN"]

        def vf0(eng):
            eng.wait_ge(mk.sems["CN"], cn0)
            mk.mark(eng.memset(vres[:, :, :, 64:65], 1.0), "Vm")
        mk.v(vf0)

        with ExitStack() as stk1:
            emit_ln_stats(mk, stk1, "c_ln", x_in, lnsum, lnsq, LS[0:4])
            emit_stats_finalize(mk, stk1, "c_lnf", lnsum, lnsq, D, EPS_LN, rstd, negb)
            wqkb = mk.sb(stk1, "c_wqkb", [128, NC, 2048], BF16)
            wvb = mk.sb(stk1, "c_wvb", [128, NC, D], BF16)
            wstage = mk.sb(stk1, "c_wstg", [128, 2, 2048], F32)
            emit_load_cast(mk, wstage, "c_wqk", W["wqk_t"], wqkb, D, 2048, LS[4:6])
            emit_load_cast(mk, wstage, "c_wv", W["wv_t"], wvb, D, D, LS[4:6])

            xin1 = mk.sb(stk1, "c_xin1", [128, 4, D], F32)
            ztmp = mk.sb(stk1, "c_ztmp", [128, 2, D], F32)
            zchunk = mk.sb(stk1, "c_zch", [128, NC, 512], BF16)
            qkstage = mk.sb(stk1, "c_qkst", [128, 2, 512], BF16)
            ptr = mk.psum(stk1, "c_ptr", [128, 8, 128], F32)
            pqk = mk.psum(stk1, "c_pqk", [128, 2, 512], F32)
            pv = mk.psum(stk1, "c_pv", [128, 2, 512], F32)

            for g in range(8):
                tiles = list(range(4 * g, 4 * g + 4))
                qvs = {}
                for ii, i in enumerate(tiles):
                    sl = LS[6 + ii]

                    def yf(eng, i=i, ii=ii, sl=sl):
                        qvs[ii] = sl.dma(eng, xin1[:, ii, :],
                                         x_in[i * 128:(i + 1) * 128, :])
                    mk.y(yf)

                for ii, i in enumerate(tiles):
                    sl = LS[6 + ii]

                    def sf(eng, i=i, ii=ii, sl=sl):
                        eng.wait_ge(mk.sems[sl.name], qvs[ii])
                        mk.w(eng, "Vm")
                        mk.w(eng, "Tm")
                        mk.mark(eng.activation(ztmp[:, ii % 2, :], xin1[:, ii, :],
                                               AF.Identity, bias=negb[:, i:i + 1],
                                               scale=rstd[:, i:i + 1]), "Sm")
                    mk.s(sf)
                    sl.consumed("Sm")

                    def tf(eng, i=i, ii=ii):
                        mk.w(eng, "Sm")
                        mk.w(eng, "Vm")
                        last = None
                        for j in range(NC):
                            last = eng.transpose(ptr[:, j, :],
                                                 ztmp[:, ii % 2, j * 128:(j + 1) * 128],
                                                 idf[:])
                        mk.mark(last, "Tm")
                    mk.t(tf)

                    def vf(eng, i=i, ii=ii):
                        mk.w(eng, "Tm")
                        mk.dr(eng)
                        mk.mark(eng.tensor_copy(
                            zchunk[:, :, ii * 128:(ii + 1) * 128], ptr[:, :, :]), "Vm")
                    mk.v(vf)
                zdone = mk.cnt["Vm"]

                for i in range(16):
                    sls = SS[i % 2]

                    def tf(eng, i=i, zdone=zdone):
                        if i == 0:
                            mk.w(eng, "Vm", zdone)
                            mk.w(eng, "Sm")
                        if i >= 2:
                            mk.w(eng, "Sm")
                        last = None
                        for j in range(NC):
                            last = eng.matmul(pqk[:, i % 2, :],
                                              wqkb[:, j, i * 128:(i + 1) * 128],
                                              zchunk[:, j, :],
                                              start=(j == 0), stop=(j == NC - 1))
                        mk.mark(last, "Tm")
                    mk.t(tf)

                    def sf(eng, i=i, sls=sls):
                        mk.w(eng, "Tm")
                        if sls.guard is not None:
                            eng.wait_ge(mk.sems[sls.name], sls.guard[1])
                            sls.guard = None
                        mk.dr(eng)
                        mk.mark(eng.copy(qkstage[:, i % 2, :], pqk[:, i % 2, :]), "Sm")
                    mk.s(sf)
                    sv = mk.cnt["Sm"]

                    def yg(eng, i=i, g=g, sls=sls, sv=sv):
                        eng.wait_ge(mk.sems["Sm"], sv)
                        mk.mark(eng.dma_start(
                            out=qk_hbm[i * 128:(i + 1) * 128, g * 512:(g + 1) * 512],
                            in_=qkstage[:, i % 2, :]), sls.name, 16)
                    mk.g(yg)
                    sls.guard = (sls.name, mk.cnt[sls.name])

                for ii, i in enumerate(tiles):
                    for half in range(2):
                        def tf(eng, ii=ii, half=half, zdone=zdone):
                            mk.w(eng, "Vm")
                            last = None
                            for j in range(NC):
                                last = eng.matmul(
                                    pv[:, half, :],
                                    zchunk[:, j, ii * 128:(ii + 1) * 128],
                                    wvb[:, j, half * 512:(half + 1) * 512],
                                    start=(j == 0), stop=(j == NC - 1))
                            mk.mark(last, "Tm")
                        mk.t(tf)

                        def vf(eng, i=i, half=half):
                            mk.w(eng, "Tm")
                            mk.dr(eng)
                            mk.mark(eng.tensor_copy(
                                vres[:, i, half * 8:(half + 1) * 8, 0:64],
                                pv[:, half, :].rearrange("p (h d) -> p h d", d=64)),
                                "Vm")
                        mk.v(vf)

        mk.barrier()

        with ExitStack() as stk2:
            atile = mk.sb(stk2, "c_atile", [64, 2, 512], BF16)
            qh = mk.sb(stk2, "c_qh", [64, T], BF16)
            kh = mk.sb(stk2, "c_kh", [64, T], BF16)
            pbuf = mk.sb(stk2, "c_pbuf", [128, 2, 2, 512], BF16)
            qm = mk.sb(stk2, "c_qm", [64, 2], F32)
            mstat = mk.sb(stk2, "c_mst", [128, 4], F32)
            recb = mk.sb(stk2, "c_recb", [64, 512], F32)
            rec1 = mk.sb(stk2, "c_rec1", [1, 512], F32)
            psA = mk.psum(stk2, "c_psA", [128, 2, 512], F32)
            psB = mk.psum(stk2, "c_psB", [128, 2, 512], F32)
            poh = mk.psum(stk2, "c_poh", [128, 512], F32)
            pdot = mk.psum(stk2, "c_pdot", [128, 512], F32)

            for h in range(H):
                slq, slk = LS[8], LS[9]
                qv = [None, None]

                def yf(eng, h=h, qv=qv):
                    qv[0] = slq.dma(eng, qh[:, :], qk_hbm[h * 64:(h + 1) * 64, :])
                    qv[1] = slk.dma(eng, kh[:, :],
                                    qk_hbm[1024 + h * 64:1024 + (h + 1) * 64, :])
                mk.y(yf)

                def vm(eng, h=h, qv=qv):
                    eng.wait_ge(mk.sems[slq.name], qv[0])
                    eng.wait_ge(mk.sems[slk.name], qv[1])
                    mk.dr(eng)
                    eng.reduce_max(qm[:, 0:1], qh[:, :], axis=AX.X,
                                   apply_absolute_value=True)
                    eng.reduce_max(qm[:, 1:2], kh[:, :], axis=AX.X,
                                   apply_absolute_value=True)
                    mk.mark(eng.engine_nop(), "Vm")
                mk.v(vm)
                vmk = mk.cnt["Vm"]

                def tdot(eng, h=h, vmk=vmk):
                    mk.w(eng, "Vm", vmk)
                    mk.mark(eng.matmul(pdot[0:1, 0:1], qm[0:64, 0:1], qm[0:64, 1:2],
                                       start=True, stop=True), "Tm")
                mk.t(tdot)

                def vm2(eng, h=h):
                    mk.w(eng, "Tm")
                    mk.dr(eng)
                    eng.tensor_scalar(mstat[0:1, 0:1], pdot[0:1, 0:1], 0.125, 85.0,
                                      OP.mult, OP.min)
                    mk.dr(eng)
                    mk.mark(eng.tensor_scalar(mstat[0:1, 1:2], mstat[0:1, 0:1], -1.0,
                                              None, OP.mult), "Vm")
                mk.v(vm2)
                vmk2 = mk.cnt["Vm"]

                def tbc(eng, h=h, vmk2=vmk2):
                    mk.w(eng, "Vm", vmk2)
                    mk.mark(eng.matmul(pdot[:, 1:2], ones[0:1, :], mstat[0:1, 1:2],
                                       start=True, stop=True), "Tm")
                mk.t(tbc)

                def vm3(eng, h=h):
                    mk.w(eng, "Tm")
                    mk.dr(eng)
                    mk.mark(eng.tensor_copy(mstat[:, 2:3], pdot[:, 1:2]), "Vm")
                mk.v(vm3)
                negm_mark = mk.cnt["Vm"]

                for qc in range(8):
                    smarks_exp = {}
                    tmarks_sp = {}

                    def emit_sp(eng, grp, h=h, qc=qc):
                        ps = psA if grp % 2 == 0 else psB
                        if grp >= 2:
                            mk.w(eng, "Sm", smarks_exp[grp - 2])
                        last = None
                        for t2 in range(2):
                            kt = grp * 2 + t2
                            last = eng.matmul(
                                ps[:, t2, :],
                                kh[:, kt * 128:(kt + 1) * 128],
                                qh[:, qc * 512:(qc + 1) * 512],
                                start=True, stop=True)
                        tmarks_sp[grp] = mk.mark(last, "Tm")

                    def emit_exp(eng, grp, h=h, qc=qc, negm_mark=negm_mark):
                        ps = psA if grp % 2 == 0 else psB
                        pb = grp % 2
                        mk.w(eng, "Tm", tmarks_sp[grp])
                        eng.wait_ge(mk.sems["Vm"], negm_mark)
                        mk.dr(eng)
                        smarks_exp[grp] = mk.mark(eng.activation(
                            pbuf[:, pb, :, :].rearrange("p a b -> p (a b)"),
                            ps[:, :, :].rearrange("p a b -> p (a b)"),
                            AF.Exp, bias=mstat[:, 2:3], scale=0.125), "Sm")

                    def emit_pv(eng, grp, h=h, qc=qc):
                        pb = grp % 2
                        mk.w(eng, "Sm", smarks_exp[grp])
                        if grp == 0:
                            mk.w(eng, "Vm")
                        last = None
                        for t2 in range(2):
                            kt = grp * 2 + t2
                            last = eng.matmul(poh[0:65, :],
                                              vres[:, kt, h, :],
                                              pbuf[:, pb, t2, :],
                                              start=(grp == 0 and t2 == 0),
                                              stop=(grp == 15 and t2 == 1))
                        mk.mark(last, "Tm")

                    mk.t(lambda eng, f=emit_sp: f(eng, 0))
                    for grp in range(1, 16):
                        mk.t(lambda eng, f=emit_sp, g=grp: f(eng, g))
                        mk.s(lambda eng, f=emit_exp, g=grp - 1: f(eng, g))
                        mk.t(lambda eng, f=emit_pv, g=grp - 1: f(eng, g))
                    mk.s(lambda eng, f=emit_exp: f(eng, 15))
                    mk.t(lambda eng, f=emit_pv: f(eng, 15))
                    pvdone = mk.cnt["Tm"]

                    def vn(eng, h=h, qc=qc, pvdone=pvdone):
                        mk.w(eng, "Tm", pvdone)
                        mk.dr(eng)
                        eng.reciprocal(rec1[:, :], poh[64:65, :])
                        mk.mark(eng.engine_nop(), "Vm")
                    mk.v(vn)
                    vrec = mk.cnt["Vm"]

                    def tbc2(eng, h=h, qc=qc, vrec=vrec):
                        mk.w(eng, "Vm", vrec)
                        mk.mark(eng.matmul(pdot[0:64, :], ones[0:1, 0:64], rec1[:, :],
                                           start=True, stop=True), "Tm")
                    mk.t(tbc2)

                    m = h * 8 + qc
                    sla = SS[2 + m % 2]

                    def vn2(eng, h=h, qc=qc, sla=sla, m=m):
                        mk.w(eng, "Tm")
                        if sla.guard is not None:
                            eng.wait_ge(mk.sems[sla.name], sla.guard[1])
                            sla.guard = None
                        mk.dr(eng)
                        eng.tensor_copy(recb[:, :], pdot[0:64, :])
                        mk.dr(eng)
                        mk.mark(eng.tensor_tensor(
                            atile[:, m % 2, :], poh[0:64, :], recb[:, :],
                            OP.mult), "Vm")
                    mk.v(vn2)
                    vv2 = mk.cnt["Vm"]

                    def yga(eng, h=h, qc=qc, sla=sla, m=m, vv2=vv2):
                        eng.wait_ge(mk.sems["Vm"], vv2)
                        mk.mark(eng.dma_start(
                            out=attn_hbm[h * 64:(h + 1) * 64,
                                         qc * 512:(qc + 1) * 512],
                            in_=atile[:, m % 2, :]), sla.name, 16)
                    mk.g(yga)
                    sla.guard = (sla.name, mk.cnt[sla.name])
                slq.consumed("Tm")
                slk.consumed("Tm")

        mk.barrier()

        mix_store_marks = {}
        with ExitStack() as stk3:
            wob = mk.sb(stk3, "c_wob", [128, NC, D], BF16)
            wstage3 = mk.sb(stk3, "c_wstg3", [128, 2, D], F32)
            emit_load_cast(mk, wstage3, "c_wo", W["wo_t"], wob, D, D, LS[4:6])
            astage = mk.sb(stk3, "c_astg", [128, 2, NC, 128], BF16)
            mixtok = mk.sb(stk3, "c_mixtok", [128, 2, D], F32)
            junk3 = mk.sb(stk3, "c_junk3", [128, D], F32)
            pout = mk.psum(stk3, "c_pout", [128, 2, D], F32)
            wdone = mk.cnt["Vm"]
            attn_r = attn_hbm.rearrange("(j p) t -> p j t", p=128)

            for i in range(NT):
                sls = SS[i % 2]
                sla = LS[10 + i % 2]
                qva = [None]

                def ya(eng, i=i, sla=sla, qva=qva):
                    qva[0] = sla.dma(eng, astage[:, i % 2, :, :],
                                     attn_r[:, :, i * 128:(i + 1) * 128])
                mk.y(ya)

                def tf(eng, i=i, wdone=wdone, sla=sla, qva=qva):
                    if i == 0:
                        mk.w(eng, "Vm", wdone)
                    if i >= 2:
                        mk.w(eng, "Vm")
                    eng.wait_ge(mk.sems[sla.name], qva[0])
                    last = None
                    for j in range(NC):
                        for half in range(2):
                            last = eng.matmul(
                                pout[:, i % 2, half * 512:(half + 1) * 512],
                                astage[:, i % 2, j, :],
                                wob[:, j, half * 512:(half + 1) * 512],
                                start=(j == 0), stop=(j == NC - 1))
                    mk.mark(last, "Tm")
                mk.t(tf)
                sla.consumed("Tm")

                def vf(eng, i=i, sls=sls):
                    mk.w(eng, "Tm")
                    if sls.guard is not None:
                        eng.wait_ge(mk.sems[sls.name], sls.guard[1])
                        sls.guard = None
                    if i >= 2:
                        mk.w(eng, "Sm")
                    mk.dr(eng)
                    eng.tensor_copy(mixtok[:, i % 2, :], pout[:, i % 2, :])
                    mk.dr(eng)
                    mk.mark(eng.reduce_sum(mxsum[:, i:i + 1], mixtok[:, i % 2, :],
                                           axis=AX.X), "Vm")
                mk.v(vf)
                vv = mk.cnt["Vm"]

                def sq(eng, i=i, vv=vv):
                    eng.wait_ge(mk.sems["Vm"], vv)
                    mk.dr(eng)
                    mk.mark(eng.activation(junk3[:], mixtok[:, i % 2, :], AF.Square,
                                           accum_out=mxsq[:, i:i + 1]), "Sm")
                mk.s(sq)

                def yg(eng, i=i, sls=sls, vv=vv):
                    eng.wait_ge(mk.sems["Vm"], vv)
                    mk.mark(eng.dma_start(out=mixbuf[i * 128:(i + 1) * 128, :],
                                          in_=mixtok[:, i % 2, :]), sls.name, 16)
                mk.g(yg)
                sls.guard = (sls.name, mk.cnt[sls.name])
                mix_store_marks[i] = (sls.name, mk.cnt[sls.name])

        mk.barrier()
        with ExitStack() as stk4:
            emit_residual_update(mk, stk4, "c_up", x_in, mixbuf, x_out,
                                 mxsum, mxsq, LS[0:6], SS[0:3], mix_store_marks)


def prep_l1attn(w_qkv, w_out):
    return {
        "wqk_t": np.ascontiguousarray(w_qkv[0:2048].T),
        "wv_t": np.ascontiguousarray(w_qkv[2048:3072].T),
        "wo_t": np.ascontiguousarray(w_out.T),
    }




def phase_l0ffn(mk, x_in, x_out, W, mixbuf, mixed_hbm, nx_hbm):
    LS = mk.lslots
    SS = mk.sslots
    F = 4096
    NF = F // 128
    with ExitStack() as stk:
        idf = mk.sb(stk, "b_idf", [128, 128], F32)
        idb = mk.sb(stk, "b_idb", [128, 128], BF16)
        normw = mk.sb(stk, "b_normw", [128, D], F32)
        lnsum = mk.sb(stk, "b_lnsum", [128, NT], F32)
        lnsq = mk.sb(stk, "b_lnsq", [128, NT], F32)
        rstd = mk.sb(stk, "b_rstd", [128, NT], F32)
        negb = mk.sb(stk, "b_negb", [128, NT], F32)
        mxsum = mk.sb(stk, "b_mxsum", [128, NT], F32)
        mxsq = mk.sb(stk, "b_mxsq", [128, NT], F32)
        wsl = mk.sb(stk, "b_wsl", [128, NT, 2], F32)
        slots = mk.sb(stk, "b_slots", [128, NT, 2, 8], BF16)
        slotT = mk.sb(stk, "b_slotT", [8, 2, T], BF16)
        wslotT = mk.sb(stk, "b_wslotT", [8, 2, T], BF16)
        r2acc = mk.sb(stk, "b_r2acc", [128, 2, NT, 128], F32)
        expd = mk.sb(stk, "b_expd", [8, 128], BF16)

        def yf(eng):
            mk.mark(eng.dma_start(out=idf[:, :], in_=W["ident"][:, :]), "CN", 16)
            mk.mark(eng.dma_start(out=normw[:, :], in_=W["normw_b"][:, :]), "CN", 16)
        mk.y(yf)
        cn0 = mk.cnt["CN"]

        def vcn(eng):
            eng.wait_ge(mk.sems["CN"], cn0)
            eng.memset(r2acc[:, :, :, :], 0.0)
            mk.dr(eng)
            mk.mark(eng.tensor_copy(idb[:, :], idf[:, :]), "Vm")
        mk.v(vcn)

        with ExitStack() as stkln:
            emit_ln_stats(mk, stkln, "b_ln", x_in, lnsum, lnsq, LS[0:4])
            emit_stats_finalize(mk, stkln, "b_lnf", lnsum, lnsq, D, EPS_LN,
                                rstd, negb)

        nx_store_marks = {}
        with ExitStack() as stk1:
            gatet = mk.sb(stk1, "b_gatet", [128, NC, 8], BF16)
            a1s = mk.sb(stk1, "b_a1s", [128, NC, 128], BF16)
            a3s = mk.sb(stk1, "b_a3s", [128, NC, 128], BF16)
            a2s = mk.sb(stk1, "b_a2s", [128, NF, 128], BF16)
            b1s = mk.sb(stk1, "b_b1s", [128, 1, F], BF16)
            b3s = mk.sb(stk1, "b_b3s", [128, 1, F], BF16)
            with ExitStack() as stkw:
                wstage = mk.sb(stkw, "b_wstg", [128, 2, 4096], F32)
                expstg = mk.sb(stkw, "b_expstg", [8, 128], F32)
                emit_load_cast(mk, wstage, "b_ga", W["gate_t"], gatet, D, 8, LS[4:6])
                emit_load_cast(mk, wstage, "b_a1", W["a1_s"], a1s, D, 128, LS[4:6])
                emit_load_cast(mk, wstage, "b_a3", W["a3_s"], a3s, D, 128, LS[4:6])
                emit_load_cast(mk, wstage, "b_b1", W["b1_s"], b1s, 128, 4096, LS[4:6])
                emit_load_cast(mk, wstage, "b_b3", W["b3_s"], b3s, 128, 4096, LS[4:6])
                emit_load_cast(mk, wstage, "b_a2", W["a2_sr"], a2s, F, 128, LS[4:6])
                sle = LS[4]
                qve = [None]

                def ye(eng):
                    qve[0] = sle.dma(eng, expstg[:, :], W["expand"][:, :])
                mk.y(ye)

                def ve(eng):
                    eng.wait_ge(mk.sems[sle.name], qve[0])
                    mk.dr(eng)
                    mk.mark(eng.tensor_copy(expd[:, :], expstg[:, :]), "Vm")
                mk.v(ve)
                sle.consumed("Vm")

            with ExitStack() as stk1b:
                xin1 = mk.sb(stk1b, "b_xin1", [128, 4, D], F32)
                ztmp = mk.sb(stk1b, "b_ztmp", [128, 2, D], F32)
                nxtok = mk.sb(stk1b, "b_nxtok", [128, 2, D], BF16)
                cmtile = mk.sb(stk1b, "b_cmtile", [128, 2, NC, 128], BF16)
                junkb = mk.sb(stk1b, "b_junkb", [128, D], F32)
                sqt = mk.sb(stk1b, "b_sqt", [128, 4], F32)
                lgt = mk.sb(stk1b, "b_lgt", [128, 8], F32)
                lgt2 = mk.sb(stk1b, "b_lgt2", [128, 8], F32)
                gm12 = mk.sb(stk1b, "b_gm12", [128, NT], F32)
                gtmp = mk.sb(stk1b, "b_gtmp", [128, 8], F32)
                ptr = mk.psum(stk1b, "b_ptr", [128, NC, 128], BF16)
                plg = mk.psum(stk1b, "b_plg", [128, 8], F32)

                for i in range(NT):
                    sl = LS[6 + i % 4]
                    snx = SS[2 + i % 2]
                    qv = [None]

                    def yf(eng, i=i, sl=sl, qv=qv):
                        qv[0] = sl.dma(eng, xin1[:, i % 4, :],
                                       x_in[i * 128:(i + 1) * 128, :])
                    mk.y(yf)

                    def sf(eng, i=i, sl=sl, qv=qv):
                        eng.wait_ge(mk.sems[sl.name], qv[0])
                        mk.w(eng, "Vm")
                        mk.dr(eng)
                        eng.activation(ztmp[:, i % 2, :], xin1[:, i % 4, :],
                                       AF.Identity, bias=negb[:, i:i + 1],
                                       scale=rstd[:, i:i + 1])
                        mk.dr(eng)
                        mk.mark(eng.activation(junkb[:], ztmp[:, i % 2, :], AF.Square,
                                               accum_out=sqt[:, i % 4:i % 4 + 1]),
                                "Sm")
                    mk.s(sf)
                    sl.consumed("Sm")
                    sm1 = mk.cnt["Sm"]

                    def vr(eng, i=i, sm1=sm1):
                        eng.wait_ge(mk.sems["Sm"], sm1)
                        mk.dr(eng)
                        eng.tensor_scalar(gtmp[:, 0:1], sqt[:, i % 4:i % 4 + 1],
                                          1.0 / D, 1e-6, OP.mult, OP.add)
                        mk.dr(eng)
                        mk.mark(eng.reciprocal(gtmp[:, 1:2], gtmp[:, 0:1]), "Vm")
                    mk.v(vr)
                    vm1 = mk.cnt["Vm"]

                    def sr(eng, i=i, vm1=vm1):
                        eng.wait_ge(mk.sems["Vm"], vm1)
                        mk.dr(eng)
                        mk.mark(eng.activation(gtmp[:, 2:3], gtmp[:, 1:2], AF.Sqrt),
                                "Sm")
                    mk.s(sr)
                    sm2 = mk.cnt["Sm"]

                    def vnx(eng, i=i, sm2=sm2):
                        eng.wait_ge(mk.sems["Sm"], sm2)
                        mk.w(eng, "Tm")
                        mk.dr(eng)
                        mk.mark(eng.scalar_tensor_tensor(
                            nxtok[:, i % 2, :], ztmp[:, i % 2, :], gtmp[:, 2:3],
                            normw[:, :], OP.mult, OP.mult), "Vm")
                    mk.v(vnx)
                    vm2 = mk.cnt["Vm"]

                    def tt(eng, i=i, vm2=vm2):
                        eng.wait_ge(mk.sems["Vm"], vm2)
                        mk.w(eng, "Vm")
                        last = None
                        for j in range(NC):
                            last = eng.transpose(
                                ptr[:, j, :], nxtok[:, i % 2, j * 128:(j + 1) * 128],
                                idb[:])
                        mk.mark(last, "Tm")
                    mk.t(tt)
                    tm1 = mk.cnt["Tm"]

                    def vcp(eng, i=i, tm1=tm1, snx=snx):
                        eng.wait_ge(mk.sems["Tm"], tm1)
                        if snx.guard is not None:
                            eng.wait_ge(mk.sems[snx.name], snx.guard[1])
                            snx.guard = None
                        mk.dr(eng)
                        mk.mark(eng.tensor_copy(cmtile[:, i % 2, :, :], ptr[:, :, :]),
                                "Vm")
                    mk.v(vcp)
                    vm3 = mk.cnt["Vm"]

                    def ynx(eng, i=i, snx=snx, vm3=vm3):
                        eng.wait_ge(mk.sems["Vm"], vm3)
                        mk.mark(eng.dma_start(
                            out=nx_hbm.rearrange("(c p) t -> p c t", p=128)[
                                :, :, i * 128:(i + 1) * 128],
                            in_=cmtile[:, i % 2, :, :]), snx.name, 16)
                    mk.g(ynx)
                    snx.guard = (snx.name, mk.cnt[snx.name])
                    nx_store_marks[i] = (snx.name, mk.cnt[snx.name])

                    def tg(eng, i=i, vm3=vm3):
                        eng.wait_ge(mk.sems["Vm"], vm3)
                        last = None
                        for j in range(NC):
                            last = eng.matmul(plg[:, :], cmtile[:, i % 2, j, :],
                                              gatet[:, j, :],
                                              start=(j == 0), stop=(j == NC - 1))
                        mk.mark(last, "Tm")
                    mk.t(tg)
                    tm2 = mk.cnt["Tm"]

                    def vg(eng, i=i, tm2=tm2):
                        eng.wait_ge(mk.sems["Tm"], tm2)
                        mk.dr(eng)
                        eng.tensor_copy(lgt[:, :], plg[:, :])
                        mk.dr(eng)
                        eng.reduce_max(gtmp[:, 3:4], lgt[:, :], axis=AX.X)
                        mk.dr(eng)
                        eng.tensor_scalar(slots[:, i, 0, :], lgt[:, :], gtmp[:, 3:4],
                                          None, OP.is_ge)
                        mk.dr(eng)
                        eng.scalar_tensor_tensor(lgt2[:, :], slots[:, i, 0, :],
                                                 -1e30, lgt[:, :], OP.mult, OP.add)
                        mk.dr(eng)
                        eng.reduce_max(gtmp[:, 4:5], lgt2[:, :], axis=AX.X)
                        mk.dr(eng)
                        eng.tensor_scalar(slots[:, i, 1, :], lgt2[:, :],
                                          gtmp[:, 4:5], None, OP.is_ge)
                        mk.dr(eng)
                        mk.mark(eng.tensor_tensor(gm12[:, i:i + 1], gtmp[:, 4:5],
                                                  gtmp[:, 3:4], OP.subtract), "Vm")
                    mk.v(vg)

                vme = mk.cnt["Vm"]

                def se(eng, vme=vme):
                    eng.wait_ge(mk.sems["Vm"], vme)
                    mk.dr(eng)
                    mk.mark(eng.activation(gm12[:, :], gm12[:, :], AF.Exp), "Sm")
                mk.s(se)
                sme = mk.cnt["Sm"]

                def vwf(eng, sme=sme):
                    eng.wait_ge(mk.sems["Sm"], sme)
                    mk.dr(eng)
                    eng.tensor_scalar(wsl[:, :, 0], gm12[:, :], 1.0, None, OP.add)
                    mk.dr(eng)
                    eng.reciprocal(wsl[:, :, 0], wsl[:, :, 0])
                    mk.dr(eng)
                    mk.mark(eng.tensor_tensor(wsl[:, :, 1], gm12[:, :],
                                              wsl[:, :, 0], OP.mult), "Vm")
                mk.v(vwf)

                wstok = mk.sb(stk1b, "b_wstok", [128, 2, 8], BF16)
                pslt = mk.psum(stk1b, "b_pslt", [8, 4, 128], BF16)
                for i in range(NT):
                    def vws(eng, i=i):
                        mk.w(eng, "Tm")
                        mk.dr(eng)
                        eng.tensor_scalar(wstok[:, 0, :], slots[:, i, 0, :],
                                          wsl[:, i, 0:1], None, OP.mult)
                        mk.dr(eng)
                        mk.mark(eng.tensor_scalar(wstok[:, 1, :], slots[:, i, 1, :],
                                                  wsl[:, i, 1:2], None, OP.mult),
                                "Vm")
                    mk.v(vws)
                    vmw = mk.cnt["Vm"]

                    def ts(eng, i=i, vmw=vmw):
                        eng.wait_ge(mk.sems["Vm"], vmw)
                        mk.w(eng, "Vm")
                        eng.transpose(pslt[0:8, 0, :], slots[:, i, 0, :], idb[:])
                        eng.transpose(pslt[0:8, 1, :], slots[:, i, 1, :], idb[:])
                        eng.transpose(pslt[0:8, 2, :], wstok[:, 0, :], idb[:])
                        mk.mark(eng.transpose(pslt[0:8, 3, :], wstok[:, 1, :],
                                              idb[:]), "Tm")
                    mk.t(ts)
                    tms = mk.cnt["Tm"]

                    def vse(eng, i=i, tms=tms):
                        eng.wait_ge(mk.sems["Tm"], tms)
                        mk.dr(eng)
                        isl = slice(i * 128, (i + 1) * 128)
                        eng.tensor_copy(slotT[:, 0, isl], pslt[0:8, 0, :])
                        eng.tensor_copy(slotT[:, 1, isl], pslt[0:8, 1, :])
                        eng.tensor_copy(wslotT[:, 0, isl], pslt[0:8, 2, :])
                        mk.mark(eng.tensor_copy(wslotT[:, 1, isl], pslt[0:8, 3, :]),
                                "Vm")
                    mk.v(vse)

            mk.barrier()
            msk1_0 = mk.sb(stk1, "b_msk10", [128, T], BF16)
            msk1_1 = mk.sb(stk1, "b_msk11", [128, T], BF16)
            msk3_0 = mk.sb(stk1, "b_msk30", [128, T], BF16)
            msk3_1 = mk.sb(stk1, "b_msk31", [128, T], BF16)
            with ExitStack() as stk2a:
                nxch = mk.sb(stk2a, "b_nxch", [128, 2, NC, 512], BF16)
                etr1c = mk.sb(stk2a, "b_etr1c", [128, 512], BF16)
                etr3c = mk.sb(stk2a, "b_etr3c", [128, 512], BF16)
                petr = mk.psum(stk2a, "b_petr", [128, 4, 512], F32)
                nxr = nx_hbm.rearrange("(c p) t -> p c t", p=128)

                for c in range(8):
                    csl = slice(c * 512, (c + 1) * 512)
                    sln = LS[6 + c % 2]
                    qvn = [None]

                    def yn(eng, c=c, sln=sln, qvn=qvn):
                        for i4 in range(4 * c, 4 * c + 4):
                            ms = nx_store_marks[i4]
                            eng.wait_ge(mk.sems[ms[0]], ms[1])
                        qvn[0] = sln.dma(eng, nxch[:, c % 2, :, :],
                                         nxr[:, :, c * 512:(c + 1) * 512])
                    mk.y(yn)

                    def te(eng, c=c, csl=csl, sln=sln, qvn=qvn):
                        eng.wait_ge(mk.sems[sln.name], qvn[0])
                        mk.w(eng, "Vm")
                        last = None
                        for j in range(NC):
                            eng.matmul(petr[:, 0, :], a1s[:, j, :],
                                       nxch[:, c % 2, j, :],
                                       start=(j == 0), stop=(j == NC - 1))
                        for j in range(NC):
                            eng.matmul(petr[:, 1, :], a3s[:, j, :],
                                       nxch[:, c % 2, j, :],
                                       start=(j == 0), stop=(j == NC - 1))
                        eng.matmul(petr[:, 2, :], expd[:, :], slotT[:, 0, csl],
                                   start=True, stop=True)
                        last = eng.matmul(petr[:, 3, :], expd[:, :],
                                          slotT[:, 1, csl], start=True, stop=True)
                        mk.mark(last, "Tm")
                    mk.t(te)
                    sln.consumed("Tm")
                    tme = mk.cnt["Tm"]

                    def vee(eng, c=c, csl=csl, tme=tme):
                        eng.wait_ge(mk.sems["Tm"], tme)
                        mk.dr(eng)
                        eng.tensor_copy(etr1c[:, :], petr[:, 0, :])
                        eng.tensor_copy(etr3c[:, :], petr[:, 1, :])
                        mk.dr(eng)
                        eng.tensor_tensor(msk1_0[:, csl], etr1c[:, :], petr[:, 2, :],
                                          OP.mult)
                        mk.mark(eng.tensor_tensor(msk1_1[:, csl], etr1c[:, :],
                                                  petr[:, 3, :], OP.mult), "Vm")
                    mk.v(vee)
                    vmee = mk.cnt["Vm"]

                    def te2(eng, c=c, csl=csl, vmee=vmee):
                        eng.wait_ge(mk.sems["Vm"], vmee)
                        eng.matmul(petr[:, 2, :], expd[:, :], wslotT[:, 0, csl],
                                   start=True, stop=True)
                        mk.mark(eng.matmul(petr[:, 3, :], expd[:, :],
                                           wslotT[:, 1, csl], start=True,
                                           stop=True), "Tm")
                    mk.t(te2)
                    tme2 = mk.cnt["Tm"]

                    def vmm(eng, c=c, csl=csl, tme2=tme2):
                        eng.wait_ge(mk.sems["Tm"], tme2)
                        mk.dr(eng)
                        eng.tensor_tensor(msk3_0[:, csl], etr3c[:, :], petr[:, 2, :],
                                          OP.mult)
                        mk.mark(eng.tensor_tensor(msk3_1[:, csl], etr3c[:, :],
                                                  petr[:, 3, :], OP.mult), "Vm")
                    mk.v(vmm)

            mk.barrier()

            with ExitStack() as stk2:
                wf1 = mk.sb(stk2, "b_wf1", [128, 2, NC, 512], BF16)
                wf3 = mk.sb(stk2, "b_wf3", [128, 2, NC, 512], BF16)
                wstg2 = mk.sb(stk2, "b_wstg2", [128, NC, 512], F32)
                nxst = mk.sb(stk2, "b_nxst", [128, 2, NC, 128], BF16)
                eb1 = mk.sb(stk2, "b_eb1", [128, 512], BF16)
                eb3 = mk.sb(stk2, "b_eb3", [128, 512], BF16)
                h1b = mk.sb(stk2, "b_h1b", [128, 2, 512], BF16)
                silub = mk.sb(stk2, "b_silub", [128, 2, 512], BF16)
                h3wb = mk.sb(stk2, "b_h3wb", [128, 2, 512], BF16)
                wactb = mk.sb(stk2, "b_wactb", [128, 2, 512], BF16)
                wactcm = mk.sb(stk2, "b_wactcm", [128, 2, 4, 128], BF16)
                mixcm = mk.sb(stk2, "b_mixcm", [128, 2, 4, 128], BF16)
                pb1 = mk.psum(stk2, "b_pb1", [128, 512], F32)
                pb3 = mk.psum(stk2, "b_pb3", [128, 512], F32)
                pl1 = mk.psum(stk2, "b_pl1", [128, 2, 512], F32)
                pl3 = mk.psum(stk2, "b_pl3", [128, 2, 512], F32)
                ptw = mk.psum(stk2, "b_ptw", [128, 2, 4, 128], BF16)
                pr2 = mk.psum(stk2, "b_pr2", [128, 2, 128], F32)
                nxr = nx_hbm.rearrange("(c p) t -> p c t", p=128)

                prev = {"sev": 0, "vh3": 0, "ttr": 0, "vtr": 0, "tr2": 0, "vr2": 0}
                for fs in range(8):
                    wb = fs % 2
                    slw1, slw3 = LS[4], LS[5]
                    qw = [None, None]
                    w1r = W["w1_t"].rearrange("(c p) f -> p c f", p=128)
                    w3r = W["w3_t"].rearrange("(c p) f -> p c f", p=128)

                    def yw(eng, fs=fs, wb=wb, qw=qw, w1r=w1r):
                        qw[0] = slw1.dma(eng, wstg2[:, :, :],
                                         w1r[:, :, fs * 512:(fs + 1) * 512])
                    mk.y(yw)

                    def vw1(eng, fs=fs, wb=wb, qw=qw):
                        eng.wait_ge(mk.sems[slw1.name], qw[0])
                        mk.dr(eng)
                        mk.mark(eng.tensor_copy(wf1[:, wb, :, :], wstg2[:, :, :]),
                                "Vm")
                    mk.v(vw1)
                    slw1.consumed("Vm")
                    w1c = mk.cnt["Vm"]

                    def yw3(eng, fs=fs, wb=wb, qw=qw, w3r=w3r, w1c=w1c):
                        eng.wait_ge(mk.sems["Vm"], w1c)
                        qw[1] = slw3.dma(eng, wstg2[:, :, :],
                                         w3r[:, :, fs * 512:(fs + 1) * 512])
                    mk.y(yw3)

                    def vw3(eng, fs=fs, wb=wb, qw=qw):
                        eng.wait_ge(mk.sems[slw3.name], qw[1])
                        mk.dr(eng)
                        mk.mark(eng.tensor_copy(wf3[:, wb, :, :], wstg2[:, :, :]),
                                "Vm")
                    mk.v(vw3)
                    slw3.consumed("Vm")
                    wready = mk.cnt["Vm"]

                    for ti in range(NT):
                        inst = fs * NT + ti
                        tsl = slice(ti * 128, (ti + 1) * 128)
                        bsl = slice(fs * 512, (fs + 1) * 512)
                        sstore = SS[2 + inst % 2]
                        slnx = LS[8 + inst % 2]
                        qnx = [None]

                        def ynx2(eng, ti=ti, slnx=slnx, qnx=qnx):
                            qnx[0] = slnx.dma(eng, nxst[:, inst % 2, :, :],
                                              nxr[:, :, ti * 128:(ti + 1) * 128])
                        mk.y(ynx2)

                        def tmm(eng, fs=fs, ti=ti, wb=wb, wready=wready, tsl=tsl,
                                bsl=bsl, p=dict(prev), slnx=slnx, qnx=qnx,
                                inst=inst):
                            if ti == 0:
                                mk.w(eng, "Vm", wready)
                            eng.wait_ge(mk.sems[slnx.name], qnx[0])
                            mk.w(eng, "Sm", p["sev"])
                            mk.w(eng, "Vm", p["vh3"])
                            last = None
                            for j in range(NC):
                                eng.matmul(pb1[:, :], nxst[:, inst % 2, j, :],
                                           wf1[:, wb, j, :],
                                           start=(j == 0), stop=(j == NC - 1))
                            for j in range(NC):
                                eng.matmul(pb3[:, :], nxst[:, inst % 2, j, :],
                                           wf3[:, wb, j, :],
                                           start=(j == 0), stop=(j == NC - 1))
                            eng.matmul(pl1[:, 0, :], msk1_0[:, tsl], b1s[:, 0, bsl],
                                       start=True, stop=True)
                            eng.matmul(pl1[:, 1, :], msk1_1[:, tsl], b1s[:, 0, bsl],
                                       start=True, stop=True)
                            eng.matmul(pl3[:, 0, :], msk3_0[:, tsl], b3s[:, 0, bsl],
                                       start=True, stop=True)
                            last = eng.matmul(pl3[:, 1, :], msk3_1[:, tsl],
                                              b3s[:, 0, bsl], start=True, stop=True)
                            mk.mark(last, "Tm")
                        mk.t(tmm)
                        slnx.consumed("Tm")
                        tmmm = mk.cnt["Tm"]

                        def sev(eng, tmmm=tmmm, p=dict(prev)):
                            eng.wait_ge(mk.sems["Tm"], tmmm)
                            mk.w(eng, "Vm", p["vh3"])
                            mk.dr(eng)
                            eng.copy(eb1[:, :], pb1[:, :])
                            mk.dr(eng)
                            mk.mark(eng.copy(eb3[:, :], pb3[:, :]), "Sm")
                        mk.s(sev)
                        prev["sev"] = mk.cnt["Sm"]
                        smev = prev["sev"]

                        def vh(eng, smev=smev):
                            eng.wait_ge(mk.sems["Sm"], smev)
                            mk.dr(eng)
                            eng.tensor_tensor(h1b[:, 0, :], eb1[:, :], pl1[:, 0, :],
                                              OP.add)
                            mk.dr(eng)
                            mk.mark(eng.tensor_tensor(h1b[:, 1, :], eb1[:, :],
                                                      pl1[:, 1, :], OP.add), "Vm")
                        mk.v(vh)
                        vmh = mk.cnt["Vm"]

                        def ssl(eng, vmh=vmh):
                            eng.wait_ge(mk.sems["Vm"], vmh)
                            mk.dr(eng)
                            eng.activation(silub[:, 0, :], h1b[:, 0, :], AF.Silu)
                            mk.dr(eng)
                            mk.mark(eng.activation(silub[:, 1, :], h1b[:, 1, :],
                                                   AF.Silu), "Sm")
                        mk.s(ssl)
                        smsl = mk.cnt["Sm"]

                        def vh3(eng, ti=ti, smsl=smsl, p=dict(prev)):
                            eng.wait_ge(mk.sems["Sm"], smsl)
                            eng.wait_ge(mk.sems["Tm"], p["ttr"])
                            mk.dr(eng)
                            eng.scalar_tensor_tensor(h3wb[:, 0, :], eb3[:, :],
                                                     wsl[:, ti, 0:1], pl3[:, 0, :],
                                                     OP.mult, OP.add)
                            mk.dr(eng)
                            eng.tensor_tensor(wactb[:, 0, :], silub[:, 0, :],
                                              h3wb[:, 0, :], OP.mult)
                            mk.dr(eng)
                            eng.scalar_tensor_tensor(h3wb[:, 1, :], eb3[:, :],
                                                     wsl[:, ti, 1:2], pl3[:, 1, :],
                                                     OP.mult, OP.add)
                            mk.dr(eng)
                            mk.mark(eng.tensor_tensor(wactb[:, 1, :], silub[:, 1, :],
                                                      h3wb[:, 1, :], OP.mult), "Vm")
                        mk.v(vh3)
                        prev["vh3"] = mk.cnt["Vm"]
                        vmw2 = prev["vh3"]

                        def ttr(eng, vmw2=vmw2, p=dict(prev)):
                            eng.wait_ge(mk.sems["Vm"], vmw2)
                            mk.w(eng, "Vm", p["vtr"])
                            last = None
                            for sl2 in range(2):
                                for q in range(4):
                                    last = eng.transpose(
                                        ptw[:, sl2, q, :],
                                        wactb[:, sl2, q * 128:(q + 1) * 128], idb[:])
                            mk.mark(last, "Tm")
                        mk.t(ttr)
                        prev["ttr"] = mk.cnt["Tm"]
                        tmtr = prev["ttr"]

                        def vtr(eng, inst=inst, tmtr=tmtr, sstore=sstore,
                                p=dict(prev)):
                            eng.wait_ge(mk.sems["Tm"], tmtr)
                            eng.wait_ge(mk.sems["Tm"], p["tr2"])
                            if sstore.guard is not None:
                                eng.wait_ge(mk.sems[sstore.name], sstore.guard[1])
                                sstore.guard = None
                            mk.dr(eng)
                            eng.tensor_copy(wactcm[:, 0, :, :], ptw[:, 0, :, :])
                            eng.tensor_copy(wactcm[:, 1, :, :], ptw[:, 1, :, :])
                            mk.dr(eng)
                            mk.mark(eng.tensor_tensor(mixcm[:, inst % 2, :, :],
                                                      wactcm[:, 0, :, :],
                                                      wactcm[:, 1, :, :], OP.add),
                                    "Vm")
                        mk.v(vtr)
                        prev["vtr"] = mk.cnt["Vm"]
                        vmtr = prev["vtr"]

                        def ygm(eng, fs=fs, ti=ti, inst=inst, sstore=sstore,
                                vmtr=vmtr):
                            eng.wait_ge(mk.sems["Vm"], vmtr)
                            mk.mark(eng.dma_start(
                                out=mixed_hbm[ti, fs * 512:(fs + 1) * 512, :]
                                .rearrange("(a p) b -> p a b", p=128),
                                in_=mixcm[:, inst % 2, :, :]), sstore.name, 16)
                        mk.g(ygm)
                        sstore.guard = (sstore.name, mk.cnt[sstore.name])

                        def tr2(eng, fs=fs, vmtr=vmtr, p=dict(prev)):
                            eng.wait_ge(mk.sems["Vm"], vmtr)
                            eng.wait_ge(mk.sems["Vm"], p["vr2"])
                            last = None
                            for sl2 in range(2):
                                for q in range(4):
                                    last = eng.matmul(
                                        pr2[:, sl2, :], wactcm[:, sl2, q, :],
                                        a2s[:, fs * 4 + q, :],
                                        start=(q == 0), stop=(q == 3))
                            mk.mark(last, "Tm")
                        mk.t(tr2)
                        prev["tr2"] = mk.cnt["Tm"]
                        tmr2 = prev["tr2"]

                        def vr2(eng, ti=ti, tmr2=tmr2):
                            eng.wait_ge(mk.sems["Tm"], tmr2)
                            mk.dr(eng)
                            eng.tensor_tensor(r2acc[:, 0, ti, :], r2acc[:, 0, ti, :],
                                              pr2[:, 0, :], OP.add)
                            mk.dr(eng)
                            mk.mark(eng.tensor_tensor(r2acc[:, 1, ti, :],
                                                      r2acc[:, 1, ti, :],
                                                      pr2[:, 1, :], OP.add), "Vm")
                        mk.v(vr2)
                        prev["vr2"] = mk.cnt["Vm"]

            mk.barrier()

        mix_store_marks = {}
        with ExitStack() as stk3:
            w2b = mk.sb(stk3, "b_w2b", [128, NF, D], BF16)
            wstg4 = mk.sb(stk3, "b_wstg4", [128, 2, D], F32)
            emit_load_cast(mk, wstg4, "b_w2", W["w2_t"], w2b, F, D, LS[4:6])
            b2s = mk.sb(stk3, "b_b2s", [128, D], BF16)
            b2stg = mk.sb(stk3, "b_b2stg", [128, D], F32)
            slb = LS[6]
            qvb = [None]

            def yb2(eng):
                qvb[0] = slb.dma(eng, b2stg[:, :], W["b2_s"][:, :])
            mk.y(yb2)

            def vb2(eng):
                eng.wait_ge(mk.sems[slb.name], qvb[0])
                mk.dr(eng)
                mk.mark(eng.tensor_copy(b2s[:, :], b2stg[:, :]), "Vm")
            mk.v(vb2)
            slb.consumed("Vm")

            mstr = mk.sb(stk3, "b_mstr", [128, 2, NF, 128], BF16)
            mr2tok = mk.sb(stk3, "b_mr2tok", [128, 128], F32)
            mr2tmp = mk.sb(stk3, "b_mr2tmp", [128, 128], F32)
            mr2b = mk.sb(stk3, "b_mr2b", [128, 128], BF16)
            mr2cm = mk.sb(stk3, "b_mr2cm", [128, 128], BF16)
            mixtok = mk.sb(stk3, "b_mixtok", [128, 2, D], F32)
            junk4 = mk.sb(stk3, "b_junk4", [128, D], F32)
            pout = mk.psum(stk3, "b_pout", [128, 2, D], F32)
            pmsk = mk.psum(stk3, "b_pmsk", [128, 2, 128], F32)
            pmr2 = mk.psum(stk3, "b_pmr2", [128, 128], BF16)
            wdone2 = mk.cnt["Vm"]

            prevB = {"vmr": 0, "vout": 0, "sq": 0}
            for ti in range(NT):
                slm = LS[10 + ti % 2]
                sls = SS[ti % 2]
                qvm = [None]

                def ym(eng, ti=ti, slm=slm, qvm=qvm):
                    qvm[0] = slm.dma(eng, mstr[:, ti % 2, :, :],
                        mixed_hbm[ti, :, :].rearrange("(a p) b -> p a b", p=128))
                mk.y(ym)

                def tmsk(eng, ti=ti, wdone2=wdone2, p=dict(prevB)):
                    if ti == 0:
                        mk.w(eng, "Vm", wdone2)
                    mk.w(eng, "Vm", p["vmr"])
                    eng.matmul(pmsk[:, 0, :], slotT[:, 0, ti * 128:(ti + 1) * 128],
                               expd[:, :], start=True, stop=True)
                    mk.mark(eng.matmul(pmsk[:, 1, :],
                                       slotT[:, 1, ti * 128:(ti + 1) * 128],
                                       expd[:, :], start=True, stop=True), "Tm")
                mk.t(tmsk)
                tmk = mk.cnt["Tm"]

                def vmr(eng, ti=ti, tmk=tmk):
                    eng.wait_ge(mk.sems["Tm"], tmk)
                    mk.dr(eng)
                    eng.tensor_tensor(mr2tok[:, :], r2acc[:, 0, ti, :],
                                      pmsk[:, 0, :], OP.mult)
                    eng.tensor_tensor(mr2tmp[:, :], r2acc[:, 1, ti, :],
                                      pmsk[:, 1, :], OP.mult)
                    mk.dr(eng)
                    mk.mark(eng.tensor_tensor(mr2b[:, :], mr2tok[:, :],
                                              mr2tmp[:, :], OP.add), "Vm")
                mk.v(vmr)
                prevB["vmr"] = mk.cnt["Vm"]
                vmrm = prevB["vmr"]

                def tmr(eng, vmrm=vmrm):
                    eng.wait_ge(mk.sems["Vm"], vmrm)
                    mk.mark(eng.transpose(pmr2[:, :], mr2b[:, :], idb[:]), "Tm")
                mk.t(tmr)
                tmrm = mk.cnt["Tm"]

                def vmr2(eng, tmrm=tmrm):
                    eng.wait_ge(mk.sems["Tm"], tmrm)
                    mk.dr(eng)
                    mk.mark(eng.tensor_copy(mr2cm[:, :], pmr2[:, :]), "Vm")
                mk.v(vmr2)
                vmr2m = mk.cnt["Vm"]

                def tw2(eng, ti=ti, vmr2m=vmr2m, slm=slm, qvm=qvm, p=dict(prevB)):
                    eng.wait_ge(mk.sems["Vm"], vmr2m)
                    eng.wait_ge(mk.sems[slm.name], qvm[0])
                    mk.w(eng, "Vm", p["vout"])
                    last = None
                    for half in range(2):
                        hs = slice(half * 512, (half + 1) * 512)
                        for j in range(NF):
                            eng.matmul(pout[:, ti % 2, hs], mstr[:, ti % 2, j, :],
                                       w2b[:, j, hs], start=(j == 0), stop=False)
                        last = eng.matmul(pout[:, ti % 2, hs], mr2cm[:, :],
                                          b2s[:, hs], start=False, stop=True)
                    mk.mark(last, "Tm")
                mk.t(tw2)
                slm.consumed("Tm")
                tw2m = mk.cnt["Tm"]

                def vout(eng, ti=ti, tw2m=tw2m, sls=sls, p=dict(prevB)):
                    eng.wait_ge(mk.sems["Tm"], tw2m)
                    if sls.guard is not None:
                        eng.wait_ge(mk.sems[sls.name], sls.guard[1])
                        sls.guard = None
                    eng.wait_ge(mk.sems["Sm"], p["sq"])
                    mk.dr(eng)
                    eng.tensor_copy(mixtok[:, ti % 2, :], pout[:, ti % 2, :])
                    mk.dr(eng)
                    mk.mark(eng.reduce_sum(mxsum[:, ti:ti + 1], mixtok[:, ti % 2, :],
                                           axis=AX.X), "Vm")
                mk.v(vout)
                prevB["vout"] = mk.cnt["Vm"]
                voutm = prevB["vout"]

                def sq(eng, ti=ti, voutm=voutm):
                    eng.wait_ge(mk.sems["Vm"], voutm)
                    mk.dr(eng)
                    mk.mark(eng.activation(junk4[:], mixtok[:, ti % 2, :], AF.Square,
                                           accum_out=mxsq[:, ti:ti + 1]), "Sm")
                mk.s(sq)
                prevB["sq"] = mk.cnt["Sm"]

                def yg(eng, ti=ti, sls=sls, voutm=voutm):
                    eng.wait_ge(mk.sems["Vm"], voutm)
                    mk.mark(eng.dma_start(out=mixbuf[ti * 128:(ti + 1) * 128, :],
                                          in_=mixtok[:, ti % 2, :]), sls.name, 16)
                mk.g(yg)
                sls.guard = (sls.name, mk.cnt[sls.name])
                mix_store_marks[ti] = (sls.name, mk.cnt[sls.name])

        mk.barrier()
        with ExitStack() as stk4:
            emit_residual_update(mk, stk4, "b_up", x_in, mixbuf, x_out,
                                 mxsum, mxsq, LS[0:6], SS[0:3], mix_store_marks)


def prep_l0ffn(norm_w, w1, w3, w2, gate_w, a1, b1, a3, b3, a2, b2):
    E, R = 8, 16
    return {
        "normw_b": np.ascontiguousarray(np.broadcast_to(norm_w, (128, 1024))),
        "gate_t": np.ascontiguousarray(gate_w.T),
        "w1_t": np.ascontiguousarray(w1.T),
        "w3_t": np.ascontiguousarray(w3.T),
        "w2_t": np.ascontiguousarray(w2.T),
        "a1_s": np.ascontiguousarray(a1.reshape(E * R, 1024).T),
        "a3_s": np.ascontiguousarray(a3.reshape(E * R, 1024).T),
        "b1_s": np.ascontiguousarray(b1.transpose(0, 2, 1).reshape(E * R, 4096)),
        "b3_s": np.ascontiguousarray(b3.transpose(0, 2, 1).reshape(E * R, 4096)),
        "a2_sr": np.ascontiguousarray(a2.reshape(E * R, 4096).T),
        "b2_s": np.ascontiguousarray(b2.transpose(0, 2, 1).reshape(E * R, 1024)),
        "expand": np.ascontiguousarray(
            2.0 * (np.arange(128)[None, :] // R == np.arange(E)[:, None])
        ).astype(np.float32),
    }


def emit_outproj(mk, stk, tag, acts_hbm, w_t_hbm, mixbuf, mxsum, mxsq,
                 act_ready_marks, mix_store_marks):
    LS = mk.lslots
    SS = mk.sslots
    wob = mk.sb(stk, tag + "_wob", [128, NC, D], BF16)
    wstage = mk.sb(stk, tag + "_wstg", [128, 2, D], F32)
    emit_load_cast(mk, wstage, tag + "_wo", w_t_hbm, wob, D, D, LS[4:6])
    astage = mk.sb(stk, tag + "_astg", [128, 2, NC, 128], BF16)
    mixtok = mk.sb(stk, tag + "_mixtok", [128, 2, D], F32)
    junk = mk.sb(stk, tag + "_junk", [128, D], F32)
    pout = mk.psum(stk, tag + "_pout", [128, 2, D], F32)
    wdone = mk.cnt["Vm"]
    acts_r = acts_hbm.rearrange("(j p) t -> p j t", p=128)
    prevO = {"vout": 0, "sq": 0}

    for i in range(NT):
        sls = SS[i % 2]
        sla = LS[10 + i % 2]
        qva = [None]

        def ya(eng, i=i, sla=sla, qva=qva):
            for m in act_ready_marks.get(i, []):
                eng.wait_ge(mk.sems[m[0]], m[1])
            qva[0] = sla.dma(eng, astage[:, i % 2, :, :],
                             acts_r[:, :, i * 128:(i + 1) * 128])
        mk.y(ya)

        def tf(eng, i=i, wdone=wdone, sla=sla, qva=qva, p=dict(prevO)):
            if i == 0:
                mk.w(eng, "Vm", wdone)
            mk.w(eng, "Vm", p["vout"])
            eng.wait_ge(mk.sems[sla.name], qva[0])
            last = None
            for j in range(NC):
                for half in range(2):
                    last = eng.matmul(
                        pout[:, i % 2, half * 512:(half + 1) * 512],
                        astage[:, i % 2, j, :],
                        wob[:, j, half * 512:(half + 1) * 512],
                        start=(j == 0), stop=(j == NC - 1))
            mk.mark(last, "Tm")
        mk.t(tf)
        sla.consumed("Tm")
        tfm = mk.cnt["Tm"]

        def vf(eng, i=i, sls=sls, tfm=tfm, p=dict(prevO)):
            eng.wait_ge(mk.sems["Tm"], tfm)
            if sls.guard is not None:
                eng.wait_ge(mk.sems[sls.name], sls.guard[1])
                sls.guard = None
            eng.wait_ge(mk.sems["Sm"], p["sq"])
            mk.dr(eng)
            eng.tensor_copy(mixtok[:, i % 2, :], pout[:, i % 2, :])
            mk.dr(eng)
            mk.mark(eng.reduce_sum(mxsum[:, i:i + 1], mixtok[:, i % 2, :],
                                   axis=AX.X), "Vm")
        mk.v(vf)
        prevO["vout"] = mk.cnt["Vm"]
        vv = prevO["vout"]

        def sq(eng, i=i, vv=vv):
            eng.wait_ge(mk.sems["Vm"], vv)
            mk.dr(eng)
            mk.mark(eng.activation(junk[:], mixtok[:, i % 2, :], AF.Square,
                                   accum_out=mxsq[:, i:i + 1]), "Sm")
        mk.s(sq)
        prevO["sq"] = mk.cnt["Sm"]

        def yg(eng, i=i, sls=sls, vv=vv):
            eng.wait_ge(mk.sems["Vm"], vv)
            mk.mark(eng.dma_start(out=mixbuf[i * 128:(i + 1) * 128, :],
                                  in_=mixtok[:, i % 2, :]), sls.name, 16)
        mk.g(yg)
        sls.guard = (sls.name, mk.cnt[sls.name])
        mix_store_marks[i] = (sls.name, mk.cnt[sls.name])


def phase_l0attn(mk, x_in, x_out, W, mixbuf, oattn_hbm, c0=0, c1=32,
                 s_in=None, s_out=None, oattn_in=None):
    LS = mk.lslots
    SS = mk.sslots
    H, HD = 16, 64
    NCH = 32
    with ExitStack() as stk:
        idf = mk.sb(stk, "a_idf", [128, 128], F32)
        idb = mk.sb(stk, "a_idb", [128, 128], BF16)
        umneg = mk.sb(stk, "a_umneg", [128, 128], F32)
        udiag = mk.sb(stk, "a_udiag", [128, 128], F32)
        lnsum = mk.sb(stk, "a_lnsum", [128, NT], F32)
        lnsq = mk.sb(stk, "a_lnsq", [128, NT], F32)
        rstd = mk.sb(stk, "a_rstd", [128, NT], F32)
        negb = mk.sb(stk, "a_negb", [128, NT], F32)
        mxsum = mk.sb(stk, "a_mxsum", [128, NT], F32)
        mxsq = mk.sb(stk, "a_mxsq", [128, NT], F32)
        rkcm = mk.sb(stk, "a_rkcm", [128, NC, T + 128], BF16)
        vtok = mk.sb(stk, "a_vtok", [128, NT, D], BF16)
        gb = mk.sb(stk, "a_gb", [128, NT, 32], F32)

        def yc(eng):
            mk.mark(eng.dma_start(out=idf[:, :], in_=W["ident"][:, :]), "CN", 16)
            mk.mark(eng.dma_start(out=umneg[:, :], in_=W["umaskneg"][:, :]), "CN", 16)
            mk.mark(eng.dma_start(out=udiag[:, :], in_=W["udiag"][:, :]), "CN", 16)
        mk.y(yc)
        cn0 = mk.cnt["CN"]

        def vcn(eng):
            eng.wait_ge(mk.sems["CN"], cn0)
            eng.memset(rkcm[:, :, 0:128], 0.0)
            mk.dr(eng)
            mk.mark(eng.tensor_copy(idb[:, :], idf[:, :]), "Vm")
        mk.v(vcn)

        with ExitStack() as stkln:
            emit_ln_stats(mk, stkln, "a_ln", x_in, lnsum, lnsq, LS[0:4])
            emit_stats_finalize(mk, stkln, "a_lnf", lnsum, lnsq, D, EPS_LN,
                                rstd, negb)

        with ExitStack() as stk1:
            wvb = mk.sb(stk1, "a_wvb", [128, NC, D], BF16)
            wgbb = mk.sb(stk1, "a_wgbb", [128, NC, 32], BF16)
            wstage = mk.sb(stk1, "a_wstg", [128, 2, D], F32)
            for _sl in (LS[4], LS[5]):
                _sl.guard = ("Sm", mk.cnt["Sm"])
                _sl.extra_guard = ("Vm", mk.cnt["Vm"])
            emit_load_cast(mk, wstage, "a_wv", W["wv_t"], wvb, D, D, LS[4:6])
            emit_load_cast(mk, wstage, "a_wgb", W["wgb_t"], wgbb, D, 32, LS[4:6])

            xin1 = mk.sb(stk1, "a_xin1", [128, 4, D], F32)
            ztmp = mk.sb(stk1, "a_ztmp", [128, 2, D], F32)
            zcb = mk.sb(stk1, "a_zcb", [128, 2, NC, 128], BF16)
            rktok = mk.sb(stk1, "a_rktok", [128, 2, D], BF16)
            rss = mk.sb(stk1, "a_rss", [128, 16], F32)
            rsc = mk.sb(stk1, "a_rsc", [128, 16], F32)
            junka = mk.sb(stk1, "a_junka", [128, D], F32)
            ptrz = mk.psum(stk1, "a_ptrz", [128, NC, 128], F32)
            ptrr = mk.psum(stk1, "a_ptrr", [128, NC, 128], BF16)
            psv = mk.psum(stk1, "a_psv", [128, 2, 512], F32)
            pg = mk.psum(stk1, "a_pg", [128, 32], F32)
            pA = {"vz": 0, "vv": 0, "vrk": 0, "vg": 0, "trz": 0, "trr": 0,
                  "tv": 0, "tg": 0, "srk": 0, "szq": 0}

            for i in range(NT):
                sl = LS[6 + i % 4]
                qv = [None]

                def yf(eng, i=i, sl=sl, qv=qv):
                    qv[0] = sl.dma(eng, xin1[:, i % 4, :],
                                   x_in[i * 128:(i + 1) * 128, :])
                mk.y(yf)

                def sf(eng, i=i, sl=sl, qv=qv, p=dict(pA)):
                    eng.wait_ge(mk.sems[sl.name], qv[0])
                    mk.w(eng, "Vm")
                    mk.w(eng, "Tm", p["trz"])
                    mk.w(eng, "Vm", p["vrk"])
                    mk.dr(eng)
                    eng.activation(ztmp[:, i % 2, :], xin1[:, i % 4, :],
                                   AF.Identity, bias=negb[:, i:i + 1],
                                   scale=rstd[:, i:i + 1])
                    mk.dr(eng)
                    last = None
                    for h in range(H):
                        last = eng.activation(junka[:, h * 64:(h + 1) * 64],
                                              ztmp[:, i % 2, h * 64:(h + 1) * 64],
                                              AF.Square,
                                              accum_out=rss[:, h:h + 1])
                    mk.mark(last, "Sm")
                mk.s(sf)
                sl.consumed("Sm")
                smz = mk.cnt["Sm"]

                def vsc(eng, i=i, smz=smz):
                    eng.wait_ge(mk.sems["Sm"], smz)
                    mk.dr(eng)
                    eng.tensor_scalar(rsc[:, :], rss[:, :], 1e-24, None, OP.max)
                    mk.dr(eng)
                    mk.mark(eng.reciprocal(rsc[:, :], rsc[:, :]), "Vm")
                mk.v(vsc)
                vscm = mk.cnt["Vm"]

                def ssc(eng, i=i, vscm=vscm):
                    eng.wait_ge(mk.sems["Vm"], vscm)
                    mk.dr(eng)
                    mk.mark(eng.activation(rsc[:, :], rsc[:, :], AF.Sqrt), "Sm")
                mk.s(ssc)
                sscm = mk.cnt["Sm"]

                def vrk(eng, i=i, sscm=sscm, p=dict(pA)):
                    eng.wait_ge(mk.sems["Sm"], sscm)
                    eng.wait_ge(mk.sems["Tm"], p["trr"])
                    mk.dr(eng)
                    last = None
                    for h in range(H):
                        last = eng.tensor_scalar(
                            rktok[:, i % 2, h * 64:(h + 1) * 64],
                            ztmp[:, i % 2, h * 64:(h + 1) * 64],
                            rsc[:, h:h + 1], None, OP.mult)
                    mk.mark(last, "Vm")
                mk.v(vrk)
                pA["vrk"] = mk.cnt["Vm"]
                vrkm = pA["vrk"]

                def trz(eng, i=i, smz=smz, p=dict(pA)):
                    eng.wait_ge(mk.sems["Sm"], smz)
                    mk.w(eng, "Vm", p["vz"])
                    last = None
                    for j in range(NC):
                        last = eng.transpose(ptrz[:, j, :],
                                             ztmp[:, i % 2, j * 128:(j + 1) * 128],
                                             idf[:])
                    mk.mark(last, "Tm")
                mk.t(trz)
                pA["trz"] = mk.cnt["Tm"]
                trzm = pA["trz"]

                def vz(eng, i=i, trzm=trzm, p=dict(pA)):
                    eng.wait_ge(mk.sems["Tm"], trzm)
                    eng.wait_ge(mk.sems["Tm"], p["tv"])
                    eng.wait_ge(mk.sems["Tm"], p["tg"])
                    mk.dr(eng)
                    mk.mark(eng.tensor_copy(zcb[:, i % 2, :, :], ptrz[:, :, :]),
                            "Vm")
                mk.v(vz)
                pA["vz"] = mk.cnt["Vm"]
                vzm = pA["vz"]

                def trr(eng, i=i, vrkm=vrkm, p=dict(pA)):
                    eng.wait_ge(mk.sems["Vm"], vrkm)
                    mk.w(eng, "Vm", p["vrkcm"] if "vrkcm" in p else 0)
                    last = None
                    for j in range(NC):
                        last = eng.transpose(ptrr[:, j, :],
                                             rktok[:, i % 2, j * 128:(j + 1) * 128],
                                             idb[:])
                    mk.mark(last, "Tm")
                mk.t(trr)
                pA["trr"] = mk.cnt["Tm"]
                trrm = pA["trr"]

                def vrkcm(eng, i=i, trrm=trrm):
                    eng.wait_ge(mk.sems["Tm"], trrm)
                    mk.dr(eng)
                    mk.mark(eng.tensor_copy(
                        rkcm[:, :, 128 + i * 128:128 + (i + 1) * 128],
                        ptrr[:, :, :]), "Vm")
                mk.v(vrkcm)
                pA["vrkcm"] = mk.cnt["Vm"]

                def tv(eng, i=i, vzm=vzm, p=dict(pA)):
                    eng.wait_ge(mk.sems["Vm"], vzm)
                    mk.w(eng, "Vm", p["vv"])
                    last = None
                    for half in range(2):
                        for j in range(NC):
                            last = eng.matmul(
                                psv[:, half, :], zcb[:, i % 2, j, :],
                                wvb[:, j, half * 512:(half + 1) * 512],
                                start=(j == 0), stop=(j == NC - 1))
                    mk.mark(last, "Tm")
                mk.t(tv)
                pA["tv"] = mk.cnt["Tm"]
                tvm = pA["tv"]

                def vv(eng, i=i, tvm=tvm):
                    eng.wait_ge(mk.sems["Tm"], tvm)
                    mk.dr(eng)
                    mk.mark(eng.tensor_copy(
                        vtok[:, i, :],
                        psv[:, :, :].rearrange("p a b -> p (a b)")), "Vm")
                mk.v(vv)
                pA["vv"] = mk.cnt["Vm"]

                def tg(eng, i=i, vzm=vzm, p=dict(pA)):
                    mk.w(eng, "Vm", p["vg"])
                    last = None
                    for j in range(NC):
                        last = eng.matmul(pg[:, :], zcb[:, i % 2, j, :],
                                          wgbb[:, j, :],
                                          start=(j == 0), stop=(j == NC - 1))
                    mk.mark(last, "Tm")
                mk.t(tg)
                pA["tg"] = mk.cnt["Tm"]
                tgm = pA["tg"]

                def vg(eng, i=i, tgm=tgm):
                    eng.wait_ge(mk.sems["Tm"], tgm)
                    mk.dr(eng)
                    mk.mark(eng.tensor_copy(gb[:, i, :], pg[:, :]), "Vm")
                mk.v(vg)
                pA["vg"] = mk.cnt["Vm"]

            allg = mk.cnt["Vm"]

            def sg(eng, allg=allg):
                eng.wait_ge(mk.sems["Vm"], allg)
                mk.dr(eng)
                mk.mark(eng.activation(
                    gb[:, :, :].rearrange("p a b -> p (a b)"),
                    gb[:, :, :].rearrange("p a b -> p (a b)"), AF.Sigmoid), "Sm")
            mk.s(sg)
            gates_ready = mk.cnt["Sm"]

        mk.barrier()

        ostore_marks = {}
        with ExitStack() as stk2:
            ecm = mk.sb(stk2, "a_ecm", [64, 2, 16, 128], BF16)
            fgs = mk.sb(stk2, "a_fgs", [128, 2, 16, 64], BF16)
            gcs = mk.sb(stk2, "a_gcs", [64, 2, 16, 64], BF16)
            pts = mk.sb(stk2, "a_pts", [64, 2, 16, 64], BF16)
            sfp = mk.sb(stk2, "a_sfp", [64, 16, 64], F32)
            sbb = mk.sb(stk2, "a_sbb", [64, 16, 64], BF16)
            ktk = mk.sb(stk2, "a_ktk", [128, 2, 64], BF16)
            krstg = mk.sb(stk2, "a_krstg", [64, 2, 2, 128], BF16)
            zbuf = mk.sb(stk2, "a_zbuf", [128, 2, 2, 128], BF16)
            lbuf = mk.sb(stk2, "a_lbuf", [128, 2, 128], BF16)
            mbuf = mk.sb(stk2, "a_mbuf", [128, 2, 128], BF16)
            uw = mk.sb(stk2, "a_uw", [128, 2, 128], BF16)
            tut = mk.sb(stk2, "a_tut", [128, 2, 128], BF16)
            otok = mk.sb(stk2, "a_otok", [128, 16, 64], BF16)
            ocms = mk.sb(stk2, "a_ocms", [128, 2, 8, 128], BF16)
            stmp = mk.sb(stk2, "a_stmp", [64, 16, 64], F32)
            psG = mk.psum(stk2, "a_psG", [128, 2, 128], F32)
            psZ = mk.psum(stk2, "a_psZ", [128, 2, 128], F32)
            psLM = mk.psum(stk2, "a_psLM", [128, 2, 2, 128], F32)
            ptb = mk.psum(stk2, "a_ptb", [128, 2, 2, 128], BF16)
            psO = mk.psum(stk2, "a_psO", [128, 8, 64], F32)
            psS = mk.psum(stk2, "a_psS", [64, 8, 64], F32)
            psOT = mk.psum(stk2, "a_psOT", [64, 8, 128], BF16)

            if s_in is None:
                def vinit(eng):
                    eng.memset(sfp[:, :, :], 0.0)
                    mk.dr(eng)
                    mk.mark(eng.memset(sbb[:, :, :], 0.0), "Vm")
                mk.v(vinit)
            else:
                sls_ = mk.lslots[0]
                qvs_ = [None]

                def yinit(eng):
                    qvs_[0] = sls_.dma(eng, sfp[:, :, :],
                                       s_in.rearrange("p (a b) -> p a b", b=64))
                mk.y(yinit)

                def vinit(eng):
                    eng.wait_ge(mk.sems[sls_.name], qvs_[0])
                    mk.dr(eng)
                    mk.mark(eng.tensor_copy(
                        sbb[:, :, :].rearrange("p a b -> p (a b)"),
                        sfp[:, :, :].rearrange("p a b -> p (a b)")), "Vm")
                mk.v(vinit)
                sls_.consumed("Vm")
            if oattn_in is not None:
                def ycp(eng):
                    mk.mark(eng.dma_start(out=oattn_hbm[:, 0:c0 * 128],
                                          in_=oattn_in[:, :]), "CN", 16)
                mk.y(ycp)
                ocp_mark = ("CN", mk.cnt["CN"])
            else:
                ocp_mark = None

            ph1m = [None] * NCH

            def hsl(h):
                return (slice((h % 2) * 64, (h % 2) * 64 + 64), h // 2)

            def emit_ph1(c):
                cb = c % 2
                kof = 127 + c * 128
                rof = 128 + c * 128
                for gidx in range(8):
                    heads = [2 * gidx, 2 * gidx + 1]

                    gpar = gidx % 2

                    def v0(eng, c=c, heads=heads, kof=kof, rof=rof, gpar=gpar):
                        mk.w(eng, "Tm")
                        mk.dr(eng)
                        h1 = heads[1]
                        ps_, ch = hsl(h1)
                        eng.tensor_copy(krstg[:, gpar, 0, :],
                                        rkcm[ps_, ch, kof:kof + 128])
                        mk.mark(eng.tensor_copy(krstg[:, gpar, 1, :],
                                                rkcm[ps_, ch, rof:rof + 128]),
                                "Vm")
                    mk.v(v0)
                    v0m = mk.cnt["Vm"]

                    def t1(eng, c=c, heads=heads, kof=kof, v0m=v0m, gpar=gpar):
                        mk.w(eng, "Vm")
                        last = None
                        h0, h1 = heads
                        ps0, ch0 = hsl(h0)
                        eng.matmul(psG[:, 0, :],
                                   rkcm[ps0, ch0, kof:kof + 128],
                                   rkcm[ps0, ch0, kof:kof + 128],
                                   start=True, stop=True)
                        eng.matmul(psG[:, 1, :], krstg[:, gpar, 0, :],
                                   krstg[:, gpar, 0, :], start=True, stop=True)
                        for ii, h in enumerate(heads):
                            ps_, ch = hsl(h)
                            iid = idb[0:64, 0:64] if h % 2 == 0 else idb[64:128, 64:128]
                            last = eng.transpose(ptb[:, 0, ii, 0:64],
                                                 rkcm[ps_, ch, kof:kof + 128],
                                                 iid)
                        mk.mark(last, "Tm")
                    mk.t(t1)
                    t1m = mk.cnt["Tm"]

                    def v1(eng, c=c, heads=heads, t1m=t1m):
                        eng.wait_ge(mk.sems["Tm"], t1m)
                        mk.dr(eng)
                        eng.tensor_copy(ktk[:, :, :], ptb[:, 0, :, 0:64])
                        mk.dr(eng)
                        for ii, h in enumerate(heads):
                            eng.tensor_tensor(lbuf[:, ii, :], psG[:, ii, :],
                                              umneg[:, :], OP.mult)
                        mk.dr(eng)
                        for ii, h in enumerate(heads):
                            eng.tensor_scalar(lbuf[:, ii, :], lbuf[:, ii, :],
                                              gb[:, c, 16 + h:17 + h], None,
                                              OP.mult)
                        mk.dr(eng)
                        last = None
                        for ii, h in enumerate(heads):
                            eng.tensor_copy(zbuf[:, 0, ii, 0:64],
                                            vtok[:, c, h * 64:(h + 1) * 64])
                            last = eng.tensor_copy(zbuf[:, 0, ii, 64:128],
                                                   ktk[:, ii, :])
                        mk.mark(last, "Vm")
                    mk.v(v1)
                    v1m = mk.cnt["Vm"]

                    def t2(eng, v1m=v1m):
                        eng.wait_ge(mk.sems["Vm"], v1m)
                        last = None
                        for ii in range(2):
                            last = eng.transpose(ptb[:, 1, ii, :], lbuf[:, ii, :],
                                                 idb[:])
                        mk.mark(last, "Tm")
                    mk.t(t2)
                    t2m = mk.cnt["Tm"]

                    def v2(eng, t2m=t2m):
                        eng.wait_ge(mk.sems["Tm"], t2m)
                        mk.dr(eng)
                        mk.mark(eng.tensor_copy(mbuf[:, :, :], ptb[:, 1, :, :]),
                                "Vm")
                    mk.v(v2)
                    lastv = mk.cnt["Vm"]

                    zcur = 0
                    for lev in range(7):
                        def tz(eng, lev=lev, zcur=zcur, lastv=lastv):
                            eng.wait_ge(mk.sems["Vm"], lastv)
                            last = None
                            for ii in range(2):
                                last = eng.matmul(psZ[:, ii, :], lbuf[:, ii, :],
                                                  zbuf[:, zcur, ii, :],
                                                  start=True, stop=True)
                            if lev < 6:
                                for ii in range(2):
                                    last = eng.matmul(psLM[:, 0, ii, :],
                                                      mbuf[:, ii, :],
                                                      lbuf[:, ii, :],
                                                      start=True, stop=True)
                                if lev < 5:
                                    for ii in range(2):
                                        last = eng.matmul(psLM[:, 1, ii, :],
                                                          lbuf[:, ii, :],
                                                          mbuf[:, ii, :],
                                                          start=True, stop=True)
                            mk.mark(last, "Tm")
                        mk.t(tz)
                        tzm = mk.cnt["Tm"]

                        def vz2(eng, lev=lev, zcur=zcur, tzm=tzm):
                            eng.wait_ge(mk.sems["Tm"], tzm)
                            mk.dr(eng)
                            ztt = eng.tensor_tensor(
                                zbuf[:, 1 - zcur, :, :].rearrange("p a b -> p (a b)"),
                                zbuf[:, zcur, :, :].rearrange("p a b -> p (a b)"),
                                psZ[:, :, :].rearrange("p a b -> p (a b)"), OP.add)
                            last = ztt
                            if lev < 6:
                                mk.dr(eng)
                                last = eng.tensor_copy(
                                    lbuf[:, :, :].rearrange("p a b -> p (a b)"),
                                    psLM[:, 0, :, :].rearrange("p a b -> p (a b)"))
                                if lev < 5:
                                    mk.dr(eng)
                                    last = eng.tensor_copy(
                                        mbuf[:, :, :].rearrange("p a b -> p (a b)"),
                                        psLM[:, 1, :, :].rearrange("p a b -> p (a b)"))
                            mk.mark(last, "Vm")
                        mk.v(vz2)
                        lastv = mk.cnt["Vm"]
                        zcur = 1 - zcur

                    def t3(eng, heads=heads, kof=kof, rof=rof, lastv=lastv,
                           gpar=gpar):
                        eng.wait_ge(mk.sems["Vm"], lastv)
                        last = None
                        h0, h1 = heads
                        ps0, ch0 = hsl(h0)
                        eng.matmul(psG[:, 0, :],
                                   rkcm[ps0, ch0, kof:kof + 128],
                                   rkcm[ps0, ch0, rof:rof + 128],
                                   start=True, stop=True)
                        last = eng.matmul(psG[:, 1, :], krstg[:, gpar, 0, :],
                                          krstg[:, gpar, 1, :],
                                          start=True, stop=True)
                        mk.mark(last, "Tm")
                    mk.t(t3)
                    t3m = mk.cnt["Tm"]

                    def v3(eng, c=c, heads=heads, t3m=t3m, zcur=zcur):
                        eng.wait_ge(mk.sems["Tm"], t3m)
                        mk.dr(eng)
                        for ii, h in enumerate(heads):
                            eng.tensor_scalar(uw[:, ii, :], zbuf[:, zcur, ii, :],
                                              gb[:, c, 16 + h:17 + h], None,
                                              OP.mult)
                        mk.dr(eng)
                        last = None
                        for ii in range(2):
                            last = eng.tensor_tensor(tut[:, ii, :], psG[:, ii, :],
                                                     udiag[:, :], OP.mult)
                        mk.mark(last, "Vm")
                    mk.v(v3)
                    v3m = mk.cnt["Vm"]

                    def t4(eng, v3m=v3m):
                        eng.wait_ge(mk.sems["Vm"], v3m)
                        last = None
                        for ii in range(2):
                            eng.matmul(psG[:, ii, 0:64], tut[:, ii, :],
                                       uw[:, ii, 0:64], start=True, stop=True)
                            eng.matmul(psZ[0:64, ii, :], uw[:, ii, 64:128],
                                       tut[:, ii, :], start=True, stop=True)
                            eng.matmul(psLM[0:64, 0, ii, 0:64], ktk[:, ii, :],
                                       uw[:, ii, 0:64], start=True, stop=True)
                            last = eng.matmul(psLM[0:64, 0, ii, 64:128],
                                              uw[:, ii, 64:128], ktk[:, ii, :],
                                              start=True, stop=True)
                        mk.mark(last, "Tm")
                    mk.t(t4)
                    t4m = mk.cnt["Tm"]

                    def v4(eng, c=c, cb=cb, heads=heads, t4m=t4m):
                        eng.wait_ge(mk.sems["Tm"], t4m)
                        mk.dr(eng)
                        last = None
                        for ii, h in enumerate(heads):
                            ps_, ch = hsl(h)
                            eng.tensor_tensor(
                                ecm[:, cb, h, :],
                                rkcm[ps_, ch, 128 + c * 128:128 + (c + 1) * 128],
                                psZ[0:64, ii, :], OP.subtract)
                            eng.tensor_scalar(fgs[:, cb, h, :], psG[:, ii, 0:64],
                                              gb[:, c, h:h + 1], None, OP.mult)
                            eng.tensor_copy(gcs[:, cb, h, :],
                                            psLM[0:64, 0, ii, 0:64])
                            last = eng.tensor_copy(pts[:, cb, h, :],
                                                   psLM[0:64, 0, ii, 64:128])
                        mk.mark(last, "Vm")
                    mk.v(v4)
                ph1m[c] = mk.cnt["Vm"]

            def emit_ph2(c):
                cb = c % 2
                so = SS[2 + c % 2]
                for hb in range(2):
                    def tes(eng, c=c, cb=cb, hb=hb):
                        mk.w(eng, "Vm")
                        last = None
                        for hh in range(8):
                            h = hb * 8 + hh
                            last = eng.matmul(psO[:, hh, :],
                                              ecm[:, cb, h, :],
                                              sbb[:, h, :], start=True, stop=True)
                        for hh in range(8):
                            h = hb * 8 + hh
                            last = eng.matmul(psS[:, hh, :], pts[:, cb, h, :],
                                              sbb[:, h, :], start=True, stop=True)
                        mk.mark(last, "Tm")
                    mk.t(tes)
                    tesm = mk.cnt["Tm"]

                    def vo(eng, c=c, cb=cb, hb=hb, tesm=tesm):
                        eng.wait_ge(mk.sems["Tm"], tesm)
                        mk.dr(eng)
                        hs = slice(hb * 8, hb * 8 + 8)
                        for hh in range(8):
                            h = hb * 8 + hh
                            eng.scalar_tensor_tensor(
                                otok[:, h, :], psO[:, hh, :], gb[:, c, h:h + 1],
                                fgs[:, cb, h, :], OP.mult, OP.add)
                        mk.dr(eng)
                        eng.tensor_tensor(
                            stmp[:, hs, :].rearrange("p a b -> p (a b)"),
                            gcs[:, cb, hs, :].rearrange("p a b -> p (a b)"),
                            psS[:, :, :].rearrange("p a b -> p (a b)"),
                            OP.subtract)
                        mk.dr(eng)
                        mk.mark(eng.tensor_tensor(
                            sfp[:, hs, :].rearrange("p a b -> p (a b)"),
                            sfp[:, hs, :].rearrange("p a b -> p (a b)"),
                            stmp[:, hs, :].rearrange("p a b -> p (a b)"),
                            OP.add), "Vm")
                    mk.v(vo)
                vsplit = mk.cnt["Vm"]

                def vsb(eng, vsplit=vsplit):
                    eng.wait_ge(mk.sems["Vm"], vsplit)
                    mk.dr(eng)
                    mk.mark(eng.tensor_copy(
                        sbb[:, :, :].rearrange("p a b -> p (a b)"),
                        sfp[:, :, :].rearrange("p a b -> p (a b)")), "Vm")
                mk.v(vsb)
                vom = mk.cnt["Vm"]

                for hb in range(2):
                    def tot(eng, hb=hb, vom=vom):
                        if hb == 0:
                            eng.wait_ge(mk.sems["Vm"], vom)
                        mk.w(eng, "Vm")
                        last = None
                        for hh in range(8):
                            h = hb * 8 + hh
                            last = eng.transpose(psOT[:, hh, :], otok[:, h, :],
                                                 idb[:])
                        mk.mark(last, "Tm")
                    mk.t(tot)
                    totm = mk.cnt["Tm"]

                    def voc(eng, c=c, cb=cb, hb=hb, totm=totm, so=so):
                        eng.wait_ge(mk.sems["Tm"], totm)
                        if so.guard is not None:
                            eng.wait_ge(mk.sems[so.name], so.guard[1])
                            so.guard = None
                        mk.dr(eng)
                        last = None
                        for hh in range(8):
                            h = hb * 8 + hh
                            last = eng.tensor_copy(
                                ocms[(h % 2) * 64:(h % 2) * 64 + 64, cb, h // 2, :],
                                psOT[:, hh, :])
                        mk.mark(last, "Vm")
                    mk.v(voc)
                vocm = mk.cnt["Vm"]

                def yo(eng, c=c, cb=cb, so=so, vocm=vocm):
                    eng.wait_ge(mk.sems["Vm"], vocm)
                    mk.mark(eng.dma_start(
                        out=oattn_hbm.rearrange("(j p) t -> p j t", p=128)[
                            :, :, c * 128:(c + 1) * 128],
                        in_=ocms[:, cb, :, :]), so.name, 16)
                mk.g(yo)
                so.guard = (so.name, mk.cnt[so.name])
                ostore_marks[c] = (so.name, mk.cnt[so.name])

            def vwait(eng, gates_ready=gates_ready):
                eng.wait_ge(mk.sems["Sm"], gates_ready)
                mk.mark(eng.engine_nop(), "Vm")
            mk.v(vwait)

            emit_ph1(c0)
            for c in range(c0 + 1, c1):
                emit_ph1(c)
                emit_ph2(c - 1)
            emit_ph2(c1 - 1)

            if s_out is not None:
                svm = mk.cnt["Vm"]
                sst = SS[0]

                def ys(eng, svm=svm, sst=sst):
                    eng.wait_ge(mk.sems["Vm"], svm)
                    mk.mark(eng.dma_start(
                        out=s_out.rearrange("p (a b) -> p a b", b=64),
                        in_=sfp[:, :, :]), sst.name, 16)
                    mk.w(eng, sst.name)
                mk.g(ys)

        mk.barrier()
        if c1 < NCH:
            return

        mix_store_marks = {}
        act_marks = {i: [ostore_marks[i]] for i in ostore_marks}
        if ocp_mark is not None:
            for i in range(c0):
                act_marks[i] = [ocp_mark]
        with ExitStack() as stk3:
            emit_outproj(mk, stk3, "a_op", oattn_hbm, W["wo_t"], mixbuf,
                         mxsum, mxsq, act_marks, mix_store_marks)
        mk.barrier()
        with ExitStack() as stk4:
            emit_residual_update(mk, stk4, "a_up", x_in, mixbuf, x_out,
                                 mxsum, mxsq, LS[0:6], SS[0:3], mix_store_marks)


def prep_l0attn(Wv, Wo, Wg, Wb):
    iu = np.triu(np.ones((128, 128), np.float32), 1)
    ud = np.triu(np.ones((128, 128), np.float32), 0)
    return {
        "wv_t": np.ascontiguousarray(Wv.T),
        "wo_t": np.ascontiguousarray(Wo.T),
        "wgb_t": np.ascontiguousarray(np.concatenate([Wg, Wb], 0).T),
        "umaskneg": -iu,
        "udiag": ud,
    }




_PROGS = {}


def _build_a1(prep):
    mk = MK()
    nc = mk.nc
    xd = nc.dram_tensor("x", [T, D], F32, kind="ExternalInput").ap()
    oattn = nc.dram_tensor("oattn1", [D, 2048], BF16, kind="ExternalOutput").ap()
    sout = nc.dram_tensor("s_state", [64, 1024], F32, kind="ExternalOutput").ap()
    mixbuf = nc.dram_tensor("mixbuf", [T, D], F32).ap()
    W = {k: nc.dram_tensor("l0a_" + k, list(v.shape), F32, kind="ExternalInput").ap()
         for k, v in prep.items()}
    W["ident"] = nc.dram_tensor("ident", [128, 128], F32, kind="ExternalInput").ap()
    mk.start()
    phase_l0attn(mk, xd, None, W, mixbuf, oattn, c0=0, c1=16, s_out=sout)
    mk.stack.close()
    return nc


def _build_a2(prep):
    mk = MK()
    nc = mk.nc
    xd = nc.dram_tensor("x", [T, D], F32, kind="ExternalInput").ap()
    yd = nc.dram_tensor("y", [T, D], F32, kind="ExternalOutput").ap()
    oin = nc.dram_tensor("oattn1", [D, 2048], BF16, kind="ExternalInput").ap()
    sin = nc.dram_tensor("s_state", [64, 1024], F32, kind="ExternalInput").ap()
    oattn = nc.dram_tensor("oattn_scr", [D, T], BF16).ap()
    mixbuf = nc.dram_tensor("mixbuf", [T, D], F32).ap()
    W = {k: nc.dram_tensor("l0a_" + k, list(v.shape), F32, kind="ExternalInput").ap()
         for k, v in prep.items()}
    W["ident"] = nc.dram_tensor("ident", [128, 128], F32, kind="ExternalInput").ap()
    mk.start()
    phase_l0attn(mk, xd, yd, W, mixbuf, oattn, c0=16, c1=32, s_in=sin,
                 oattn_in=oin)
    mk.stack.close()
    return nc


def _build_b(prep):
    mk = MK()
    nc = mk.nc
    xd = nc.dram_tensor("x", [T, D], F32, kind="ExternalInput").ap()
    yd = nc.dram_tensor("y", [T, D], F32, kind="ExternalOutput").ap()
    mixbuf = nc.dram_tensor("mixbuf", [T, D], F32).ap()
    mixed_hbm = nc.dram_tensor("mixed_scr", [NT, 4096, 128], BF16).ap()
    nx_hbm = nc.dram_tensor("nx_scr", [D, T], BF16).ap()
    W = {k: nc.dram_tensor("l0f_" + k, list(v.shape), F32, kind="ExternalInput").ap()
         for k, v in prep.items()}
    W["ident"] = nc.dram_tensor("ident", [128, 128], F32, kind="ExternalInput").ap()
    mk.start()
    phase_l0ffn(mk, xd, yd, W, mixbuf, mixed_hbm, nx_hbm)
    mk.stack.close()
    return nc


def _build_c(prep):
    mk = MK()
    nc = mk.nc
    xd = nc.dram_tensor("x", [T, D], F32, kind="ExternalInput").ap()
    yd = nc.dram_tensor("y", [T, D], F32, kind="ExternalOutput").ap()
    mixbuf = nc.dram_tensor("mixbuf", [T, D], F32).ap()
    qk_hbm = nc.dram_tensor("qk_scr", [2048, T], BF16).ap()
    attn_hbm = nc.dram_tensor("attn_scr", [1024, T], BF16).ap()
    W = {k: nc.dram_tensor("l1a_" + k, list(v.shape), F32, kind="ExternalInput").ap()
         for k, v in prep.items()}
    W["ident"] = nc.dram_tensor("ident", [128, 128], F32, kind="ExternalInput").ap()
    W["ones"] = nc.dram_tensor("ones", [128, 128], F32, kind="ExternalInput").ap()
    mk.start()
    phase_l1attn(mk, xd, yd, W, mixbuf, qk_hbm, attn_hbm)
    mk.stack.close()
    return nc


def _build_d(prep):
    mk = MK()
    nc = mk.nc
    xd = nc.dram_tensor("x", [T, D], F32, kind="ExternalInput").ap()
    yd = nc.dram_tensor("y", [T, D], F32, kind="ExternalOutput").ap()
    mixbuf = nc.dram_tensor("mixbuf", [T, D], F32).ap()
    W = {k: nc.dram_tensor("l1f_" + k, list(v.shape), F32, kind="ExternalInput").ap()
         for k, v in prep.items()}
    W["ident"] = nc.dram_tensor("ident", [128, 128], F32, kind="ExternalInput").ap()
    mk.start()
    phase_l1ffn(mk, xd, yd, W, mixbuf)
    mk.stack.close()
    return nc


def kernel(L0_attn_x, L0_attn_Wv, L0_attn_Wo, L0_attn_Wg, L0_attn_Wb,
           L0_ffn_norm_w, L0_ffn_w1, L0_ffn_w3, L0_ffn_w2, L0_ffn_gate_w,
           L0_ffn_a1, L0_ffn_b1, L0_ffn_a3, L0_ffn_b3, L0_ffn_a2, L0_ffn_b2,
           L1_attn_w_qkv, L1_attn_w_out, L1_ffn_W_r, L1_ffn_W_k, L1_ffn_W_v,
           L1_ffn_mu_r, L1_ffn_mu_k):
    ncores = 8
    xs = [np.ascontiguousarray(np.asarray(L0_attn_x)[i].astype(np.float32,
                                                              copy=False))
          for i in range(ncores)]
    pa = prep_l0attn(np.asarray(L0_attn_Wv), np.asarray(L0_attn_Wo),
                     np.asarray(L0_attn_Wg), np.asarray(L0_attn_Wb))
    pb = prep_l0ffn(np.asarray(L0_ffn_norm_w), np.asarray(L0_ffn_w1),
                    np.asarray(L0_ffn_w3), np.asarray(L0_ffn_w2),
                    np.asarray(L0_ffn_gate_w), np.asarray(L0_ffn_a1),
                    np.asarray(L0_ffn_b1), np.asarray(L0_ffn_a3),
                    np.asarray(L0_ffn_b3), np.asarray(L0_ffn_a2),
                    np.asarray(L0_ffn_b2))
    pc = prep_l1attn(np.asarray(L1_attn_w_qkv), np.asarray(L1_attn_w_out))
    pd = prep_l1ffn(np.asarray(L1_ffn_W_r), np.asarray(L1_ffn_W_k),
                    np.asarray(L1_ffn_W_v), np.asarray(L1_ffn_mu_r),
                    np.asarray(L1_ffn_mu_k))
    ident = np.eye(128, dtype=np.float32)
    ones = np.ones((128, 128), dtype=np.float32)

    if "a1" not in _PROGS:
        _PROGS["a1"] = _build_a1(pa)
        _PROGS["a2"] = _build_a2(pa)
        _PROGS["b"] = _build_b(pb)
        _PROGS["c"] = _build_c(pc)
        _PROGS["d"] = _build_d(pd)

    core_ids = list(range(ncores))

    base_a = {"ident": ident}
    for k, v in pa.items():
        base_a["l0a_" + k] = v
    maps = [dict(base_a, x=xs[i]) for i in range(ncores)]
    r1 = run_bass_kernel_spmd(_PROGS["a1"], maps, core_ids=core_ids)

    maps = [dict(base_a, x=xs[i], oattn1=r1.results[i]["oattn1"],
                 s_state=r1.results[i]["s_state"]) for i in range(ncores)]
    r2 = run_bass_kernel_spmd(_PROGS["a2"], maps, core_ids=core_ids)
    x1 = [r2.results[i]["y"] for i in range(ncores)]

    base_b = {"ident": ident}
    for k, v in pb.items():
        base_b["l0f_" + k] = v
    maps = [dict(base_b, x=x1[i]) for i in range(ncores)]
    r3 = run_bass_kernel_spmd(_PROGS["b"], maps, core_ids=core_ids)
    x2 = [r3.results[i]["y"] for i in range(ncores)]

    base_c = {"ident": ident, "ones": ones}
    for k, v in pc.items():
        base_c["l1a_" + k] = v
    maps = [dict(base_c, x=x2[i]) for i in range(ncores)]
    r4 = run_bass_kernel_spmd(_PROGS["c"], maps, core_ids=core_ids)
    x3 = [r4.results[i]["y"] for i in range(ncores)]

    base_d = {"ident": ident}
    for k, v in pd.items():
        base_d["l1f_" + k] = v
    maps = [dict(base_d, x=x3[i]) for i in range(ncores)]
    r5 = run_bass_kernel_spmd(_PROGS["d"], maps, core_ids=core_ids)

    out = np.stack([np.asarray(r5.results[i]["y"]) for i in range(ncores)], axis=0)
    return out.astype(np.float32, copy=False)


# version 02
# speedup vs baseline: 1.0440x; 1.0440x over previous
import numpy as np
from contextlib import ExitStack
import concourse.bass as bass
import concourse.mybir as mybir
from concourse.bass_utils import run_bass_kernel_spmd

F32 = mybir.dt.float32
BF16 = mybir.dt.bfloat16
AF = mybir.ActivationFunctionType
OP = mybir.AluOpType
AX = mybir.AxisListType

T = 4096
D = 1024
NT = T // 128
NC = D // 128
EPS_LN = 1e-5

N_LOAD_SLOTS = 12
N_STORE_SLOTS = 4


class Slot:

    def __init__(self, mk, name):
        self.mk = mk
        self.name = name
        self.guard = None
        self.extra_guard = None

    def dma(self, eng, out, in_):
        mk = self.mk
        if self.guard is not None:
            eng.wait_ge(mk.sems[self.guard[0]], self.guard[1])
            self.guard = None
        if self.extra_guard is not None:
            eng.wait_ge(mk.sems[self.extra_guard[0]], self.extra_guard[1])
            self.extra_guard = None
        mk.mark(eng.dma_start(out=out, in_=in_), self.name, 16)
        return mk.cnt[self.name]

    def consumed(self, sem_name):
        self.guard = (sem_name, self.mk.cnt[sem_name])


class MK:
    def __init__(self):
        self.nc = bass.Bass("TRN2", target_bir_lowering=False)
        self.stack = ExitStack()
        self.cnt = {}
        self.sems = {}
        self.blk = None
        self.drains = True

    def start(self):
        names = ["CN", "Tm", "Vm", "Sm", "Gm", "PH"]
        names += ["L%d" % i for i in range(N_LOAD_SLOTS)]
        names += ["St%d" % i for i in range(N_STORE_SLOTS)]
        for n in names:
            self.sems[n] = self.stack.enter_context(self.nc.semaphore(n))
            self.cnt[n] = 0
        self.lslots = [Slot(self, "L%d" % i) for i in range(N_LOAD_SLOTS)]
        self.sslots = [Slot(self, "St%d" % i) for i in range(N_STORE_SLOTS)]
        def boot(eng):
            for n in ("Tm", "Vm", "Sm", "Gm"):
                eng.sem_inc(self.sems[n], 1024)
                self.cnt[n] = 1024
        self.y(boot)

    def sb(self, stk, name, shape, dt=F32):
        return stk.enter_context(self.nc.sbuf_tensor(name, shape, dt))

    def psum(self, stk, name, shape, dt=F32):
        return stk.enter_context(self.nc.psum_tensor(name, shape, dt))

    def _section(self, kind, fn):
        if self.blk is None:
            self.blk = self.nc.Block()
            self.blk.__enter__()
        getattr(self.blk, kind)(fn)

    def flush(self):
        if self.blk is not None:
            self.blk.__exit__(None, None, None)
            self.blk = None

    def t(self, fn):
        self._section("tensor", fn)

    def v(self, fn):
        self._section("vector", fn)

    def s(self, fn):
        self._section("scalar", fn)

    def g(self, fn):
        self._section("gpsimd", fn)

    def y(self, fn):
        self._section("sync", fn)

    def finish(self):
        self.flush()
        self.stack.close()

    def mark(self, instr, sem, k=1):
        self.cnt[sem] += k
        instr.then_inc(self.sems[sem], k)
        return self.cnt[sem]

    def w(self, eng, sem, val=None, const=False):
        v = self.cnt[sem] if val is None else val
        lr = getattr(self, "loop_rates", None)
        if lr is not None and not const:
            r = lr.get(sem, 0)
            if r:
                eng.wait_ge(self.sems[sem], self.loop_iv * r + v)
                return
        eng.wait_ge(self.sems[sem], v)

    def new_sems(self, names):
        for n in names:
            self.sems[n] = self.stack.enter_context(self.nc.semaphore(n))
            self.cnt[n] = 0

    def loop_enter(self, rates, iv):
        self.loop_rates = rates
        self.loop_base = dict(self.cnt)
        self.loop_iv = iv

    def loop_exit(self, extra_iters):
        for s, r in self.loop_rates.items():
            self.cnt[s] += r * extra_iters
        self.loop_rates = None

    def dr(self, eng):
        if self.drains:
            eng.drain()

    def barrier(self):
        mk = self
        tgt = {k: v for k, v in self.cnt.items()
               if v and k not in ("PH", "Vm")}

        def vf(eng):
            for k, v in tgt.items():
                if v:
                    eng.wait_ge(mk.sems[k], v)
            mk.mark(eng.engine_nop(), "PH")
        self.v(vf)
        ph = self.cnt["PH"]

        def wf(eng):
            eng.wait_ge(mk.sems["PH"], ph)
        for e in (self.t, self.s, self.g, self.y):
            e(wf)






def emit_ln_stats(mk, stk, tag, x_hbm, sums, sqs, slots, ntiles=NT):
    nbuf = len(slots)
    xb = mk.sb(stk, tag + "_xb", [128, nbuf, D], F32)
    junk = mk.sb(stk, tag + "_junk", [128, D], F32)
    for i in range(ntiles):
        b = i % nbuf
        sl = slots[b]
        qv = [None]

        def yf(eng, i=i, b=b, sl=sl, qv=qv):
            qv[0] = sl.dma(eng, xb[:, b, :], x_hbm[i * 128:(i + 1) * 128, :])
        mk.y(yf)

        def vf(eng, i=i, b=b, sl=sl, qv=qv):
            eng.wait_ge(mk.sems[sl.name], qv[0])
            mk.mark(eng.reduce_sum(sums[:, i:i + 1], xb[:, b, :], axis=AX.X), "Vm")
        mk.v(vf)
        vmark = mk.cnt["Vm"]

        def sf(eng, i=i, b=b, sl=sl, qv=qv):
            eng.wait_ge(mk.sems[sl.name], qv[0])
            mk.dr(eng)
            mk.mark(eng.activation(junk[:], xb[:, b, :], AF.Square,
                                   accum_out=sqs[:, i:i + 1]), "Sm")
        mk.s(sf)
        sl.guard = ("Sm", mk.cnt["Sm"])
        sl.extra_guard = ("Vm", vmark)




def emit_stats_finalize(mk, stk, tag, sums, sqs, n, eps, rstd, negb, ntiles=NT):
    mean = mk.sb(stk, tag + "_mean", [128, ntiles], F32)
    msq = mk.sb(stk, tag + "_msq", [128, ntiles], F32)
    var = mk.sb(stk, tag + "_var", [128, ntiles], F32)

    def vf(eng):
        mk.w(eng, "Sm")
        mk.dr(eng)
        eng.tensor_scalar(mean[:], sums[:], 1.0 / n, None, OP.mult)
        mk.dr(eng)
        eng.tensor_tensor(msq[:], mean[:], mean[:], OP.mult)
        eng.tensor_scalar(var[:], sqs[:], 1.0 / n, eps, OP.mult, OP.add)
        mk.dr(eng)
        eng.tensor_tensor(var[:], var[:], msq[:], OP.subtract)
        mk.dr(eng)
        mk.mark(eng.reciprocal(var[:], var[:]), "Vm")
    mk.v(vf)

    def sf(eng):
        mk.w(eng, "Vm")
        mk.mark(eng.activation(rstd[:], var[:], AF.Sqrt), "Sm")
    mk.s(sf)

    def vf2(eng):
        mk.w(eng, "Sm")
        mk.dr(eng)
        mk.mark(eng.scalar_tensor_tensor(negb[:], mean[:], -1.0, rstd[:],
                                         OP.mult, OP.mult), "Vm")
    mk.v(vf2)


def emit_load_cast(mk, stage, tag, w_hbm, dst_bf, rows, cols, slots):
    nchunk = rows // 128
    for j in range(nchunk):
        b = j % 2
        sl = slots[b]
        qv = [None]

        def yf(eng, j=j, b=b, sl=sl, qv=qv):
            qv[0] = sl.dma(eng, stage[:, b, 0:cols], w_hbm[j * 128:(j + 1) * 128, :])
        mk.y(yf)

        def vf(eng, j=j, b=b, sl=sl, qv=qv):
            eng.wait_ge(mk.sems[sl.name], qv[0])
            mk.mark(eng.tensor_copy(dst_bf[:, j, :], stage[:, b, 0:cols]), "Vm")
        mk.v(vf)
        sl.consumed("Vm")


def emit_residual_update(mk, stk, tag, x_in, mix_hbm, x_out, sums, sqs,
                         lslots, sslots, mix_store_marks):
    rstd = mk.sb(stk, tag + "_rstd", [128, NT], F32)
    negb = mk.sb(stk, tag + "_negb", [128, NT], F32)
    emit_stats_finalize(mk, stk, tag + "_fin", sums, sqs, D, EPS_LN, rstd, negb)

    xb = mk.sb(stk, tag + "_uxb", [128, 3, D], F32)
    mb = mk.sb(stk, tag + "_umb", [128, 3, D], F32)
    ob = mk.sb(stk, tag + "_uob", [128, 3, D], F32)
    lntmp = mk.sb(stk, tag + "_ulnt", [128, 2, D], F32)
    for i in range(NT):
        b = i % 3
        slx, slm, sls = lslots[b], lslots[3 + b], sslots[b]
        qv = [None, None]

        def yf(eng, i=i, b=b, slx=slx, slm=slm, qv=qv):
            ms = mix_store_marks.get(i)
            if ms is not None:
                eng.wait_ge(mk.sems[ms[0]], ms[1])
            qv[0] = slx.dma(eng, xb[:, b, :], x_in[i * 128:(i + 1) * 128, :])
            qv[1] = slm.dma(eng, mb[:, b, :], mix_hbm[i * 128:(i + 1) * 128, :])
        mk.y(yf)

        def sf(eng, i=i, b=b, slm=slm, qv=qv):
            eng.wait_ge(mk.sems[slm.name], qv[1])
            mk.w(eng, "Vm")
            mk.mark(eng.activation(lntmp[:, i % 2, :], mb[:, b, :], AF.Identity,
                                   bias=negb[:, i:i + 1], scale=rstd[:, i:i + 1]), "Sm")
        mk.s(sf)
        slm.consumed("Sm")

        def vf(eng, i=i, b=b, slx=slx, sls=sls, qv=qv):
            eng.wait_ge(mk.sems[slx.name], qv[0])
            mk.w(eng, "Sm")
            if sls.guard is not None:
                eng.wait_ge(mk.sems[sls.name], sls.guard[1])
            mk.mark(eng.tensor_tensor(ob[:, b, :], xb[:, b, :],
                                      lntmp[:, i % 2, :], OP.add), "Vm")
        mk.v(vf)
        slx.consumed("Vm")
        vv = mk.cnt["Vm"]

        def yg(eng, i=i, b=b, sls=sls, vv=vv):
            eng.wait_ge(mk.sems["Vm"], vv)
            mk.mark(eng.dma_start(out=x_out[i * 128:(i + 1) * 128, :],
                                  in_=ob[:, b, :]), sls.name, 16)
        mk.g(yg)
        sls.guard = (sls.name, mk.cnt[sls.name])



def phase_l1ffn(mk, x_in, x_out, W, mixbuf):
    LS = mk.lslots
    SS = mk.sslots
    with ExitStack() as stk:
        wrb = mk.sb(stk, "d_wrb", [128, NC, D], BF16)
        wkb = mk.sb(stk, "d_wkb", [128, NC, D], BF16)
        wvb = mk.sb(stk, "d_wvb", [128, NC, D], BF16)
        mur = mk.sb(stk, "d_mur", [128, NC], F32)
        muk = mk.sb(stk, "d_muk", [128, NC], F32)
        murm = mk.sb(stk, "d_murm", [128, NC], F32)
        mukm = mk.sb(stk, "d_mukm", [128, NC], F32)
        idf = mk.sb(stk, "d_idf", [128, 128], F32)
        wstage = mk.sb(stk, "d_wstg", [128, 2, D], F32)

        lnsum = mk.sb(stk, "d_lnsum", [128, NT], F32)
        lnsq = mk.sb(stk, "d_lnsq", [128, NT], F32)
        rstd = mk.sb(stk, "d_rstd", [128, NT], F32)
        negb = mk.sb(stk, "d_negb", [128, NT], F32)
        mxsum = mk.sb(stk, "d_mxsum", [128, NT], F32)
        mxsq = mk.sb(stk, "d_mxsq", [128, NT], F32)

        def yf(eng):
            mk.mark(eng.dma_start(out=mur[:, :], in_=W["mur_s"][:, :]), "CN", 16)
            mk.mark(eng.dma_start(out=muk[:, :], in_=W["muk_s"][:, :]), "CN", 16)
            mk.mark(eng.dma_start(out=idf[:, :], in_=W["ident"][:, :]), "CN", 16)
        mk.y(yf)
        cn0 = mk.cnt["CN"]

        def vf0(eng):
            eng.wait_ge(mk.sems["CN"], cn0)
            eng.tensor_scalar(murm[:], mur[:], -1.0, 1.0, OP.mult, OP.add)
            mk.mark(eng.tensor_scalar(mukm[:], muk[:], -1.0, 1.0, OP.mult, OP.add), "Vm")
        mk.v(vf0)
        emit_load_cast(mk, wstage, "d_wr", W["wr_t"], wrb, D, D, LS[4:6])
        emit_load_cast(mk, wstage, "d_wk", W["wk_t"], wkb, D, D, LS[4:6])
        emit_load_cast(mk, wstage, "d_wv", W["wv_t"], wvb, D, D, LS[4:6])

        emit_ln_stats(mk, stk, "d_ln", x_in, lnsum, lnsq, LS[0:4])
        emit_stats_finalize(mk, stk, "d_lnf", lnsum, lnsq, D, EPS_LN, rstd, negb)

        mix_store_marks = {}
        with ExitStack() as stk2:
            xin2 = mk.sb(stk2, "d_xin2", [128, 4, D], F32)
            ztmp = mk.sb(stk2, "d_ztmp", [128, 2, D], F32)
            zcm = mk.sb(stk2, "d_zcm", [128, NC, 513], BF16)
            xrk = mk.sb(stk2, "d_xrk", [128, NC, 512], BF16)
            rcm = mk.sb(stk2, "d_rcm", [128, NC, 512], F32)
            kact = mk.sb(stk2, "d_kact", [128, NC, 512], BF16)
            ktmp = mk.sb(stk2, "d_ktmp", [128, 512], F32)
            junk2 = mk.sb(stk2, "d_junk2", [128, D], F32)
            mixcm = mk.sb(stk2, "d_mixcm", [128, NC, 512], F32)
            mixtok = mk.sb(stk2, "d_mixtok", [128, 2, D], F32)
            ptr = mk.psum(stk2, "d_ptr", [128, 2, 512], F32)
            prk = mk.psum(stk2, "d_prk", [128, 4, 512], F32)
            pvo = mk.psum(stk2, "d_pvo", [128, 2, 512], F32)

            def vz(eng):
                mk.mark(eng.memset(zcm[:, :, 0:1], 0.0), "Vm")
            mk.v(vz)

            for g in range(8):
                tiles = list(range(4 * g, 4 * g + 4))
                qvs = {}
                for ii, i in enumerate(tiles):
                    sl = LS[6 + ii]

                    def yf(eng, i=i, ii=ii, sl=sl):
                        qvs[ii] = sl.dma(eng, xin2[:, ii, :], x_in[i * 128:(i + 1) * 128, :])
                    mk.y(yf)

                if g > 0:
                    def vb(eng):
                        mk.dr(eng)
                        mk.mark(eng.tensor_copy(zcm[:, :, 0:1], zcm[:, :, 512:513]), "Vm")
                    mk.v(vb)

                for ii, i in enumerate(tiles):
                    sl = LS[6 + ii]

                    def sf(eng, i=i, ii=ii, sl=sl):
                        eng.wait_ge(mk.sems[sl.name], qvs[ii])
                        mk.w(eng, "Vm")
                        mk.w(eng, "Tm")
                        mk.mark(eng.activation(ztmp[:, ii % 2, :], xin2[:, ii, :],
                                               AF.Identity, bias=negb[:, i:i + 1],
                                               scale=rstd[:, i:i + 1]), "Sm")
                    mk.s(sf)
                    sl.consumed("Sm")

                    def tf(eng, i=i, ii=ii):
                        mk.w(eng, "Sm")
                        mk.w(eng, "Vm")
                        last = None
                        for j in range(NC):
                            last = eng.transpose(
                                ptr[:, j % 2, (j // 2) * 128:(j // 2) * 128 + 128],
                                ztmp[:, ii % 2, j * 128:(j + 1) * 128], idf[:])
                        mk.mark(last, "Tm")
                    mk.t(tf)

                    def vf(eng, i=i, ii=ii):
                        mk.w(eng, "Tm")
                        mk.dr(eng)
                        c0 = 1 + ii * 128
                        last = None
                        for j in range(NC):
                            last = eng.tensor_copy(
                                zcm[:, j, c0:c0 + 128],
                                ptr[:, j % 2, (j // 2) * 128:(j // 2) * 128 + 128])
                        mk.mark(last, "Vm")
                    mk.v(vf)

                def vxr(eng):
                    mk.w(eng, "Tm")
                    last = None
                    for j in range(NC):
                        mk.dr(eng)
                        eng.tensor_scalar(ktmp[:], zcm[:, j, 1:513], mur[:, j:j + 1],
                                          None, OP.mult)
                        mk.dr(eng)
                        last = eng.scalar_tensor_tensor(
                            xrk[:, j, :], zcm[:, j, 0:512], murm[:, j:j + 1],
                            ktmp[:], OP.mult, OP.add)
                    mk.mark(last, "Vm")
                mk.v(vxr)
                xrdone = mk.cnt["Vm"]

                for i in range(NC):
                    def tf(eng, i=i, xrdone=xrdone):
                        if i == 0:
                            mk.w(eng, "Vm", xrdone)
                            mk.w(eng, "Sm")
                        if i >= 2:
                            mk.w(eng, "Sm")
                        last = None
                        for j in range(NC):
                            last = eng.matmul(prk[:, i % 2, :],
                                              wrb[:, j, i * 128:(i + 1) * 128],
                                              xrk[:, j, :],
                                              start=(j == 0), stop=(j == NC - 1))
                        mk.mark(last, "Tm")
                    mk.t(tf)

                    def sf(eng, i=i):
                        mk.w(eng, "Tm")
                        mk.mark(eng.activation(rcm[:, i, :], prk[:, i % 2, :],
                                               AF.Sigmoid), "Sm")
                    mk.s(sf)
                rdone_s = mk.cnt["Sm"]

                def vxk(eng):
                    mk.w(eng, "Tm")
                    last = None
                    for j in range(NC):
                        mk.dr(eng)
                        eng.tensor_scalar(ktmp[:], zcm[:, j, 1:513], muk[:, j:j + 1],
                                          None, OP.mult)
                        mk.dr(eng)
                        last = eng.scalar_tensor_tensor(
                            xrk[:, j, :], zcm[:, j, 0:512], mukm[:, j:j + 1],
                            ktmp[:], OP.mult, OP.add)
                    mk.mark(last, "Vm")
                mk.v(vxk)

                for i in range(NC):
                    def tf(eng, i=i):
                        mk.w(eng, "Vm")
                        last = None
                        for j in range(NC):
                            last = eng.matmul(prk[:, 2 + i % 2, :],
                                              wkb[:, j, i * 128:(i + 1) * 128],
                                              xrk[:, j, :],
                                              start=(j == 0), stop=(j == NC - 1))
                        mk.mark(last, "Tm")
                    mk.t(tf)

                    def vf(eng, i=i):
                        mk.w(eng, "Tm")
                        mk.dr(eng)
                        eng.tensor_scalar(ktmp[:], prk[:, 2 + i % 2, :], 0.0, None, OP.max)
                        mk.dr(eng)
                        mk.mark(eng.tensor_tensor(kact[:, i, :], ktmp[:], ktmp[:],
                                                  OP.mult), "Vm")
                    mk.v(vf)

                for i in range(NC):
                    def tf(eng, i=i):
                        mk.w(eng, "Vm")
                        last = None
                        for j in range(NC):
                            last = eng.matmul(pvo[:, i % 2, :],
                                              wvb[:, j, i * 128:(i + 1) * 128],
                                              kact[:, j, :],
                                              start=(j == 0), stop=(j == NC - 1))
                        mk.mark(last, "Tm")
                    mk.t(tf)

                    def vf(eng, i=i, rdone_s=rdone_s):
                        mk.w(eng, "Tm")
                        eng.wait_ge(mk.sems["Sm"], rdone_s)
                        mk.mark(eng.tensor_tensor(mixcm[:, i, :], rcm[:, i, :],
                                                  pvo[:, i % 2, :], OP.mult), "Vm")
                    mk.v(vf)
                mixdone = mk.cnt["Vm"]

                for ii, i in enumerate(tiles):
                    sls = SS[i % 2]

                    def tf(eng, i=i, ii=ii, mixdone=mixdone):
                        if ii == 0:
                            mk.w(eng, "Vm", mixdone)
                        else:
                            mk.w(eng, "Vm")
                        last = None
                        for j in range(NC):
                            last = eng.transpose(
                                ptr[:, j % 2, (j // 2) * 128:(j // 2) * 128 + 128],
                                mixcm[:, j, ii * 128:(ii + 1) * 128], idf[:])
                        mk.mark(last, "Tm")
                    mk.t(tf)

                    def vf(eng, i=i, ii=ii, sls=sls):
                        mk.w(eng, "Tm")
                        if sls.guard is not None:
                            eng.wait_ge(mk.sems[sls.name], sls.guard[1])
                        if i >= 2:
                            mk.w(eng, "Sm")
                        for j in range(NC):
                            eng.tensor_copy(
                                mixtok[:, ii % 2, j * 128:(j + 1) * 128],
                                ptr[:, j % 2, (j // 2) * 128:(j // 2) * 128 + 128])
                        mk.dr(eng)
                        mk.mark(eng.reduce_sum(mxsum[:, i:i + 1], mixtok[:, ii % 2, :],
                                               axis=AX.X), "Vm")
                    mk.v(vf)
                    vv = mk.cnt["Vm"]

                    def sq(eng, i=i, ii=ii, vv=vv):
                        eng.wait_ge(mk.sems["Vm"], vv)
                        mk.dr(eng)
                        mk.mark(eng.activation(junk2[:], mixtok[:, ii % 2, :], AF.Square,
                                               accum_out=mxsq[:, i:i + 1]), "Sm")
                    mk.s(sq)

                    def yg(eng, i=i, ii=ii, sls=sls, vv=vv):
                        eng.wait_ge(mk.sems["Vm"], vv)
                        mk.mark(eng.dma_start(out=mixbuf[i * 128:(i + 1) * 128, :],
                                              in_=mixtok[:, ii % 2, :]), sls.name, 16)
                    mk.g(yg)
                    sls.guard = (sls.name, mk.cnt[sls.name])
                    mix_store_marks[i] = (sls.name, mk.cnt[sls.name])

        mk.barrier()
        with ExitStack() as stk3:
            emit_residual_update(mk, stk3, "d_up", x_in, mixbuf, x_out,
                                 mxsum, mxsq, LS[0:6], SS[0:3], mix_store_marks)


def prep_l1ffn(W_r, W_k, W_v, mu_r, mu_k):
    return {
        "wr_t": np.ascontiguousarray(W_r.T),
        "wk_t": np.ascontiguousarray(W_k.T),
        "wv_t": np.ascontiguousarray(W_v.T),
        "mur_s": np.ascontiguousarray(mu_r.reshape(NC, 128).T),
        "muk_s": np.ascontiguousarray(mu_k.reshape(NC, 128).T),
    }




def phase_l1attn(mk, x_in, x_out, W, mixbuf, qk_hbm, attn_hbm):
    LS = mk.lslots
    SS = mk.sslots
    H = 16
    with ExitStack() as stk:
        idf = mk.sb(stk, "c_idf", [128, 128], F32)
        ones = mk.sb(stk, "c_ones", [128, 128], F32)
        lnsum = mk.sb(stk, "c_lnsum", [128, NT], F32)
        lnsq = mk.sb(stk, "c_lnsq", [128, NT], F32)
        rstd = mk.sb(stk, "c_rstd", [128, NT], F32)
        negb = mk.sb(stk, "c_negb", [128, NT], F32)
        mxsum = mk.sb(stk, "c_mxsum", [128, NT], F32)
        mxsq = mk.sb(stk, "c_mxsq", [128, NT], F32)
        vres = mk.sb(stk, "c_vres", [128, NT, H, 65], BF16)

        def yf(eng):
            mk.mark(eng.dma_start(out=idf[:, :], in_=W["ident"][:, :]), "CN", 16)
            mk.mark(eng.dma_start(out=ones[:, :], in_=W["ones"][:, :]), "CN", 16)
        mk.y(yf)
        cn0 = mk.cnt["CN"]

        def vf0(eng):
            eng.wait_ge(mk.sems["CN"], cn0)
            mk.mark(eng.memset(vres[:, :, :, 64:65], 1.0), "Vm")
        mk.v(vf0)

        with ExitStack() as stk1:
            emit_ln_stats(mk, stk1, "c_ln", x_in, lnsum, lnsq, LS[0:4])
            emit_stats_finalize(mk, stk1, "c_lnf", lnsum, lnsq, D, EPS_LN, rstd, negb)
            wqkb = mk.sb(stk1, "c_wqkb", [128, NC, 2048], BF16)
            wvb = mk.sb(stk1, "c_wvb", [128, NC, D], BF16)
            wstage = mk.sb(stk1, "c_wstg", [128, 2, 2048], F32)
            emit_load_cast(mk, wstage, "c_wqk", W["wqk_t"], wqkb, D, 2048, LS[4:6])
            emit_load_cast(mk, wstage, "c_wv", W["wv_t"], wvb, D, D, LS[4:6])

            xin1 = mk.sb(stk1, "c_xin1", [128, 4, D], F32)
            ztmp = mk.sb(stk1, "c_ztmp", [128, 2, D], F32)
            zchunk = mk.sb(stk1, "c_zch", [128, NC, 512], BF16)
            qkstage = mk.sb(stk1, "c_qkst", [128, 2, 512], BF16)
            ptr = mk.psum(stk1, "c_ptr", [128, 8, 128], F32)
            pqk = mk.psum(stk1, "c_pqk", [128, 2, 512], F32)
            pv = mk.psum(stk1, "c_pv", [128, 2, 512], F32)

            for g in range(8):
                tiles = list(range(4 * g, 4 * g + 4))
                qvs = {}
                for ii, i in enumerate(tiles):
                    sl = LS[6 + ii]

                    def yf(eng, i=i, ii=ii, sl=sl):
                        qvs[ii] = sl.dma(eng, xin1[:, ii, :],
                                         x_in[i * 128:(i + 1) * 128, :])
                    mk.y(yf)

                for ii, i in enumerate(tiles):
                    sl = LS[6 + ii]

                    def sf(eng, i=i, ii=ii, sl=sl):
                        eng.wait_ge(mk.sems[sl.name], qvs[ii])
                        mk.w(eng, "Vm")
                        mk.w(eng, "Tm")
                        mk.mark(eng.activation(ztmp[:, ii % 2, :], xin1[:, ii, :],
                                               AF.Identity, bias=negb[:, i:i + 1],
                                               scale=rstd[:, i:i + 1]), "Sm")
                    mk.s(sf)
                    sl.consumed("Sm")

                    def tf(eng, i=i, ii=ii):
                        mk.w(eng, "Sm")
                        mk.w(eng, "Vm")
                        last = None
                        for j in range(NC):
                            last = eng.transpose(ptr[:, j, :],
                                                 ztmp[:, ii % 2, j * 128:(j + 1) * 128],
                                                 idf[:])
                        mk.mark(last, "Tm")
                    mk.t(tf)

                    def vf(eng, i=i, ii=ii):
                        mk.w(eng, "Tm")
                        mk.dr(eng)
                        mk.mark(eng.tensor_copy(
                            zchunk[:, :, ii * 128:(ii + 1) * 128], ptr[:, :, :]), "Vm")
                    mk.v(vf)
                zdone = mk.cnt["Vm"]

                for i in range(16):
                    sls = SS[i % 2]

                    def tf(eng, i=i, zdone=zdone):
                        if i == 0:
                            mk.w(eng, "Vm", zdone)
                            mk.w(eng, "Sm")
                        if i >= 2:
                            mk.w(eng, "Sm")
                        last = None
                        for j in range(NC):
                            last = eng.matmul(pqk[:, i % 2, :],
                                              wqkb[:, j, i * 128:(i + 1) * 128],
                                              zchunk[:, j, :],
                                              start=(j == 0), stop=(j == NC - 1))
                        mk.mark(last, "Tm")
                    mk.t(tf)

                    def sf(eng, i=i, sls=sls):
                        mk.w(eng, "Tm")
                        if sls.guard is not None:
                            eng.wait_ge(mk.sems[sls.name], sls.guard[1])
                            sls.guard = None
                        mk.dr(eng)
                        mk.mark(eng.copy(qkstage[:, i % 2, :], pqk[:, i % 2, :]), "Sm")
                    mk.s(sf)
                    sv = mk.cnt["Sm"]

                    def yg(eng, i=i, g=g, sls=sls, sv=sv):
                        eng.wait_ge(mk.sems["Sm"], sv)
                        mk.mark(eng.dma_start(
                            out=qk_hbm[i * 128:(i + 1) * 128, g * 512:(g + 1) * 512],
                            in_=qkstage[:, i % 2, :]), sls.name, 16)
                    mk.g(yg)
                    sls.guard = (sls.name, mk.cnt[sls.name])

                for ii, i in enumerate(tiles):
                    for half in range(2):
                        def tf(eng, ii=ii, half=half, zdone=zdone):
                            mk.w(eng, "Vm")
                            last = None
                            for j in range(NC):
                                last = eng.matmul(
                                    pv[:, half, :],
                                    zchunk[:, j, ii * 128:(ii + 1) * 128],
                                    wvb[:, j, half * 512:(half + 1) * 512],
                                    start=(j == 0), stop=(j == NC - 1))
                            mk.mark(last, "Tm")
                        mk.t(tf)

                        def vf(eng, i=i, half=half):
                            mk.w(eng, "Tm")
                            mk.dr(eng)
                            mk.mark(eng.tensor_copy(
                                vres[:, i, half * 8:(half + 1) * 8, 0:64],
                                pv[:, half, :].rearrange("p (h d) -> p h d", d=64)),
                                "Vm")
                        mk.v(vf)

        mk.barrier()

        with ExitStack() as stk2:
            atile = mk.sb(stk2, "c_atile", [64, 2, 512], BF16)
            qh = mk.sb(stk2, "c_qh", [64, T], BF16)
            kh = mk.sb(stk2, "c_kh", [64, T], BF16)
            pbuf = mk.sb(stk2, "c_pbuf", [128, 2, 2, 512], BF16)
            qm = mk.sb(stk2, "c_qm", [64, 2], F32)
            mstat = mk.sb(stk2, "c_mst", [128, 4], F32)
            recb = mk.sb(stk2, "c_recb", [64, 512], F32)
            rec1 = mk.sb(stk2, "c_rec1", [1, 512], F32)
            psA = mk.psum(stk2, "c_psA", [128, 2, 512], F32)
            psB = mk.psum(stk2, "c_psB", [128, 2, 512], F32)
            poh = mk.psum(stk2, "c_poh", [128, 512], F32)
            pdot = mk.psum(stk2, "c_pdot", [128, 512], F32)

            for h in range(H):
                slq, slk = LS[8], LS[9]
                qv = [None, None]

                def yf(eng, h=h, qv=qv):
                    qv[0] = slq.dma(eng, qh[:, :], qk_hbm[h * 64:(h + 1) * 64, :])
                    qv[1] = slk.dma(eng, kh[:, :],
                                    qk_hbm[1024 + h * 64:1024 + (h + 1) * 64, :])
                mk.y(yf)

                def vm(eng, h=h, qv=qv):
                    eng.wait_ge(mk.sems[slq.name], qv[0])
                    eng.wait_ge(mk.sems[slk.name], qv[1])
                    mk.dr(eng)
                    eng.reduce_max(qm[:, 0:1], qh[:, :], axis=AX.X,
                                   apply_absolute_value=True)
                    eng.reduce_max(qm[:, 1:2], kh[:, :], axis=AX.X,
                                   apply_absolute_value=True)
                    mk.mark(eng.engine_nop(), "Vm")
                mk.v(vm)
                vmk = mk.cnt["Vm"]

                def tdot(eng, h=h, vmk=vmk):
                    mk.w(eng, "Vm", vmk)
                    mk.mark(eng.matmul(pdot[0:1, 0:1], qm[0:64, 0:1], qm[0:64, 1:2],
                                       start=True, stop=True), "Tm")
                mk.t(tdot)

                def vm2(eng, h=h):
                    mk.w(eng, "Tm")
                    mk.dr(eng)
                    eng.tensor_scalar(mstat[0:1, 0:1], pdot[0:1, 0:1], 0.125, 85.0,
                                      OP.mult, OP.min)
                    mk.dr(eng)
                    mk.mark(eng.tensor_scalar(mstat[0:1, 1:2], mstat[0:1, 0:1], -1.0,
                                              None, OP.mult), "Vm")
                mk.v(vm2)
                vmk2 = mk.cnt["Vm"]

                def tbc(eng, h=h, vmk2=vmk2):
                    mk.w(eng, "Vm", vmk2)
                    mk.mark(eng.matmul(pdot[:, 1:2], ones[0:1, :], mstat[0:1, 1:2],
                                       start=True, stop=True), "Tm")
                mk.t(tbc)

                def vm3(eng, h=h):
                    mk.w(eng, "Tm")
                    mk.dr(eng)
                    mk.mark(eng.tensor_copy(mstat[:, 2:3], pdot[:, 1:2]), "Vm")
                mk.v(vm3)
                negm_mark = mk.cnt["Vm"]

                for qc in range(8):
                    smarks_exp = {}
                    tmarks_sp = {}

                    def emit_sp(eng, grp, h=h, qc=qc):
                        ps = psA if grp % 2 == 0 else psB
                        if grp >= 2:
                            mk.w(eng, "Sm", smarks_exp[grp - 2])
                        last = None
                        for t2 in range(2):
                            kt = grp * 2 + t2
                            last = eng.matmul(
                                ps[:, t2, :],
                                kh[:, kt * 128:(kt + 1) * 128],
                                qh[:, qc * 512:(qc + 1) * 512],
                                start=True, stop=True)
                        tmarks_sp[grp] = mk.mark(last, "Tm")

                    def emit_exp(eng, grp, h=h, qc=qc, negm_mark=negm_mark):
                        ps = psA if grp % 2 == 0 else psB
                        pb = grp % 2
                        mk.w(eng, "Tm", tmarks_sp[grp])
                        eng.wait_ge(mk.sems["Vm"], negm_mark)
                        mk.dr(eng)
                        smarks_exp[grp] = mk.mark(eng.activation(
                            pbuf[:, pb, :, :].rearrange("p a b -> p (a b)"),
                            ps[:, :, :].rearrange("p a b -> p (a b)"),
                            AF.Exp, bias=mstat[:, 2:3], scale=0.125), "Sm")

                    def emit_pv(eng, grp, h=h, qc=qc):
                        pb = grp % 2
                        mk.w(eng, "Sm", smarks_exp[grp])
                        if grp == 0:
                            mk.w(eng, "Vm")
                        last = None
                        for t2 in range(2):
                            kt = grp * 2 + t2
                            last = eng.matmul(poh[0:65, :],
                                              vres[:, kt, h, :],
                                              pbuf[:, pb, t2, :],
                                              start=(grp == 0 and t2 == 0),
                                              stop=(grp == 15 and t2 == 1))
                        mk.mark(last, "Tm")

                    mk.t(lambda eng, f=emit_sp: f(eng, 0))
                    for grp in range(1, 16):
                        mk.t(lambda eng, f=emit_sp, g=grp: f(eng, g))
                        mk.s(lambda eng, f=emit_exp, g=grp - 1: f(eng, g))
                        mk.t(lambda eng, f=emit_pv, g=grp - 1: f(eng, g))
                    mk.s(lambda eng, f=emit_exp: f(eng, 15))
                    mk.t(lambda eng, f=emit_pv: f(eng, 15))
                    pvdone = mk.cnt["Tm"]

                    def vn(eng, h=h, qc=qc, pvdone=pvdone):
                        mk.w(eng, "Tm", pvdone)
                        mk.dr(eng)
                        eng.reciprocal(rec1[:, :], poh[64:65, :])
                        mk.mark(eng.engine_nop(), "Vm")
                    mk.v(vn)
                    vrec = mk.cnt["Vm"]

                    def tbc2(eng, h=h, qc=qc, vrec=vrec):
                        mk.w(eng, "Vm", vrec)
                        mk.mark(eng.matmul(pdot[0:64, :], ones[0:1, 0:64], rec1[:, :],
                                           start=True, stop=True), "Tm")
                    mk.t(tbc2)

                    m = h * 8 + qc
                    sla = SS[2 + m % 2]

                    def vn2(eng, h=h, qc=qc, sla=sla, m=m):
                        mk.w(eng, "Tm")
                        if sla.guard is not None:
                            eng.wait_ge(mk.sems[sla.name], sla.guard[1])
                            sla.guard = None
                        mk.dr(eng)
                        eng.tensor_copy(recb[:, :], pdot[0:64, :])
                        mk.dr(eng)
                        mk.mark(eng.tensor_tensor(
                            atile[:, m % 2, :], poh[0:64, :], recb[:, :],
                            OP.mult), "Vm")
                    mk.v(vn2)
                    vv2 = mk.cnt["Vm"]

                    def yga(eng, h=h, qc=qc, sla=sla, m=m, vv2=vv2):
                        eng.wait_ge(mk.sems["Vm"], vv2)
                        mk.mark(eng.dma_start(
                            out=attn_hbm[h * 64:(h + 1) * 64,
                                         qc * 512:(qc + 1) * 512],
                            in_=atile[:, m % 2, :]), sla.name, 16)
                    mk.g(yga)
                    sla.guard = (sla.name, mk.cnt[sla.name])
                slq.consumed("Tm")
                slk.consumed("Tm")

        mk.barrier()

        mix_store_marks = {}
        with ExitStack() as stk3:
            wob = mk.sb(stk3, "c_wob", [128, NC, D], BF16)
            wstage3 = mk.sb(stk3, "c_wstg3", [128, 2, D], F32)
            emit_load_cast(mk, wstage3, "c_wo", W["wo_t"], wob, D, D, LS[4:6])
            astage = mk.sb(stk3, "c_astg", [128, 2, NC, 128], BF16)
            mixtok = mk.sb(stk3, "c_mixtok", [128, 2, D], F32)
            junk3 = mk.sb(stk3, "c_junk3", [128, D], F32)
            pout = mk.psum(stk3, "c_pout", [128, 2, D], F32)
            wdone = mk.cnt["Vm"]
            attn_r = attn_hbm.rearrange("(j p) t -> p j t", p=128)

            for i in range(NT):
                sls = SS[i % 2]
                sla = LS[10 + i % 2]
                qva = [None]

                def ya(eng, i=i, sla=sla, qva=qva):
                    qva[0] = sla.dma(eng, astage[:, i % 2, :, :],
                                     attn_r[:, :, i * 128:(i + 1) * 128])
                mk.y(ya)

                def tf(eng, i=i, wdone=wdone, sla=sla, qva=qva):
                    if i == 0:
                        mk.w(eng, "Vm", wdone)
                    if i >= 2:
                        mk.w(eng, "Vm")
                    eng.wait_ge(mk.sems[sla.name], qva[0])
                    last = None
                    for j in range(NC):
                        for half in range(2):
                            last = eng.matmul(
                                pout[:, i % 2, half * 512:(half + 1) * 512],
                                astage[:, i % 2, j, :],
                                wob[:, j, half * 512:(half + 1) * 512],
                                start=(j == 0), stop=(j == NC - 1))
                    mk.mark(last, "Tm")
                mk.t(tf)
                sla.consumed("Tm")

                def vf(eng, i=i, sls=sls):
                    mk.w(eng, "Tm")
                    if sls.guard is not None:
                        eng.wait_ge(mk.sems[sls.name], sls.guard[1])
                        sls.guard = None
                    if i >= 2:
                        mk.w(eng, "Sm")
                    mk.dr(eng)
                    eng.tensor_copy(mixtok[:, i % 2, :], pout[:, i % 2, :])
                    mk.dr(eng)
                    mk.mark(eng.reduce_sum(mxsum[:, i:i + 1], mixtok[:, i % 2, :],
                                           axis=AX.X), "Vm")
                mk.v(vf)
                vv = mk.cnt["Vm"]

                def sq(eng, i=i, vv=vv):
                    eng.wait_ge(mk.sems["Vm"], vv)
                    mk.dr(eng)
                    mk.mark(eng.activation(junk3[:], mixtok[:, i % 2, :], AF.Square,
                                           accum_out=mxsq[:, i:i + 1]), "Sm")
                mk.s(sq)

                def yg(eng, i=i, sls=sls, vv=vv):
                    eng.wait_ge(mk.sems["Vm"], vv)
                    mk.mark(eng.dma_start(out=mixbuf[i * 128:(i + 1) * 128, :],
                                          in_=mixtok[:, i % 2, :]), sls.name, 16)
                mk.g(yg)
                sls.guard = (sls.name, mk.cnt[sls.name])
                mix_store_marks[i] = (sls.name, mk.cnt[sls.name])

        mk.barrier()
        with ExitStack() as stk4:
            emit_residual_update(mk, stk4, "c_up", x_in, mixbuf, x_out,
                                 mxsum, mxsq, LS[0:6], SS[0:3], mix_store_marks)


def prep_l1attn(w_qkv, w_out):
    return {
        "wqk_t": np.ascontiguousarray(w_qkv[0:2048].T),
        "wv_t": np.ascontiguousarray(w_qkv[2048:3072].T),
        "wo_t": np.ascontiguousarray(w_out.T),
    }




def phase_l0ffn(mk, x_in, x_out, W, mixbuf, mixed_hbm, nx_hbm):
    LS = mk.lslots
    SS = mk.sslots
    F = 4096
    NF = F // 128
    with ExitStack() as stk:
        idf = mk.sb(stk, "b_idf", [128, 128], F32)
        idb = mk.sb(stk, "b_idb", [128, 128], BF16)
        normw = mk.sb(stk, "b_normw", [128, D], F32)
        lnsum = mk.sb(stk, "b_lnsum", [128, NT], F32)
        lnsq = mk.sb(stk, "b_lnsq", [128, NT], F32)
        rstd = mk.sb(stk, "b_rstd", [128, NT], F32)
        negb = mk.sb(stk, "b_negb", [128, NT], F32)
        mxsum = mk.sb(stk, "b_mxsum", [128, NT], F32)
        mxsq = mk.sb(stk, "b_mxsq", [128, NT], F32)
        wsl = mk.sb(stk, "b_wsl", [128, NT, 2], F32)
        slots = mk.sb(stk, "b_slots", [128, NT, 2, 8], BF16)
        slotT = mk.sb(stk, "b_slotT", [8, 2, T], BF16)
        wslotT = mk.sb(stk, "b_wslotT", [8, 2, T], BF16)
        r2acc = mk.sb(stk, "b_r2acc", [128, 2, NT, 128], F32)
        expd = mk.sb(stk, "b_expd", [8, 128], BF16)

        def yf(eng):
            mk.mark(eng.dma_start(out=idf[:, :], in_=W["ident"][:, :]), "CN", 16)
            mk.mark(eng.dma_start(out=normw[:, :], in_=W["normw_b"][:, :]), "CN", 16)
        mk.y(yf)
        cn0 = mk.cnt["CN"]

        def vcn(eng):
            eng.wait_ge(mk.sems["CN"], cn0)
            eng.memset(r2acc[:, :, :, :], 0.0)
            mk.dr(eng)
            mk.mark(eng.tensor_copy(idb[:, :], idf[:, :]), "Vm")
        mk.v(vcn)

        with ExitStack() as stkln:
            emit_ln_stats(mk, stkln, "b_ln", x_in, lnsum, lnsq, LS[0:4])
            emit_stats_finalize(mk, stkln, "b_lnf", lnsum, lnsq, D, EPS_LN,
                                rstd, negb)

        nx_store_marks = {}
        with ExitStack() as stk1:
            gatet = mk.sb(stk1, "b_gatet", [128, NC, 8], BF16)
            a1s = mk.sb(stk1, "b_a1s", [128, NC, 128], BF16)
            a3s = mk.sb(stk1, "b_a3s", [128, NC, 128], BF16)
            a2s = mk.sb(stk1, "b_a2s", [128, NF, 128], BF16)
            b1s = mk.sb(stk1, "b_b1s", [128, 1, F], BF16)
            b3s = mk.sb(stk1, "b_b3s", [128, 1, F], BF16)
            with ExitStack() as stkw:
                wstage = mk.sb(stkw, "b_wstg", [128, 2, 4096], F32)
                expstg = mk.sb(stkw, "b_expstg", [8, 128], F32)
                emit_load_cast(mk, wstage, "b_ga", W["gate_t"], gatet, D, 8, LS[4:6])
                emit_load_cast(mk, wstage, "b_a1", W["a1_s"], a1s, D, 128, LS[4:6])
                emit_load_cast(mk, wstage, "b_a3", W["a3_s"], a3s, D, 128, LS[4:6])
                emit_load_cast(mk, wstage, "b_b1", W["b1_s"], b1s, 128, 4096, LS[4:6])
                emit_load_cast(mk, wstage, "b_b3", W["b3_s"], b3s, 128, 4096, LS[4:6])
                emit_load_cast(mk, wstage, "b_a2", W["a2_sr"], a2s, F, 128, LS[4:6])
                sle = LS[4]
                qve = [None]

                def ye(eng):
                    qve[0] = sle.dma(eng, expstg[:, :], W["expand"][:, :])
                mk.y(ye)

                def ve(eng):
                    eng.wait_ge(mk.sems[sle.name], qve[0])
                    mk.dr(eng)
                    mk.mark(eng.tensor_copy(expd[:, :], expstg[:, :]), "Vm")
                mk.v(ve)
                sle.consumed("Vm")

            with ExitStack() as stk1b:
                xin1 = mk.sb(stk1b, "b_xin1", [128, 4, D], F32)
                ztmp = mk.sb(stk1b, "b_ztmp", [128, 2, D], F32)
                nxtok = mk.sb(stk1b, "b_nxtok", [128, 2, D], BF16)
                cmtile = mk.sb(stk1b, "b_cmtile", [128, 2, NC, 128], BF16)
                junkb = mk.sb(stk1b, "b_junkb", [128, D], F32)
                sqt = mk.sb(stk1b, "b_sqt", [128, 4], F32)
                lgt = mk.sb(stk1b, "b_lgt", [128, 8], F32)
                lgt2 = mk.sb(stk1b, "b_lgt2", [128, 8], F32)
                gm12 = mk.sb(stk1b, "b_gm12", [128, NT], F32)
                gtmp = mk.sb(stk1b, "b_gtmp", [128, 8], F32)
                ptr = mk.psum(stk1b, "b_ptr", [128, NC, 128], BF16)
                plg = mk.psum(stk1b, "b_plg", [128, 8], F32)

                for i in range(NT):
                    sl = LS[6 + i % 4]
                    snx = SS[2 + i % 2]
                    qv = [None]

                    def yf(eng, i=i, sl=sl, qv=qv):
                        qv[0] = sl.dma(eng, xin1[:, i % 4, :],
                                       x_in[i * 128:(i + 1) * 128, :])
                    mk.y(yf)

                    def sf(eng, i=i, sl=sl, qv=qv):
                        eng.wait_ge(mk.sems[sl.name], qv[0])
                        mk.w(eng, "Vm")
                        mk.dr(eng)
                        eng.activation(ztmp[:, i % 2, :], xin1[:, i % 4, :],
                                       AF.Identity, bias=negb[:, i:i + 1],
                                       scale=rstd[:, i:i + 1])
                        mk.dr(eng)
                        mk.mark(eng.activation(junkb[:], ztmp[:, i % 2, :], AF.Square,
                                               accum_out=sqt[:, i % 4:i % 4 + 1]),
                                "Sm")
                    mk.s(sf)
                    sl.consumed("Sm")
                    sm1 = mk.cnt["Sm"]

                    def vr(eng, i=i, sm1=sm1):
                        eng.wait_ge(mk.sems["Sm"], sm1)
                        mk.dr(eng)
                        eng.tensor_scalar(gtmp[:, 0:1], sqt[:, i % 4:i % 4 + 1],
                                          1.0 / D, 1e-6, OP.mult, OP.add)
                        mk.dr(eng)
                        mk.mark(eng.reciprocal(gtmp[:, 1:2], gtmp[:, 0:1]), "Vm")
                    mk.v(vr)
                    vm1 = mk.cnt["Vm"]

                    def sr(eng, i=i, vm1=vm1):
                        eng.wait_ge(mk.sems["Vm"], vm1)
                        mk.dr(eng)
                        mk.mark(eng.activation(gtmp[:, 2:3], gtmp[:, 1:2], AF.Sqrt),
                                "Sm")
                    mk.s(sr)
                    sm2 = mk.cnt["Sm"]

                    def vnx(eng, i=i, sm2=sm2):
                        eng.wait_ge(mk.sems["Sm"], sm2)
                        mk.w(eng, "Tm")
                        mk.dr(eng)
                        mk.mark(eng.scalar_tensor_tensor(
                            nxtok[:, i % 2, :], ztmp[:, i % 2, :], gtmp[:, 2:3],
                            normw[:, :], OP.mult, OP.mult), "Vm")
                    mk.v(vnx)
                    vm2 = mk.cnt["Vm"]

                    def tt(eng, i=i, vm2=vm2):
                        eng.wait_ge(mk.sems["Vm"], vm2)
                        mk.w(eng, "Vm")
                        last = None
                        for j in range(NC):
                            last = eng.transpose(
                                ptr[:, j, :], nxtok[:, i % 2, j * 128:(j + 1) * 128],
                                idb[:])
                        mk.mark(last, "Tm")
                    mk.t(tt)
                    tm1 = mk.cnt["Tm"]

                    def vcp(eng, i=i, tm1=tm1, snx=snx):
                        eng.wait_ge(mk.sems["Tm"], tm1)
                        if snx.guard is not None:
                            eng.wait_ge(mk.sems[snx.name], snx.guard[1])
                            snx.guard = None
                        mk.dr(eng)
                        mk.mark(eng.tensor_copy(cmtile[:, i % 2, :, :], ptr[:, :, :]),
                                "Vm")
                    mk.v(vcp)
                    vm3 = mk.cnt["Vm"]

                    def ynx(eng, i=i, snx=snx, vm3=vm3):
                        eng.wait_ge(mk.sems["Vm"], vm3)
                        mk.mark(eng.dma_start(
                            out=nx_hbm.rearrange("(c p) t -> p c t", p=128)[
                                :, :, i * 128:(i + 1) * 128],
                            in_=cmtile[:, i % 2, :, :]), snx.name, 16)
                    mk.g(ynx)
                    snx.guard = (snx.name, mk.cnt[snx.name])
                    nx_store_marks[i] = (snx.name, mk.cnt[snx.name])

                    def tg(eng, i=i, vm3=vm3):
                        eng.wait_ge(mk.sems["Vm"], vm3)
                        last = None
                        for j in range(NC):
                            last = eng.matmul(plg[:, :], cmtile[:, i % 2, j, :],
                                              gatet[:, j, :],
                                              start=(j == 0), stop=(j == NC - 1))
                        mk.mark(last, "Tm")
                    mk.t(tg)
                    tm2 = mk.cnt["Tm"]

                    def vg(eng, i=i, tm2=tm2):
                        eng.wait_ge(mk.sems["Tm"], tm2)
                        mk.dr(eng)
                        eng.tensor_copy(lgt[:, :], plg[:, :])
                        mk.dr(eng)
                        eng.reduce_max(gtmp[:, 3:4], lgt[:, :], axis=AX.X)
                        mk.dr(eng)
                        eng.tensor_scalar(slots[:, i, 0, :], lgt[:, :], gtmp[:, 3:4],
                                          None, OP.is_ge)
                        mk.dr(eng)
                        eng.scalar_tensor_tensor(lgt2[:, :], slots[:, i, 0, :],
                                                 -1e30, lgt[:, :], OP.mult, OP.add)
                        mk.dr(eng)
                        eng.reduce_max(gtmp[:, 4:5], lgt2[:, :], axis=AX.X)
                        mk.dr(eng)
                        eng.tensor_scalar(slots[:, i, 1, :], lgt2[:, :],
                                          gtmp[:, 4:5], None, OP.is_ge)
                        mk.dr(eng)
                        mk.mark(eng.tensor_tensor(gm12[:, i:i + 1], gtmp[:, 4:5],
                                                  gtmp[:, 3:4], OP.subtract), "Vm")
                    mk.v(vg)

                vme = mk.cnt["Vm"]

                def se(eng, vme=vme):
                    eng.wait_ge(mk.sems["Vm"], vme)
                    mk.dr(eng)
                    mk.mark(eng.activation(gm12[:, :], gm12[:, :], AF.Exp), "Sm")
                mk.s(se)
                sme = mk.cnt["Sm"]

                def vwf(eng, sme=sme):
                    eng.wait_ge(mk.sems["Sm"], sme)
                    mk.dr(eng)
                    eng.tensor_scalar(wsl[:, :, 0], gm12[:, :], 1.0, None, OP.add)
                    mk.dr(eng)
                    eng.reciprocal(wsl[:, :, 0], wsl[:, :, 0])
                    mk.dr(eng)
                    mk.mark(eng.tensor_tensor(wsl[:, :, 1], gm12[:, :],
                                              wsl[:, :, 0], OP.mult), "Vm")
                mk.v(vwf)

                wstok = mk.sb(stk1b, "b_wstok", [128, 2, 8], BF16)
                pslt = mk.psum(stk1b, "b_pslt", [8, 4, 128], BF16)
                for i in range(NT):
                    def vws(eng, i=i):
                        mk.w(eng, "Tm")
                        mk.dr(eng)
                        eng.tensor_scalar(wstok[:, 0, :], slots[:, i, 0, :],
                                          wsl[:, i, 0:1], None, OP.mult)
                        mk.dr(eng)
                        mk.mark(eng.tensor_scalar(wstok[:, 1, :], slots[:, i, 1, :],
                                                  wsl[:, i, 1:2], None, OP.mult),
                                "Vm")
                    mk.v(vws)
                    vmw = mk.cnt["Vm"]

                    def ts(eng, i=i, vmw=vmw):
                        eng.wait_ge(mk.sems["Vm"], vmw)
                        mk.w(eng, "Vm")
                        eng.transpose(pslt[0:8, 0, :], slots[:, i, 0, :], idb[:])
                        eng.transpose(pslt[0:8, 1, :], slots[:, i, 1, :], idb[:])
                        eng.transpose(pslt[0:8, 2, :], wstok[:, 0, :], idb[:])
                        mk.mark(eng.transpose(pslt[0:8, 3, :], wstok[:, 1, :],
                                              idb[:]), "Tm")
                    mk.t(ts)
                    tms = mk.cnt["Tm"]

                    def vse(eng, i=i, tms=tms):
                        eng.wait_ge(mk.sems["Tm"], tms)
                        mk.dr(eng)
                        isl = slice(i * 128, (i + 1) * 128)
                        eng.tensor_copy(slotT[:, 0, isl], pslt[0:8, 0, :])
                        eng.tensor_copy(slotT[:, 1, isl], pslt[0:8, 1, :])
                        eng.tensor_copy(wslotT[:, 0, isl], pslt[0:8, 2, :])
                        mk.mark(eng.tensor_copy(wslotT[:, 1, isl], pslt[0:8, 3, :]),
                                "Vm")
                    mk.v(vse)

            mk.barrier()
            msk1_0 = mk.sb(stk1, "b_msk10", [128, T], BF16)
            msk1_1 = mk.sb(stk1, "b_msk11", [128, T], BF16)
            msk3_0 = mk.sb(stk1, "b_msk30", [128, T], BF16)
            msk3_1 = mk.sb(stk1, "b_msk31", [128, T], BF16)
            with ExitStack() as stk2a:
                nxch = mk.sb(stk2a, "b_nxch", [128, 2, NC, 512], BF16)
                etr1c = mk.sb(stk2a, "b_etr1c", [128, 512], BF16)
                etr3c = mk.sb(stk2a, "b_etr3c", [128, 512], BF16)
                petr = mk.psum(stk2a, "b_petr", [128, 4, 512], F32)
                nxr = nx_hbm.rearrange("(c p) t -> p c t", p=128)

                for c in range(8):
                    csl = slice(c * 512, (c + 1) * 512)
                    sln = LS[6 + c % 2]
                    qvn = [None]

                    def yn(eng, c=c, sln=sln, qvn=qvn):
                        for i4 in range(4 * c, 4 * c + 4):
                            ms = nx_store_marks[i4]
                            eng.wait_ge(mk.sems[ms[0]], ms[1])
                        qvn[0] = sln.dma(eng, nxch[:, c % 2, :, :],
                                         nxr[:, :, c * 512:(c + 1) * 512])
                    mk.y(yn)

                    def te(eng, c=c, csl=csl, sln=sln, qvn=qvn):
                        eng.wait_ge(mk.sems[sln.name], qvn[0])
                        mk.w(eng, "Vm")
                        last = None
                        for j in range(NC):
                            eng.matmul(petr[:, 0, :], a1s[:, j, :],
                                       nxch[:, c % 2, j, :],
                                       start=(j == 0), stop=(j == NC - 1))
                        for j in range(NC):
                            eng.matmul(petr[:, 1, :], a3s[:, j, :],
                                       nxch[:, c % 2, j, :],
                                       start=(j == 0), stop=(j == NC - 1))
                        eng.matmul(petr[:, 2, :], expd[:, :], slotT[:, 0, csl],
                                   start=True, stop=True)
                        last = eng.matmul(petr[:, 3, :], expd[:, :],
                                          slotT[:, 1, csl], start=True, stop=True)
                        mk.mark(last, "Tm")
                    mk.t(te)
                    sln.consumed("Tm")
                    tme = mk.cnt["Tm"]

                    def vee(eng, c=c, csl=csl, tme=tme):
                        eng.wait_ge(mk.sems["Tm"], tme)
                        mk.dr(eng)
                        eng.tensor_copy(etr1c[:, :], petr[:, 0, :])
                        eng.tensor_copy(etr3c[:, :], petr[:, 1, :])
                        mk.dr(eng)
                        eng.tensor_tensor(msk1_0[:, csl], etr1c[:, :], petr[:, 2, :],
                                          OP.mult)
                        mk.mark(eng.tensor_tensor(msk1_1[:, csl], etr1c[:, :],
                                                  petr[:, 3, :], OP.mult), "Vm")
                    mk.v(vee)
                    vmee = mk.cnt["Vm"]

                    def te2(eng, c=c, csl=csl, vmee=vmee):
                        eng.wait_ge(mk.sems["Vm"], vmee)
                        eng.matmul(petr[:, 2, :], expd[:, :], wslotT[:, 0, csl],
                                   start=True, stop=True)
                        mk.mark(eng.matmul(petr[:, 3, :], expd[:, :],
                                           wslotT[:, 1, csl], start=True,
                                           stop=True), "Tm")
                    mk.t(te2)
                    tme2 = mk.cnt["Tm"]

                    def vmm(eng, c=c, csl=csl, tme2=tme2):
                        eng.wait_ge(mk.sems["Tm"], tme2)
                        mk.dr(eng)
                        eng.tensor_tensor(msk3_0[:, csl], etr3c[:, :], petr[:, 2, :],
                                          OP.mult)
                        mk.mark(eng.tensor_tensor(msk3_1[:, csl], etr3c[:, :],
                                                  petr[:, 3, :], OP.mult), "Vm")
                    mk.v(vmm)

            mk.barrier()

            with ExitStack() as stk2:
                wf1 = mk.sb(stk2, "b_wf1", [128, 2, NC, 512], BF16)
                wf3 = mk.sb(stk2, "b_wf3", [128, 2, NC, 512], BF16)
                wstg2 = mk.sb(stk2, "b_wstg2", [128, NC, 512], F32)
                nxst = mk.sb(stk2, "b_nxst", [128, 2, NC, 128], BF16)
                eb1 = mk.sb(stk2, "b_eb1", [128, 512], BF16)
                eb3 = mk.sb(stk2, "b_eb3", [128, 512], BF16)
                h1b = mk.sb(stk2, "b_h1b", [128, 2, 512], BF16)
                silub = mk.sb(stk2, "b_silub", [128, 2, 512], BF16)
                h3wb = mk.sb(stk2, "b_h3wb", [128, 2, 512], BF16)
                wactb = mk.sb(stk2, "b_wactb", [128, 2, 512], BF16)
                wactcm = mk.sb(stk2, "b_wactcm", [128, 2, 4, 128], BF16)
                mixcm = mk.sb(stk2, "b_mixcm", [128, 2, 4, 128], BF16)
                pb1 = mk.psum(stk2, "b_pb1", [128, 512], F32)
                pb3 = mk.psum(stk2, "b_pb3", [128, 512], F32)
                pl1 = mk.psum(stk2, "b_pl1", [128, 2, 512], F32)
                pl3 = mk.psum(stk2, "b_pl3", [128, 2, 512], F32)
                ptw = mk.psum(stk2, "b_ptw", [128, 2, 4, 128], BF16)
                pr2 = mk.psum(stk2, "b_pr2", [128, 2, 128], F32)
                nxr = nx_hbm.rearrange("(c p) t -> p c t", p=128)

                prev = {"sev": 0, "vh3": 0, "ttr": 0, "vtr": 0, "tr2": 0, "vr2": 0}
                for fs in range(8):
                    wb = fs % 2
                    slw1, slw3 = LS[4], LS[5]
                    qw = [None, None]
                    w1r = W["w1_t"].rearrange("(c p) f -> p c f", p=128)
                    w3r = W["w3_t"].rearrange("(c p) f -> p c f", p=128)

                    def yw(eng, fs=fs, wb=wb, qw=qw, w1r=w1r):
                        qw[0] = slw1.dma(eng, wstg2[:, :, :],
                                         w1r[:, :, fs * 512:(fs + 1) * 512])
                    mk.y(yw)

                    def vw1(eng, fs=fs, wb=wb, qw=qw):
                        eng.wait_ge(mk.sems[slw1.name], qw[0])
                        mk.dr(eng)
                        mk.mark(eng.tensor_copy(wf1[:, wb, :, :], wstg2[:, :, :]),
                                "Vm")
                    mk.v(vw1)
                    slw1.consumed("Vm")
                    w1c = mk.cnt["Vm"]

                    def yw3(eng, fs=fs, wb=wb, qw=qw, w3r=w3r, w1c=w1c):
                        eng.wait_ge(mk.sems["Vm"], w1c)
                        qw[1] = slw3.dma(eng, wstg2[:, :, :],
                                         w3r[:, :, fs * 512:(fs + 1) * 512])
                    mk.y(yw3)

                    def vw3(eng, fs=fs, wb=wb, qw=qw):
                        eng.wait_ge(mk.sems[slw3.name], qw[1])
                        mk.dr(eng)
                        mk.mark(eng.tensor_copy(wf3[:, wb, :, :], wstg2[:, :, :]),
                                "Vm")
                    mk.v(vw3)
                    slw3.consumed("Vm")
                    wready = mk.cnt["Vm"]

                    for ti in range(NT):
                        inst = fs * NT + ti
                        tsl = slice(ti * 128, (ti + 1) * 128)
                        bsl = slice(fs * 512, (fs + 1) * 512)
                        sstore = SS[2 + inst % 2]
                        slnx = LS[8 + inst % 2]
                        qnx = [None]

                        def ynx2(eng, ti=ti, slnx=slnx, qnx=qnx):
                            qnx[0] = slnx.dma(eng, nxst[:, inst % 2, :, :],
                                              nxr[:, :, ti * 128:(ti + 1) * 128])
                        mk.y(ynx2)

                        def tmm(eng, fs=fs, ti=ti, wb=wb, wready=wready, tsl=tsl,
                                bsl=bsl, p=dict(prev), slnx=slnx, qnx=qnx,
                                inst=inst):
                            if ti == 0:
                                mk.w(eng, "Vm", wready)
                            eng.wait_ge(mk.sems[slnx.name], qnx[0])
                            mk.w(eng, "Sm", p["sev"])
                            mk.w(eng, "Vm", p["vh3"])
                            last = None
                            for j in range(NC):
                                eng.matmul(pb1[:, :], nxst[:, inst % 2, j, :],
                                           wf1[:, wb, j, :],
                                           start=(j == 0), stop=(j == NC - 1))
                            for j in range(NC):
                                eng.matmul(pb3[:, :], nxst[:, inst % 2, j, :],
                                           wf3[:, wb, j, :],
                                           start=(j == 0), stop=(j == NC - 1))
                            eng.matmul(pl1[:, 0, :], msk1_0[:, tsl], b1s[:, 0, bsl],
                                       start=True, stop=True)
                            eng.matmul(pl1[:, 1, :], msk1_1[:, tsl], b1s[:, 0, bsl],
                                       start=True, stop=True)
                            eng.matmul(pl3[:, 0, :], msk3_0[:, tsl], b3s[:, 0, bsl],
                                       start=True, stop=True)
                            last = eng.matmul(pl3[:, 1, :], msk3_1[:, tsl],
                                              b3s[:, 0, bsl], start=True, stop=True)
                            mk.mark(last, "Tm")
                        mk.t(tmm)
                        slnx.consumed("Tm")
                        tmmm = mk.cnt["Tm"]

                        def sev(eng, tmmm=tmmm, p=dict(prev)):
                            eng.wait_ge(mk.sems["Tm"], tmmm)
                            mk.w(eng, "Vm", p["vh3"])
                            mk.dr(eng)
                            eng.copy(eb1[:, :], pb1[:, :])
                            mk.dr(eng)
                            mk.mark(eng.copy(eb3[:, :], pb3[:, :]), "Sm")
                        mk.s(sev)
                        prev["sev"] = mk.cnt["Sm"]
                        smev = prev["sev"]

                        def vh(eng, smev=smev):
                            eng.wait_ge(mk.sems["Sm"], smev)
                            mk.dr(eng)
                            eng.tensor_tensor(h1b[:, 0, :], eb1[:, :], pl1[:, 0, :],
                                              OP.add)
                            mk.dr(eng)
                            mk.mark(eng.tensor_tensor(h1b[:, 1, :], eb1[:, :],
                                                      pl1[:, 1, :], OP.add), "Vm")
                        mk.v(vh)
                        vmh = mk.cnt["Vm"]

                        def ssl(eng, vmh=vmh):
                            eng.wait_ge(mk.sems["Vm"], vmh)
                            mk.dr(eng)
                            eng.activation(silub[:, 0, :], h1b[:, 0, :], AF.Silu)
                            mk.dr(eng)
                            mk.mark(eng.activation(silub[:, 1, :], h1b[:, 1, :],
                                                   AF.Silu), "Sm")
                        mk.s(ssl)
                        smsl = mk.cnt["Sm"]

                        def vh3(eng, ti=ti, smsl=smsl, p=dict(prev)):
                            eng.wait_ge(mk.sems["Sm"], smsl)
                            eng.wait_ge(mk.sems["Tm"], p["ttr"])
                            mk.dr(eng)
                            eng.scalar_tensor_tensor(h3wb[:, 0, :], eb3[:, :],
                                                     wsl[:, ti, 0:1], pl3[:, 0, :],
                                                     OP.mult, OP.add)
                            mk.dr(eng)
                            eng.tensor_tensor(wactb[:, 0, :], silub[:, 0, :],
                                              h3wb[:, 0, :], OP.mult)
                            mk.dr(eng)
                            eng.scalar_tensor_tensor(h3wb[:, 1, :], eb3[:, :],
                                                     wsl[:, ti, 1:2], pl3[:, 1, :],
                                                     OP.mult, OP.add)
                            mk.dr(eng)
                            mk.mark(eng.tensor_tensor(wactb[:, 1, :], silub[:, 1, :],
                                                      h3wb[:, 1, :], OP.mult), "Vm")
                        mk.v(vh3)
                        prev["vh3"] = mk.cnt["Vm"]
                        vmw2 = prev["vh3"]

                        def ttr(eng, vmw2=vmw2, p=dict(prev)):
                            eng.wait_ge(mk.sems["Vm"], vmw2)
                            mk.w(eng, "Vm", p["vtr"])
                            last = None
                            for sl2 in range(2):
                                for q in range(4):
                                    last = eng.transpose(
                                        ptw[:, sl2, q, :],
                                        wactb[:, sl2, q * 128:(q + 1) * 128], idb[:])
                            mk.mark(last, "Tm")
                        mk.t(ttr)
                        prev["ttr"] = mk.cnt["Tm"]
                        tmtr = prev["ttr"]

                        def vtr(eng, inst=inst, tmtr=tmtr, sstore=sstore,
                                p=dict(prev)):
                            eng.wait_ge(mk.sems["Tm"], tmtr)
                            eng.wait_ge(mk.sems["Tm"], p["tr2"])
                            if sstore.guard is not None:
                                eng.wait_ge(mk.sems[sstore.name], sstore.guard[1])
                                sstore.guard = None
                            mk.dr(eng)
                            eng.tensor_copy(wactcm[:, 0, :, :], ptw[:, 0, :, :])
                            eng.tensor_copy(wactcm[:, 1, :, :], ptw[:, 1, :, :])
                            mk.dr(eng)
                            mk.mark(eng.tensor_tensor(mixcm[:, inst % 2, :, :],
                                                      wactcm[:, 0, :, :],
                                                      wactcm[:, 1, :, :], OP.add),
                                    "Vm")
                        mk.v(vtr)
                        prev["vtr"] = mk.cnt["Vm"]
                        vmtr = prev["vtr"]

                        def ygm(eng, fs=fs, ti=ti, inst=inst, sstore=sstore,
                                vmtr=vmtr):
                            eng.wait_ge(mk.sems["Vm"], vmtr)
                            mk.mark(eng.dma_start(
                                out=mixed_hbm[ti, fs * 512:(fs + 1) * 512, :]
                                .rearrange("(a p) b -> p a b", p=128),
                                in_=mixcm[:, inst % 2, :, :]), sstore.name, 16)
                        mk.g(ygm)
                        sstore.guard = (sstore.name, mk.cnt[sstore.name])

                        def tr2(eng, fs=fs, vmtr=vmtr, p=dict(prev)):
                            eng.wait_ge(mk.sems["Vm"], vmtr)
                            eng.wait_ge(mk.sems["Vm"], p["vr2"])
                            last = None
                            for sl2 in range(2):
                                for q in range(4):
                                    last = eng.matmul(
                                        pr2[:, sl2, :], wactcm[:, sl2, q, :],
                                        a2s[:, fs * 4 + q, :],
                                        start=(q == 0), stop=(q == 3))
                            mk.mark(last, "Tm")
                        mk.t(tr2)
                        prev["tr2"] = mk.cnt["Tm"]
                        tmr2 = prev["tr2"]

                        def vr2(eng, ti=ti, tmr2=tmr2):
                            eng.wait_ge(mk.sems["Tm"], tmr2)
                            mk.dr(eng)
                            eng.tensor_tensor(r2acc[:, 0, ti, :], r2acc[:, 0, ti, :],
                                              pr2[:, 0, :], OP.add)
                            mk.dr(eng)
                            mk.mark(eng.tensor_tensor(r2acc[:, 1, ti, :],
                                                      r2acc[:, 1, ti, :],
                                                      pr2[:, 1, :], OP.add), "Vm")
                        mk.v(vr2)
                        prev["vr2"] = mk.cnt["Vm"]

            mk.barrier()

        mix_store_marks = {}
        with ExitStack() as stk3:
            w2b = mk.sb(stk3, "b_w2b", [128, NF, D], BF16)
            wstg4 = mk.sb(stk3, "b_wstg4", [128, 2, D], F32)
            emit_load_cast(mk, wstg4, "b_w2", W["w2_t"], w2b, F, D, LS[4:6])
            b2s = mk.sb(stk3, "b_b2s", [128, D], BF16)
            b2stg = mk.sb(stk3, "b_b2stg", [128, D], F32)
            slb = LS[6]
            qvb = [None]

            def yb2(eng):
                qvb[0] = slb.dma(eng, b2stg[:, :], W["b2_s"][:, :])
            mk.y(yb2)

            def vb2(eng):
                eng.wait_ge(mk.sems[slb.name], qvb[0])
                mk.dr(eng)
                mk.mark(eng.tensor_copy(b2s[:, :], b2stg[:, :]), "Vm")
            mk.v(vb2)
            slb.consumed("Vm")

            mstr = mk.sb(stk3, "b_mstr", [128, 2, NF, 128], BF16)
            mr2tok = mk.sb(stk3, "b_mr2tok", [128, 128], F32)
            mr2tmp = mk.sb(stk3, "b_mr2tmp", [128, 128], F32)
            mr2b = mk.sb(stk3, "b_mr2b", [128, 128], BF16)
            mr2cm = mk.sb(stk3, "b_mr2cm", [128, 128], BF16)
            mixtok = mk.sb(stk3, "b_mixtok", [128, 2, D], F32)
            junk4 = mk.sb(stk3, "b_junk4", [128, D], F32)
            pout = mk.psum(stk3, "b_pout", [128, 2, D], F32)
            pmsk = mk.psum(stk3, "b_pmsk", [128, 2, 128], F32)
            pmr2 = mk.psum(stk3, "b_pmr2", [128, 128], BF16)
            wdone2 = mk.cnt["Vm"]

            prevB = {"vmr": 0, "vout": 0, "sq": 0}
            for ti in range(NT):
                slm = LS[10 + ti % 2]
                sls = SS[ti % 2]
                qvm = [None]

                def ym(eng, ti=ti, slm=slm, qvm=qvm):
                    qvm[0] = slm.dma(eng, mstr[:, ti % 2, :, :],
                        mixed_hbm[ti, :, :].rearrange("(a p) b -> p a b", p=128))
                mk.y(ym)

                def tmsk(eng, ti=ti, wdone2=wdone2, p=dict(prevB)):
                    if ti == 0:
                        mk.w(eng, "Vm", wdone2)
                    mk.w(eng, "Vm", p["vmr"])
                    eng.matmul(pmsk[:, 0, :], slotT[:, 0, ti * 128:(ti + 1) * 128],
                               expd[:, :], start=True, stop=True)
                    mk.mark(eng.matmul(pmsk[:, 1, :],
                                       slotT[:, 1, ti * 128:(ti + 1) * 128],
                                       expd[:, :], start=True, stop=True), "Tm")
                mk.t(tmsk)
                tmk = mk.cnt["Tm"]

                def vmr(eng, ti=ti, tmk=tmk):
                    eng.wait_ge(mk.sems["Tm"], tmk)
                    mk.dr(eng)
                    eng.tensor_tensor(mr2tok[:, :], r2acc[:, 0, ti, :],
                                      pmsk[:, 0, :], OP.mult)
                    eng.tensor_tensor(mr2tmp[:, :], r2acc[:, 1, ti, :],
                                      pmsk[:, 1, :], OP.mult)
                    mk.dr(eng)
                    mk.mark(eng.tensor_tensor(mr2b[:, :], mr2tok[:, :],
                                              mr2tmp[:, :], OP.add), "Vm")
                mk.v(vmr)
                prevB["vmr"] = mk.cnt["Vm"]
                vmrm = prevB["vmr"]

                def tmr(eng, vmrm=vmrm):
                    eng.wait_ge(mk.sems["Vm"], vmrm)
                    mk.mark(eng.transpose(pmr2[:, :], mr2b[:, :], idb[:]), "Tm")
                mk.t(tmr)
                tmrm = mk.cnt["Tm"]

                def vmr2(eng, tmrm=tmrm):
                    eng.wait_ge(mk.sems["Tm"], tmrm)
                    mk.dr(eng)
                    mk.mark(eng.tensor_copy(mr2cm[:, :], pmr2[:, :]), "Vm")
                mk.v(vmr2)
                vmr2m = mk.cnt["Vm"]

                def tw2(eng, ti=ti, vmr2m=vmr2m, slm=slm, qvm=qvm, p=dict(prevB)):
                    eng.wait_ge(mk.sems["Vm"], vmr2m)
                    eng.wait_ge(mk.sems[slm.name], qvm[0])
                    mk.w(eng, "Vm", p["vout"])
                    last = None
                    for half in range(2):
                        hs = slice(half * 512, (half + 1) * 512)
                        for j in range(NF):
                            eng.matmul(pout[:, ti % 2, hs], mstr[:, ti % 2, j, :],
                                       w2b[:, j, hs], start=(j == 0), stop=False)
                        last = eng.matmul(pout[:, ti % 2, hs], mr2cm[:, :],
                                          b2s[:, hs], start=False, stop=True)
                    mk.mark(last, "Tm")
                mk.t(tw2)
                slm.consumed("Tm")
                tw2m = mk.cnt["Tm"]

                def vout(eng, ti=ti, tw2m=tw2m, sls=sls, p=dict(prevB)):
                    eng.wait_ge(mk.sems["Tm"], tw2m)
                    if sls.guard is not None:
                        eng.wait_ge(mk.sems[sls.name], sls.guard[1])
                        sls.guard = None
                    eng.wait_ge(mk.sems["Sm"], p["sq"])
                    mk.dr(eng)
                    eng.tensor_copy(mixtok[:, ti % 2, :], pout[:, ti % 2, :])
                    mk.dr(eng)
                    mk.mark(eng.reduce_sum(mxsum[:, ti:ti + 1], mixtok[:, ti % 2, :],
                                           axis=AX.X), "Vm")
                mk.v(vout)
                prevB["vout"] = mk.cnt["Vm"]
                voutm = prevB["vout"]

                def sq(eng, ti=ti, voutm=voutm):
                    eng.wait_ge(mk.sems["Vm"], voutm)
                    mk.dr(eng)
                    mk.mark(eng.activation(junk4[:], mixtok[:, ti % 2, :], AF.Square,
                                           accum_out=mxsq[:, ti:ti + 1]), "Sm")
                mk.s(sq)
                prevB["sq"] = mk.cnt["Sm"]

                def yg(eng, ti=ti, sls=sls, voutm=voutm):
                    eng.wait_ge(mk.sems["Vm"], voutm)
                    mk.mark(eng.dma_start(out=mixbuf[ti * 128:(ti + 1) * 128, :],
                                          in_=mixtok[:, ti % 2, :]), sls.name, 16)
                mk.g(yg)
                sls.guard = (sls.name, mk.cnt[sls.name])
                mix_store_marks[ti] = (sls.name, mk.cnt[sls.name])

        mk.barrier()
        with ExitStack() as stk4:
            emit_residual_update(mk, stk4, "b_up", x_in, mixbuf, x_out,
                                 mxsum, mxsq, LS[0:6], SS[0:3], mix_store_marks)


def prep_l0ffn(norm_w, w1, w3, w2, gate_w, a1, b1, a3, b3, a2, b2):
    E, R = 8, 16
    return {
        "normw_b": np.ascontiguousarray(np.broadcast_to(norm_w, (128, 1024))),
        "gate_t": np.ascontiguousarray(gate_w.T),
        "w1_t": np.ascontiguousarray(w1.T),
        "w3_t": np.ascontiguousarray(w3.T),
        "w2_t": np.ascontiguousarray(w2.T),
        "a1_s": np.ascontiguousarray(a1.reshape(E * R, 1024).T),
        "a3_s": np.ascontiguousarray(a3.reshape(E * R, 1024).T),
        "b1_s": np.ascontiguousarray(b1.transpose(0, 2, 1).reshape(E * R, 4096)),
        "b3_s": np.ascontiguousarray(b3.transpose(0, 2, 1).reshape(E * R, 4096)),
        "a2_sr": np.ascontiguousarray(a2.reshape(E * R, 4096).T),
        "b2_s": np.ascontiguousarray(b2.transpose(0, 2, 1).reshape(E * R, 1024)),
        "expand": np.ascontiguousarray(
            2.0 * (np.arange(128)[None, :] // R == np.arange(E)[:, None])
        ).astype(np.float32),
    }


def emit_outproj(mk, stk, tag, acts_hbm, w_t_hbm, mixbuf, mxsum, mxsq,
                 act_ready_marks, mix_store_marks):
    LS = mk.lslots
    SS = mk.sslots
    wob = mk.sb(stk, tag + "_wob", [128, NC, D], BF16)
    wstage = mk.sb(stk, tag + "_wstg", [128, 2, D], F32)
    emit_load_cast(mk, wstage, tag + "_wo", w_t_hbm, wob, D, D, LS[4:6])
    astage = mk.sb(stk, tag + "_astg", [128, 2, NC, 128], BF16)
    mixtok = mk.sb(stk, tag + "_mixtok", [128, 2, D], F32)
    junk = mk.sb(stk, tag + "_junk", [128, D], F32)
    pout = mk.psum(stk, tag + "_pout", [128, 2, D], F32)
    wdone = mk.cnt["Vm"]
    acts_r = acts_hbm.rearrange("(j p) t -> p j t", p=128)
    prevO = {"vout": 0, "sq": 0}

    for i in range(NT):
        sls = SS[i % 2]
        sla = LS[10 + i % 2]
        qva = [None]

        def ya(eng, i=i, sla=sla, qva=qva):
            for m in act_ready_marks.get(i, []):
                eng.wait_ge(mk.sems[m[0]], m[1])
            qva[0] = sla.dma(eng, astage[:, i % 2, :, :],
                             acts_r[:, :, i * 128:(i + 1) * 128])
        mk.y(ya)

        def tf(eng, i=i, wdone=wdone, sla=sla, qva=qva, p=dict(prevO)):
            if i == 0:
                mk.w(eng, "Vm", wdone)
            mk.w(eng, "Vm", p["vout"])
            eng.wait_ge(mk.sems[sla.name], qva[0])
            last = None
            for j in range(NC):
                for half in range(2):
                    last = eng.matmul(
                        pout[:, i % 2, half * 512:(half + 1) * 512],
                        astage[:, i % 2, j, :],
                        wob[:, j, half * 512:(half + 1) * 512],
                        start=(j == 0), stop=(j == NC - 1))
            mk.mark(last, "Tm")
        mk.t(tf)
        sla.consumed("Tm")
        tfm = mk.cnt["Tm"]

        def vf(eng, i=i, sls=sls, tfm=tfm, p=dict(prevO)):
            eng.wait_ge(mk.sems["Tm"], tfm)
            if sls.guard is not None:
                eng.wait_ge(mk.sems[sls.name], sls.guard[1])
                sls.guard = None
            eng.wait_ge(mk.sems["Sm"], p["sq"])
            mk.dr(eng)
            eng.tensor_copy(mixtok[:, i % 2, :], pout[:, i % 2, :])
            mk.dr(eng)
            mk.mark(eng.reduce_sum(mxsum[:, i:i + 1], mixtok[:, i % 2, :],
                                   axis=AX.X), "Vm")
        mk.v(vf)
        prevO["vout"] = mk.cnt["Vm"]
        vv = prevO["vout"]

        def sq(eng, i=i, vv=vv):
            eng.wait_ge(mk.sems["Vm"], vv)
            mk.dr(eng)
            mk.mark(eng.activation(junk[:], mixtok[:, i % 2, :], AF.Square,
                                   accum_out=mxsq[:, i:i + 1]), "Sm")
        mk.s(sq)
        prevO["sq"] = mk.cnt["Sm"]

        def yg(eng, i=i, sls=sls, vv=vv):
            eng.wait_ge(mk.sems["Vm"], vv)
            mk.mark(eng.dma_start(out=mixbuf[i * 128:(i + 1) * 128, :],
                                  in_=mixtok[:, i % 2, :]), sls.name, 16)
        mk.g(yg)
        sls.guard = (sls.name, mk.cnt[sls.name])
        mix_store_marks[i] = (sls.name, mk.cnt[sls.name])


def phase_l0attn(mk, x_in, x_out, W, mixbuf, oattn_hbm, c0=0, c1=32,
                 s_in=None, s_out=None, oattn_in=None):
    LS = mk.lslots
    SS = mk.sslots
    H, HD = 16, 64
    NCH = 32
    with ExitStack() as stk:
        idf = mk.sb(stk, "a_idf", [128, 128], F32)
        idb = mk.sb(stk, "a_idb", [128, 128], BF16)
        umneg = mk.sb(stk, "a_umneg", [128, 128], F32)
        udiag = mk.sb(stk, "a_udiag", [128, 128], F32)
        lnsum = mk.sb(stk, "a_lnsum", [128, NT], F32)
        lnsq = mk.sb(stk, "a_lnsq", [128, NT], F32)
        rstd = mk.sb(stk, "a_rstd", [128, NT], F32)
        negb = mk.sb(stk, "a_negb", [128, NT], F32)
        mxsum = mk.sb(stk, "a_mxsum", [128, NT], F32)
        mxsq = mk.sb(stk, "a_mxsq", [128, NT], F32)
        rkcm = mk.sb(stk, "a_rkcm", [128, NC, T + 128], BF16)
        vtok = mk.sb(stk, "a_vtok", [128, NT, D], BF16)
        gb = mk.sb(stk, "a_gb", [128, NT, 32], F32)

        def yc(eng):
            mk.mark(eng.dma_start(out=idf[:, :], in_=W["ident"][:, :]), "CN", 16)
            mk.mark(eng.dma_start(out=umneg[:, :], in_=W["umaskneg"][:, :]), "CN", 16)
            mk.mark(eng.dma_start(out=udiag[:, :], in_=W["udiag"][:, :]), "CN", 16)
        mk.y(yc)
        cn0 = mk.cnt["CN"]

        def vcn(eng):
            eng.wait_ge(mk.sems["CN"], cn0)
            eng.memset(rkcm[:, :, 0:128], 0.0)
            mk.dr(eng)
            mk.mark(eng.tensor_copy(idb[:, :], idf[:, :]), "Vm")
        mk.v(vcn)

        with ExitStack() as stkln:
            emit_ln_stats(mk, stkln, "a_ln", x_in, lnsum, lnsq, LS[0:4])
            emit_stats_finalize(mk, stkln, "a_lnf", lnsum, lnsq, D, EPS_LN,
                                rstd, negb)

        with ExitStack() as stk1:
            wvb = mk.sb(stk1, "a_wvb", [128, NC, D], BF16)
            wgbb = mk.sb(stk1, "a_wgbb", [128, NC, 32], BF16)
            wstage = mk.sb(stk1, "a_wstg", [128, 2, D], F32)
            for _sl in (LS[4], LS[5]):
                _sl.guard = ("Sm", mk.cnt["Sm"])
                _sl.extra_guard = ("Vm", mk.cnt["Vm"])
            emit_load_cast(mk, wstage, "a_wv", W["wv_t"], wvb, D, D, LS[4:6])
            emit_load_cast(mk, wstage, "a_wgb", W["wgb_t"], wgbb, D, 32, LS[4:6])

            xin1 = mk.sb(stk1, "a_xin1", [128, 4, D], F32)
            ztmp = mk.sb(stk1, "a_ztmp", [128, 2, D], F32)
            zcb = mk.sb(stk1, "a_zcb", [128, 2, NC, 128], BF16)
            rktok = mk.sb(stk1, "a_rktok", [128, 2, D], BF16)
            rss = mk.sb(stk1, "a_rss", [128, 16], F32)
            rsc = mk.sb(stk1, "a_rsc", [128, 16], F32)
            junka = mk.sb(stk1, "a_junka", [128, D], F32)
            ptrz = mk.psum(stk1, "a_ptrz", [128, NC, 128], F32)
            ptrr = mk.psum(stk1, "a_ptrr", [128, NC, 128], BF16)
            psv = mk.psum(stk1, "a_psv", [128, 2, 512], F32)
            pg = mk.psum(stk1, "a_pg", [128, 32], F32)
            pA = {"vz": 0, "vv": 0, "vrk": 0, "vg": 0, "trz": 0, "trr": 0,
                  "tv": 0, "tg": 0, "srk": 0, "szq": 0}

            for i in range(NT):
                sl = LS[6 + i % 4]
                qv = [None]

                def yf(eng, i=i, sl=sl, qv=qv):
                    qv[0] = sl.dma(eng, xin1[:, i % 4, :],
                                   x_in[i * 128:(i + 1) * 128, :])
                mk.y(yf)

                def sf(eng, i=i, sl=sl, qv=qv, p=dict(pA)):
                    eng.wait_ge(mk.sems[sl.name], qv[0])
                    mk.w(eng, "Vm")
                    mk.w(eng, "Tm", p["trz"])
                    mk.w(eng, "Vm", p["vrk"])
                    mk.dr(eng)
                    eng.activation(ztmp[:, i % 2, :], xin1[:, i % 4, :],
                                   AF.Identity, bias=negb[:, i:i + 1],
                                   scale=rstd[:, i:i + 1])
                    mk.dr(eng)
                    last = None
                    for h in range(H):
                        last = eng.activation(junka[:, h * 64:(h + 1) * 64],
                                              ztmp[:, i % 2, h * 64:(h + 1) * 64],
                                              AF.Square,
                                              accum_out=rss[:, h:h + 1])
                    mk.mark(last, "Sm")
                mk.s(sf)
                sl.consumed("Sm")
                smz = mk.cnt["Sm"]

                def vsc(eng, i=i, smz=smz):
                    eng.wait_ge(mk.sems["Sm"], smz)
                    mk.dr(eng)
                    eng.tensor_scalar(rsc[:, :], rss[:, :], 1e-24, None, OP.max)
                    mk.dr(eng)
                    mk.mark(eng.reciprocal(rsc[:, :], rsc[:, :]), "Vm")
                mk.v(vsc)
                vscm = mk.cnt["Vm"]

                def ssc(eng, i=i, vscm=vscm):
                    eng.wait_ge(mk.sems["Vm"], vscm)
                    mk.dr(eng)
                    mk.mark(eng.activation(rsc[:, :], rsc[:, :], AF.Sqrt), "Sm")
                mk.s(ssc)
                sscm = mk.cnt["Sm"]

                def vrk(eng, i=i, sscm=sscm, p=dict(pA)):
                    eng.wait_ge(mk.sems["Sm"], sscm)
                    eng.wait_ge(mk.sems["Tm"], p["trr"])
                    mk.dr(eng)
                    last = None
                    for h in range(H):
                        last = eng.tensor_scalar(
                            rktok[:, i % 2, h * 64:(h + 1) * 64],
                            ztmp[:, i % 2, h * 64:(h + 1) * 64],
                            rsc[:, h:h + 1], None, OP.mult)
                    mk.mark(last, "Vm")
                mk.v(vrk)
                pA["vrk"] = mk.cnt["Vm"]
                vrkm = pA["vrk"]

                def trz(eng, i=i, smz=smz, p=dict(pA)):
                    eng.wait_ge(mk.sems["Sm"], smz)
                    mk.w(eng, "Vm", p["vz"])
                    last = None
                    for j in range(NC):
                        last = eng.transpose(ptrz[:, j, :],
                                             ztmp[:, i % 2, j * 128:(j + 1) * 128],
                                             idf[:])
                    mk.mark(last, "Tm")
                mk.t(trz)
                pA["trz"] = mk.cnt["Tm"]
                trzm = pA["trz"]

                def vz(eng, i=i, trzm=trzm, p=dict(pA)):
                    eng.wait_ge(mk.sems["Tm"], trzm)
                    eng.wait_ge(mk.sems["Tm"], p["tv"])
                    eng.wait_ge(mk.sems["Tm"], p["tg"])
                    mk.dr(eng)
                    mk.mark(eng.tensor_copy(zcb[:, i % 2, :, :], ptrz[:, :, :]),
                            "Vm")
                mk.v(vz)
                pA["vz"] = mk.cnt["Vm"]
                vzm = pA["vz"]

                def trr(eng, i=i, vrkm=vrkm, p=dict(pA)):
                    eng.wait_ge(mk.sems["Vm"], vrkm)
                    mk.w(eng, "Vm", p["vrkcm"] if "vrkcm" in p else 0)
                    last = None
                    for j in range(NC):
                        last = eng.transpose(ptrr[:, j, :],
                                             rktok[:, i % 2, j * 128:(j + 1) * 128],
                                             idb[:])
                    mk.mark(last, "Tm")
                mk.t(trr)
                pA["trr"] = mk.cnt["Tm"]
                trrm = pA["trr"]

                def vrkcm(eng, i=i, trrm=trrm):
                    eng.wait_ge(mk.sems["Tm"], trrm)
                    mk.dr(eng)
                    mk.mark(eng.tensor_copy(
                        rkcm[:, :, 128 + i * 128:128 + (i + 1) * 128],
                        ptrr[:, :, :]), "Vm")
                mk.v(vrkcm)
                pA["vrkcm"] = mk.cnt["Vm"]

                def tv(eng, i=i, vzm=vzm, p=dict(pA)):
                    eng.wait_ge(mk.sems["Vm"], vzm)
                    mk.w(eng, "Vm", p["vv"])
                    last = None
                    for half in range(2):
                        for j in range(NC):
                            last = eng.matmul(
                                psv[:, half, :], zcb[:, i % 2, j, :],
                                wvb[:, j, half * 512:(half + 1) * 512],
                                start=(j == 0), stop=(j == NC - 1))
                    mk.mark(last, "Tm")
                mk.t(tv)
                pA["tv"] = mk.cnt["Tm"]
                tvm = pA["tv"]

                def vv(eng, i=i, tvm=tvm):
                    eng.wait_ge(mk.sems["Tm"], tvm)
                    mk.dr(eng)
                    mk.mark(eng.tensor_copy(
                        vtok[:, i, :],
                        psv[:, :, :].rearrange("p a b -> p (a b)")), "Vm")
                mk.v(vv)
                pA["vv"] = mk.cnt["Vm"]

                def tg(eng, i=i, vzm=vzm, p=dict(pA)):
                    mk.w(eng, "Vm", p["vg"])
                    last = None
                    for j in range(NC):
                        last = eng.matmul(pg[:, :], zcb[:, i % 2, j, :],
                                          wgbb[:, j, :],
                                          start=(j == 0), stop=(j == NC - 1))
                    mk.mark(last, "Tm")
                mk.t(tg)
                pA["tg"] = mk.cnt["Tm"]
                tgm = pA["tg"]

                def vg(eng, i=i, tgm=tgm):
                    eng.wait_ge(mk.sems["Tm"], tgm)
                    mk.dr(eng)
                    mk.mark(eng.tensor_copy(gb[:, i, :], pg[:, :]), "Vm")
                mk.v(vg)
                pA["vg"] = mk.cnt["Vm"]

            allg = mk.cnt["Vm"]

            def sg(eng, allg=allg):
                eng.wait_ge(mk.sems["Vm"], allg)
                mk.dr(eng)
                mk.mark(eng.activation(
                    gb[:, :, :].rearrange("p a b -> p (a b)"),
                    gb[:, :, :].rearrange("p a b -> p (a b)"), AF.Sigmoid), "Sm")
            mk.s(sg)
            gates_ready = mk.cnt["Sm"]

        mk.barrier()

        ostore_marks = {}
        with ExitStack() as stk2:
            ecm = mk.sb(stk2, "a_ecm", [64, 2, 16, 128], BF16)
            fgs = mk.sb(stk2, "a_fgs", [128, 2, 16, 64], BF16)
            gcs = mk.sb(stk2, "a_gcs", [64, 2, 16, 64], BF16)
            pts = mk.sb(stk2, "a_pts", [64, 2, 16, 64], BF16)
            sfp = mk.sb(stk2, "a_sfp", [64, 16, 64], F32)
            sbb = mk.sb(stk2, "a_sbb", [64, 16, 64], BF16)
            ktk = mk.sb(stk2, "a_ktk", [128, 2, 64], BF16)
            krstg = mk.sb(stk2, "a_krstg", [64, 2, 2, 128], BF16)
            zbuf = mk.sb(stk2, "a_zbuf", [128, 2, 2, 128], BF16)
            lbuf = mk.sb(stk2, "a_lbuf", [128, 2, 128], BF16)
            mbuf = mk.sb(stk2, "a_mbuf", [128, 2, 128], BF16)
            uw = mk.sb(stk2, "a_uw", [128, 2, 128], BF16)
            tut = mk.sb(stk2, "a_tut", [128, 2, 128], BF16)
            otok = mk.sb(stk2, "a_otok", [128, 16, 64], BF16)
            ocms = mk.sb(stk2, "a_ocms", [128, 2, 8, 128], BF16)
            stmp = mk.sb(stk2, "a_stmp", [64, 16, 64], F32)
            psG = mk.psum(stk2, "a_psG", [128, 2, 128], F32)
            psZ = mk.psum(stk2, "a_psZ", [128, 2, 128], F32)
            psLM = mk.psum(stk2, "a_psLM", [128, 2, 2, 128], F32)
            ptb = mk.psum(stk2, "a_ptb", [128, 2, 2, 128], BF16)
            psO = mk.psum(stk2, "a_psO", [128, 8, 64], F32)
            psS = mk.psum(stk2, "a_psS", [64, 8, 64], F32)
            psOT = mk.psum(stk2, "a_psOT", [64, 8, 128], BF16)

            if s_in is None:
                def vinit(eng):
                    eng.memset(sfp[:, :, :], 0.0)
                    mk.dr(eng)
                    mk.mark(eng.memset(sbb[:, :, :], 0.0), "Vm")
                mk.v(vinit)
            else:
                sls_ = mk.lslots[0]
                qvs_ = [None]

                def yinit(eng):
                    qvs_[0] = sls_.dma(eng, sfp[:, :, :],
                                       s_in.rearrange("p (a b) -> p a b", b=64))
                mk.y(yinit)

                def vinit(eng):
                    eng.wait_ge(mk.sems[sls_.name], qvs_[0])
                    mk.dr(eng)
                    mk.mark(eng.tensor_copy(
                        sbb[:, :, :].rearrange("p a b -> p (a b)"),
                        sfp[:, :, :].rearrange("p a b -> p (a b)")), "Vm")
                mk.v(vinit)
                sls_.consumed("Vm")
            if oattn_in is not None:
                def ycp(eng):
                    mk.mark(eng.dma_start(out=oattn_hbm[:, 0:c0 * 128],
                                          in_=oattn_in[:, :]), "CN", 16)
                mk.y(ycp)
                ocp_mark = ("CN", mk.cnt["CN"])
            else:
                ocp_mark = None

            ph1m = [None] * NCH

            def hsl(h):
                return (slice((h % 2) * 64, (h % 2) * 64 + 64), h // 2)

            def emit_ph1(c):
                cb = c % 2
                kof = 127 + c * 128
                rof = 128 + c * 128
                for gidx in range(8):
                    heads = [2 * gidx, 2 * gidx + 1]

                    gpar = gidx % 2

                    def v0(eng, c=c, heads=heads, kof=kof, rof=rof, gpar=gpar):
                        mk.w(eng, "Tm")
                        mk.dr(eng)
                        h1 = heads[1]
                        ps_, ch = hsl(h1)
                        eng.tensor_copy(krstg[:, gpar, 0, :],
                                        rkcm[ps_, ch, kof:kof + 128])
                        mk.mark(eng.tensor_copy(krstg[:, gpar, 1, :],
                                                rkcm[ps_, ch, rof:rof + 128]),
                                "Vm")
                    mk.v(v0)
                    v0m = mk.cnt["Vm"]

                    def t1(eng, c=c, heads=heads, kof=kof, v0m=v0m, gpar=gpar):
                        mk.w(eng, "Vm")
                        last = None
                        h0, h1 = heads
                        ps0, ch0 = hsl(h0)
                        eng.matmul(psG[:, 0, :],
                                   rkcm[ps0, ch0, kof:kof + 128],
                                   rkcm[ps0, ch0, kof:kof + 128],
                                   start=True, stop=True)
                        eng.matmul(psG[:, 1, :], krstg[:, gpar, 0, :],
                                   krstg[:, gpar, 0, :], start=True, stop=True)
                        for ii, h in enumerate(heads):
                            ps_, ch = hsl(h)
                            iid = idb[0:64, 0:64] if h % 2 == 0 else idb[64:128, 64:128]
                            last = eng.transpose(ptb[:, 0, ii, 0:64],
                                                 rkcm[ps_, ch, kof:kof + 128],
                                                 iid)
                        mk.mark(last, "Tm")
                    mk.t(t1)
                    t1m = mk.cnt["Tm"]

                    def v1(eng, c=c, heads=heads, t1m=t1m):
                        eng.wait_ge(mk.sems["Tm"], t1m)
                        mk.dr(eng)
                        eng.tensor_copy(ktk[:, :, :], ptb[:, 0, :, 0:64])
                        mk.dr(eng)
                        for ii, h in enumerate(heads):
                            eng.tensor_tensor(lbuf[:, ii, :], psG[:, ii, :],
                                              umneg[:, :], OP.mult)
                        mk.dr(eng)
                        for ii, h in enumerate(heads):
                            eng.tensor_scalar(lbuf[:, ii, :], lbuf[:, ii, :],
                                              gb[:, c, 16 + h:17 + h], None,
                                              OP.mult)
                        mk.dr(eng)
                        last = None
                        for ii, h in enumerate(heads):
                            eng.tensor_copy(zbuf[:, 0, ii, 0:64],
                                            vtok[:, c, h * 64:(h + 1) * 64])
                            last = eng.tensor_copy(zbuf[:, 0, ii, 64:128],
                                                   ktk[:, ii, :])
                        mk.mark(last, "Vm")
                    mk.v(v1)
                    v1m = mk.cnt["Vm"]

                    def t2(eng, v1m=v1m):
                        eng.wait_ge(mk.sems["Vm"], v1m)
                        last = None
                        for ii in range(2):
                            last = eng.transpose(ptb[:, 1, ii, :], lbuf[:, ii, :],
                                                 idb[:])
                        mk.mark(last, "Tm")
                    mk.t(t2)
                    t2m = mk.cnt["Tm"]

                    def v2(eng, t2m=t2m):
                        eng.wait_ge(mk.sems["Tm"], t2m)
                        mk.dr(eng)
                        mk.mark(eng.tensor_copy(mbuf[:, :, :], ptb[:, 1, :, :]),
                                "Vm")
                    mk.v(v2)
                    lastv = mk.cnt["Vm"]

                    zcur = 0
                    for lev in range(7):
                        def tz(eng, lev=lev, zcur=zcur, lastv=lastv):
                            eng.wait_ge(mk.sems["Vm"], lastv)
                            last = None
                            for ii in range(2):
                                last = eng.matmul(psZ[:, ii, :], lbuf[:, ii, :],
                                                  zbuf[:, zcur, ii, :],
                                                  start=True, stop=True)
                            if lev < 6:
                                for ii in range(2):
                                    last = eng.matmul(psLM[:, 0, ii, :],
                                                      mbuf[:, ii, :],
                                                      lbuf[:, ii, :],
                                                      start=True, stop=True)
                                if lev < 5:
                                    for ii in range(2):
                                        last = eng.matmul(psLM[:, 1, ii, :],
                                                          lbuf[:, ii, :],
                                                          mbuf[:, ii, :],
                                                          start=True, stop=True)
                            mk.mark(last, "Tm")
                        mk.t(tz)
                        tzm = mk.cnt["Tm"]

                        def vz2(eng, lev=lev, zcur=zcur, tzm=tzm):
                            eng.wait_ge(mk.sems["Tm"], tzm)
                            mk.dr(eng)
                            ztt = eng.tensor_tensor(
                                zbuf[:, 1 - zcur, :, :].rearrange("p a b -> p (a b)"),
                                zbuf[:, zcur, :, :].rearrange("p a b -> p (a b)"),
                                psZ[:, :, :].rearrange("p a b -> p (a b)"), OP.add)
                            last = ztt
                            if lev < 6:
                                mk.dr(eng)
                                last = eng.tensor_copy(
                                    lbuf[:, :, :].rearrange("p a b -> p (a b)"),
                                    psLM[:, 0, :, :].rearrange("p a b -> p (a b)"))
                                if lev < 5:
                                    mk.dr(eng)
                                    last = eng.tensor_copy(
                                        mbuf[:, :, :].rearrange("p a b -> p (a b)"),
                                        psLM[:, 1, :, :].rearrange("p a b -> p (a b)"))
                            mk.mark(last, "Vm")
                        mk.v(vz2)
                        lastv = mk.cnt["Vm"]
                        zcur = 1 - zcur

                    def t3(eng, heads=heads, kof=kof, rof=rof, lastv=lastv,
                           gpar=gpar):
                        eng.wait_ge(mk.sems["Vm"], lastv)
                        last = None
                        h0, h1 = heads
                        ps0, ch0 = hsl(h0)
                        eng.matmul(psG[:, 0, :],
                                   rkcm[ps0, ch0, kof:kof + 128],
                                   rkcm[ps0, ch0, rof:rof + 128],
                                   start=True, stop=True)
                        last = eng.matmul(psG[:, 1, :], krstg[:, gpar, 0, :],
                                          krstg[:, gpar, 1, :],
                                          start=True, stop=True)
                        mk.mark(last, "Tm")
                    mk.t(t3)
                    t3m = mk.cnt["Tm"]

                    def v3(eng, c=c, heads=heads, t3m=t3m, zcur=zcur):
                        eng.wait_ge(mk.sems["Tm"], t3m)
                        mk.dr(eng)
                        for ii, h in enumerate(heads):
                            eng.tensor_scalar(uw[:, ii, :], zbuf[:, zcur, ii, :],
                                              gb[:, c, 16 + h:17 + h], None,
                                              OP.mult)
                        mk.dr(eng)
                        last = None
                        for ii in range(2):
                            last = eng.tensor_tensor(tut[:, ii, :], psG[:, ii, :],
                                                     udiag[:, :], OP.mult)
                        mk.mark(last, "Vm")
                    mk.v(v3)
                    v3m = mk.cnt["Vm"]

                    def t4(eng, v3m=v3m):
                        eng.wait_ge(mk.sems["Vm"], v3m)
                        last = None
                        for ii in range(2):
                            eng.matmul(psG[:, ii, 0:64], tut[:, ii, :],
                                       uw[:, ii, 0:64], start=True, stop=True)
                            eng.matmul(psZ[0:64, ii, :], uw[:, ii, 64:128],
                                       tut[:, ii, :], start=True, stop=True)
                            eng.matmul(psLM[0:64, 0, ii, 0:64], ktk[:, ii, :],
                                       uw[:, ii, 0:64], start=True, stop=True)
                            last = eng.matmul(psLM[0:64, 0, ii, 64:128],
                                              uw[:, ii, 64:128], ktk[:, ii, :],
                                              start=True, stop=True)
                        mk.mark(last, "Tm")
                    mk.t(t4)
                    t4m = mk.cnt["Tm"]

                    def v4(eng, c=c, cb=cb, heads=heads, t4m=t4m):
                        eng.wait_ge(mk.sems["Tm"], t4m)
                        mk.dr(eng)
                        last = None
                        for ii, h in enumerate(heads):
                            ps_, ch = hsl(h)
                            eng.tensor_tensor(
                                ecm[:, cb, h, :],
                                rkcm[ps_, ch, 128 + c * 128:128 + (c + 1) * 128],
                                psZ[0:64, ii, :], OP.subtract)
                            eng.tensor_scalar(fgs[:, cb, h, :], psG[:, ii, 0:64],
                                              gb[:, c, h:h + 1], None, OP.mult)
                            eng.tensor_copy(gcs[:, cb, h, :],
                                            psLM[0:64, 0, ii, 0:64])
                            last = eng.tensor_copy(pts[:, cb, h, :],
                                                   psLM[0:64, 0, ii, 64:128])
                        mk.mark(last, "Vm")
                    mk.v(v4)
                ph1m[c] = mk.cnt["Vm"]

            def emit_ph2(c):
                cb = c % 2
                so = SS[2 + c % 2]
                for hb in range(2):
                    def tes(eng, c=c, cb=cb, hb=hb):
                        mk.w(eng, "Vm")
                        last = None
                        for hh in range(8):
                            h = hb * 8 + hh
                            last = eng.matmul(psO[:, hh, :],
                                              ecm[:, cb, h, :],
                                              sbb[:, h, :], start=True, stop=True)
                        for hh in range(8):
                            h = hb * 8 + hh
                            last = eng.matmul(psS[:, hh, :], pts[:, cb, h, :],
                                              sbb[:, h, :], start=True, stop=True)
                        mk.mark(last, "Tm")
                    mk.t(tes)
                    tesm = mk.cnt["Tm"]

                    def vo(eng, c=c, cb=cb, hb=hb, tesm=tesm):
                        eng.wait_ge(mk.sems["Tm"], tesm)
                        mk.dr(eng)
                        hs = slice(hb * 8, hb * 8 + 8)
                        for hh in range(8):
                            h = hb * 8 + hh
                            eng.scalar_tensor_tensor(
                                otok[:, h, :], psO[:, hh, :], gb[:, c, h:h + 1],
                                fgs[:, cb, h, :], OP.mult, OP.add)
                        mk.dr(eng)
                        eng.tensor_tensor(
                            stmp[:, hs, :].rearrange("p a b -> p (a b)"),
                            gcs[:, cb, hs, :].rearrange("p a b -> p (a b)"),
                            psS[:, :, :].rearrange("p a b -> p (a b)"),
                            OP.subtract)
                        mk.dr(eng)
                        mk.mark(eng.tensor_tensor(
                            sfp[:, hs, :].rearrange("p a b -> p (a b)"),
                            sfp[:, hs, :].rearrange("p a b -> p (a b)"),
                            stmp[:, hs, :].rearrange("p a b -> p (a b)"),
                            OP.add), "Vm")
                    mk.v(vo)
                vsplit = mk.cnt["Vm"]

                def vsb(eng, vsplit=vsplit):
                    eng.wait_ge(mk.sems["Vm"], vsplit)
                    mk.dr(eng)
                    mk.mark(eng.tensor_copy(
                        sbb[:, :, :].rearrange("p a b -> p (a b)"),
                        sfp[:, :, :].rearrange("p a b -> p (a b)")), "Vm")
                mk.v(vsb)
                vom = mk.cnt["Vm"]

                for hb in range(2):
                    def tot(eng, hb=hb, vom=vom):
                        if hb == 0:
                            eng.wait_ge(mk.sems["Vm"], vom)
                        mk.w(eng, "Vm")
                        last = None
                        for hh in range(8):
                            h = hb * 8 + hh
                            last = eng.transpose(psOT[:, hh, :], otok[:, h, :],
                                                 idb[:])
                        mk.mark(last, "Tm")
                    mk.t(tot)
                    totm = mk.cnt["Tm"]

                    def voc(eng, c=c, cb=cb, hb=hb, totm=totm, so=so):
                        eng.wait_ge(mk.sems["Tm"], totm)
                        if so.guard is not None:
                            eng.wait_ge(mk.sems[so.name], so.guard[1])
                            so.guard = None
                        mk.dr(eng)
                        last = None
                        for hh in range(8):
                            h = hb * 8 + hh
                            last = eng.tensor_copy(
                                ocms[(h % 2) * 64:(h % 2) * 64 + 64, cb, h // 2, :],
                                psOT[:, hh, :])
                        mk.mark(last, "Vm")
                    mk.v(voc)
                vocm = mk.cnt["Vm"]

                def yo(eng, c=c, cb=cb, so=so, vocm=vocm):
                    eng.wait_ge(mk.sems["Vm"], vocm)
                    mk.mark(eng.dma_start(
                        out=oattn_hbm.rearrange("(j p) t -> p j t", p=128)[
                            :, :, c * 128:(c + 1) * 128],
                        in_=ocms[:, cb, :, :]), so.name, 16)
                mk.g(yo)
                so.guard = (so.name, mk.cnt[so.name])
                ostore_marks[c] = (so.name, mk.cnt[so.name])

            def vwait(eng, gates_ready=gates_ready):
                eng.wait_ge(mk.sems["Sm"], gates_ready)
                mk.mark(eng.engine_nop(), "Vm")
            mk.v(vwait)

            emit_ph1(c0)
            for c in range(c0 + 1, c1):
                emit_ph1(c)
                emit_ph2(c - 1)
            emit_ph2(c1 - 1)

            if s_out is not None:
                svm = mk.cnt["Vm"]
                sst = SS[0]

                def ys(eng, svm=svm, sst=sst):
                    eng.wait_ge(mk.sems["Vm"], svm)
                    mk.mark(eng.dma_start(
                        out=s_out.rearrange("p (a b) -> p a b", b=64),
                        in_=sfp[:, :, :]), sst.name, 16)
                    mk.w(eng, sst.name)
                mk.g(ys)

        mk.barrier()
        if c1 < NCH:
            return

        mix_store_marks = {}
        act_marks = {i: [ostore_marks[i]] for i in ostore_marks}
        if ocp_mark is not None:
            for i in range(c0):
                act_marks[i] = [ocp_mark]
        with ExitStack() as stk3:
            emit_outproj(mk, stk3, "a_op", oattn_hbm, W["wo_t"], mixbuf,
                         mxsum, mxsq, act_marks, mix_store_marks)
        mk.barrier()
        with ExitStack() as stk4:
            emit_residual_update(mk, stk4, "a_up", x_in, mixbuf, x_out,
                                 mxsum, mxsq, LS[0:6], SS[0:3], mix_store_marks)


def prep_l0attn(Wv, Wo, Wg, Wb):
    iu = np.triu(np.ones((128, 128), np.float32), 1)
    ud = np.triu(np.ones((128, 128), np.float32), 0)
    return {
        "wv_t": np.ascontiguousarray(Wv.T),
        "wo_t": np.ascontiguousarray(Wo.T),
        "wgb_t": np.ascontiguousarray(np.concatenate([Wg, Wb], 0).T),
        "umaskneg": -iu,
        "udiag": ud,
    }


def phase_l0attn_f(mk, x_in, x_out, W, mixbuf, oattn_hbm, rk_hbm, v_hbm,
                   g_hbm):
    nc = mk.nc
    LS = mk.lslots
    SS = mk.sslots
    H, NCH, RKW = 16, 32, T + 128
    mk.new_sems(["FKa", "FKb", "FRa", "FRb", "FVa", "FVb", "FGa", "FGb",
                 "FOa", "FOb", "GS0", "GS1"])
    STEP_T = ["AT1", "AT2"] + ["AU%d" % i for i in range(7)] + ["AT3", "AT4"]
    STEP_V = ["AV1", "AV2"] + ["AZ%d" % i for i in range(7)] + ["AV3", "AV4"]
    PH2_S = ["ATE", "ATO", "AVO", "AVC", "AVS"]
    mk.new_sems(STEP_T + STEP_V + PH2_S)
    with ExitStack() as stk:
        idf = mk.sb(stk, "a_idf", [128, 128], F32)
        idb = mk.sb(stk, "a_idb", [128, 128], BF16)
        umneg = mk.sb(stk, "a_umneg", [128, 128], F32)
        udiag = mk.sb(stk, "a_udiag", [128, 128], F32)
        lnsum = mk.sb(stk, "a_lnsum", [128, NT], F32)
        lnsq = mk.sb(stk, "a_lnsq", [128, NT], F32)
        rstd = mk.sb(stk, "a_rstd", [128, NT], F32)
        negb = mk.sb(stk, "a_negb", [128, NT], F32)
        mxsum = mk.sb(stk, "a_mxsum", [128, NT], F32)
        mxsq = mk.sb(stk, "a_mxsq", [128, NT], F32)

        def yc(eng):
            mk.mark(eng.dma_start(out=idf[:, :], in_=W["ident"][:, :]), "CN", 16)
            mk.mark(eng.dma_start(out=umneg[:, :], in_=W["umaskneg"][:, :]),
                    "CN", 16)
            mk.mark(eng.dma_start(out=udiag[:, :], in_=W["udiag"][:, :]),
                    "CN", 16)
        mk.y(yc)
        cn0 = mk.cnt["CN"]

        def vcn(eng):
            eng.wait_ge(mk.sems["CN"], cn0)
            mk.dr(eng)
            mk.mark(eng.tensor_copy(idb[:, :], idf[:, :]), "Vm")
        mk.v(vcn)

        with ExitStack() as stkln:
            emit_ln_stats(mk, stkln, "a_ln", x_in, lnsum, lnsq, LS[0:4])
            emit_stats_finalize(mk, stkln, "a_lnf", lnsum, lnsq, D, EPS_LN,
                                rstd, negb)

        with ExitStack() as stk1:
            wvb = mk.sb(stk1, "a_wvb", [128, NC, D], BF16)
            wgbb = mk.sb(stk1, "a_wgbb", [128, NC, 32], BF16)
            wstage = mk.sb(stk1, "a_wstg", [128, 2, D], F32)
            for _sl in (LS[4], LS[5]):
                _sl.guard = ("Sm", mk.cnt["Sm"])
                _sl.extra_guard = ("Vm", mk.cnt["Vm"])
            emit_load_cast(mk, wstage, "a_wv", W["wv_t"], wvb, D, D, LS[4:6])
            emit_load_cast(mk, wstage, "a_wgb", W["wgb_t"], wgbb, D, 32,
                           LS[4:6])

            xin1 = mk.sb(stk1, "a_xin1", [128, 4, D], F32)
            ztmp = mk.sb(stk1, "a_ztmp", [128, 2, D], F32)
            zcb = mk.sb(stk1, "a_zcb", [128, 2, NC, 128], BF16)
            rktok = mk.sb(stk1, "a_rktok", [128, 2, D], BF16)
            rkcmt = mk.sb(stk1, "a_rkcmt", [128, 2, NC, 128], BF16)
            vstg = mk.sb(stk1, "a_vstg", [128, 2, D], BF16)
            gstg = mk.sb(stk1, "a_gstg", [128, 2, 32], F32)
            rss = mk.sb(stk1, "a_rss", [128, 16], F32)
            rsc = mk.sb(stk1, "a_rsc", [128, 16], F32)
            junka = mk.sb(stk1, "a_junka", [128, D], F32)
            ptrz = mk.psum(stk1, "a_ptrz", [128, NC, 128], F32)
            ptrr = mk.psum(stk1, "a_ptrr", [128, NC, 128], BF16)
            psv = mk.psum(stk1, "a_psv", [128, 2, 512], F32)
            pg = mk.psum(stk1, "a_pg", [128, 32], F32)
            pA = {"vz": 0, "vv": 0, "vrk": 0, "vg": 0, "trz": 0, "trr": 0,
                  "tv": 0, "tg": 0, "vrkcm": 0}
            gmarks = {}
            rk_r = rk_hbm.rearrange("(j p) t -> p j t", p=128)

            def vzero(eng):
                eng.memset(rkcmt[:, 0, :, :], 0.0)
                mk.dr(eng)
                mk.mark(eng.memset(rkcmt[:, 1, :, :], 0.0), "Vm")
            mk.v(vzero)
            vz0 = mk.cnt["Vm"]

            def ypad(eng, vz0=vz0):
                if SS[2].guard is not None:
                    eng.wait_ge(mk.sems[SS[2].guard[0]], SS[2].guard[1])
                    SS[2].guard = None
                eng.wait_ge(mk.sems["Vm"], vz0)
                mk.mark(eng.dma_start(out=rk_r[:, :, 0:128],
                                      in_=rkcmt[:, 0, :, :]), "St2", 16)
            mk.g(ypad)
            SS[2].guard = ("St2", mk.cnt["St2"])

            for i in range(NT):
                sl = LS[6 + i % 4]
                qv = [None]

                def yf(eng, i=i, sl=sl, qv=qv):
                    qv[0] = sl.dma(eng, xin1[:, i % 4, :],
                                   x_in[i * 128:(i + 1) * 128, :])
                mk.y(yf)

                def sf(eng, i=i, sl=sl, qv=qv, p=dict(pA)):
                    eng.wait_ge(mk.sems[sl.name], qv[0])
                    mk.w(eng, "Vm")
                    mk.w(eng, "Tm", p["trz"])
                    mk.w(eng, "Vm", p["vrk"])
                    mk.dr(eng)
                    eng.activation(ztmp[:, i % 2, :], xin1[:, i % 4, :],
                                   AF.Identity, bias=negb[:, i:i + 1],
                                   scale=rstd[:, i:i + 1])
                    mk.dr(eng)
                    last = None
                    for h in range(H):
                        last = eng.activation(junka[:, h * 64:(h + 1) * 64],
                                              ztmp[:, i % 2, h * 64:(h + 1) * 64],
                                              AF.Square,
                                              accum_out=rss[:, h:h + 1])
                    mk.mark(last, "Sm")
                mk.s(sf)
                sl.consumed("Sm")
                smz = mk.cnt["Sm"]

                def vsc(eng, i=i, smz=smz):
                    eng.wait_ge(mk.sems["Sm"], smz)
                    mk.dr(eng)
                    eng.tensor_scalar(rsc[:, :], rss[:, :], 1e-24, None, OP.max)
                    mk.dr(eng)
                    mk.mark(eng.reciprocal(rsc[:, :], rsc[:, :]), "Vm")
                mk.v(vsc)
                vscm = mk.cnt["Vm"]

                def ssc(eng, i=i, vscm=vscm):
                    eng.wait_ge(mk.sems["Vm"], vscm)
                    mk.dr(eng)
                    mk.mark(eng.activation(rsc[:, :], rsc[:, :], AF.Sqrt), "Sm")
                mk.s(ssc)
                sscm = mk.cnt["Sm"]

                def vrk(eng, i=i, sscm=sscm, p=dict(pA)):
                    eng.wait_ge(mk.sems["Sm"], sscm)
                    eng.wait_ge(mk.sems["Tm"], p["trr"])
                    mk.dr(eng)
                    last = None
                    for h in range(H):
                        last = eng.tensor_scalar(
                            rktok[:, i % 2, h * 64:(h + 1) * 64],
                            ztmp[:, i % 2, h * 64:(h + 1) * 64],
                            rsc[:, h:h + 1], None, OP.mult)
                    mk.mark(last, "Vm")
                mk.v(vrk)
                pA["vrk"] = mk.cnt["Vm"]
                vrkm = pA["vrk"]

                def trz(eng, i=i, smz=smz, p=dict(pA)):
                    eng.wait_ge(mk.sems["Sm"], smz)
                    mk.w(eng, "Vm", p["vz"])
                    last = None
                    for j in range(NC):
                        last = eng.transpose(ptrz[:, j, :],
                                             ztmp[:, i % 2, j * 128:(j + 1) * 128],
                                             idf[:])
                    mk.mark(last, "Tm")
                mk.t(trz)
                pA["trz"] = mk.cnt["Tm"]
                trzm = pA["trz"]

                def vz(eng, i=i, trzm=trzm, p=dict(pA)):
                    eng.wait_ge(mk.sems["Tm"], trzm)
                    eng.wait_ge(mk.sems["Tm"], p["tv"])
                    eng.wait_ge(mk.sems["Tm"], p["tg"])
                    mk.dr(eng)
                    mk.mark(eng.tensor_copy(zcb[:, i % 2, :, :], ptrz[:, :, :]),
                            "Vm")
                mk.v(vz)
                pA["vz"] = mk.cnt["Vm"]
                vzm = pA["vz"]

                def trr(eng, i=i, vrkm=vrkm, p=dict(pA)):
                    eng.wait_ge(mk.sems["Vm"], vrkm)
                    mk.w(eng, "Vm", p["vrkcm"])
                    last = None
                    for j in range(NC):
                        last = eng.transpose(ptrr[:, j, :],
                                             rktok[:, i % 2, j * 128:(j + 1) * 128],
                                             idb[:])
                    mk.mark(last, "Tm")
                mk.t(trr)
                pA["trr"] = mk.cnt["Tm"]
                trrm = pA["trr"]

                srk = SS[2 + i % 2]

                def vrkcm(eng, i=i, trrm=trrm, srk=srk):
                    eng.wait_ge(mk.sems["Tm"], trrm)
                    if srk.guard is not None:
                        eng.wait_ge(mk.sems[srk.guard[0]], srk.guard[1])
                        srk.guard = None
                    mk.dr(eng)
                    mk.mark(eng.tensor_copy(rkcmt[:, i % 2, :, :],
                                            ptrr[:, :, :]), "Vm")
                mk.v(vrkcm)
                pA["vrkcm"] = mk.cnt["Vm"]
                vcmk = pA["vrkcm"]

                def yrk(eng, i=i, srk=srk, vcmk=vcmk):
                    eng.wait_ge(mk.sems["Vm"], vcmk)
                    mk.mark(eng.dma_start(
                        out=rk_r[:, :, 128 + i * 128:128 + (i + 1) * 128],
                        in_=rkcmt[:, i % 2, :, :]), srk.name, 16)
                mk.g(yrk)
                srk.guard = (srk.name, mk.cnt[srk.name])

                def tv(eng, i=i, vzm=vzm, p=dict(pA)):
                    eng.wait_ge(mk.sems["Vm"], vzm)
                    mk.w(eng, "Vm", p["vv"])
                    last = None
                    for half in range(2):
                        for j in range(NC):
                            last = eng.matmul(
                                psv[:, half, :], zcb[:, i % 2, j, :],
                                wvb[:, j, half * 512:(half + 1) * 512],
                                start=(j == 0), stop=(j == NC - 1))
                    mk.mark(last, "Tm")
                mk.t(tv)
                pA["tv"] = mk.cnt["Tm"]
                tvm = pA["tv"]

                sv = SS[i % 2]

                def vv(eng, i=i, tvm=tvm, sv=sv):
                    eng.wait_ge(mk.sems["Tm"], tvm)
                    if sv.guard is not None:
                        eng.wait_ge(mk.sems[sv.guard[0]], sv.guard[1])
                        sv.guard = None
                    mk.dr(eng)
                    mk.mark(eng.tensor_copy(
                        vstg[:, i % 2, :],
                        psv[:, :, :].rearrange("p a b -> p (a b)")), "Vm")
                mk.v(vv)
                pA["vv"] = mk.cnt["Vm"]
                vvm = pA["vv"]

                def yv(eng, i=i, sv=sv, vvm=vvm):
                    eng.wait_ge(mk.sems["Vm"], vvm)
                    mk.mark(eng.dma_start(out=v_hbm[i, :, :],
                                          in_=vstg[:, i % 2, :]), sv.name, 16)
                mk.g(yv)
                sv.guard = (sv.name, mk.cnt[sv.name])

                def tg(eng, i=i, p=dict(pA)):
                    mk.w(eng, "Vm", p["vg"])
                    last = None
                    for j in range(NC):
                        last = eng.matmul(pg[:, :], zcb[:, i % 2, j, :],
                                          wgbb[:, j, :],
                                          start=(j == 0), stop=(j == NC - 1))
                    mk.mark(last, "Tm")
                mk.t(tg)
                pA["tg"] = mk.cnt["Tm"]
                tgm = pA["tg"]

                def vg(eng, i=i, tgm=tgm, gmarks=gmarks):
                    eng.wait_ge(mk.sems["Tm"], tgm)
                    ms = gmarks.get(i - 2)
                    if ms is not None:
                        eng.wait_ge(mk.sems[ms[0]], ms[1])
                    mk.dr(eng)
                    mk.mark(eng.tensor_copy(gstg[:, i % 2, :], pg[:, :]), "Vm")
                mk.v(vg)
                pA["vg"] = mk.cnt["Vm"]
                vgm = pA["vg"]
                gsem = "GS%d" % (i % 2)

                def yg(eng, i=i, vgm=vgm, gsem=gsem):
                    eng.wait_ge(mk.sems["Vm"], vgm)
                    mk.mark(eng.dma_start(out=g_hbm[i, :, :],
                                          in_=gstg[:, i % 2, :]), gsem, 16)
                mk.g(yg)
                gmarks[i] = (gsem, mk.cnt[gsem])

        mk.barrier()

        with ExitStack() as stk2:
            keb = mk.sb(stk2, "a_keb", [64, 2, 8, 128], BF16)
            kob = mk.sb(stk2, "a_kob", [64, 2, 8, 128], BF16)
            reb = mk.sb(stk2, "a_reb", [64, 2, 8, 128], BF16)
            rob = mk.sb(stk2, "a_rob", [64, 2, 8, 128], BF16)
            vld = mk.sb(stk2, "a_vld", [128, 2, D], BF16)
            gld = mk.sb(stk2, "a_gld", [128, 2, 32], F32)
            gsig = mk.sb(stk2, "a_gsig", [128, 2, 32], F32)
            ecm = mk.sb(stk2, "a_ecm", [64, 2, 16, 128], BF16)
            fgs = mk.sb(stk2, "a_fgs", [128, 2, 16, 64], BF16)
            gcs = mk.sb(stk2, "a_gcs", [64, 2, 16, 64], BF16)
            pts = mk.sb(stk2, "a_pts", [64, 2, 16, 64], BF16)
            sfp = mk.sb(stk2, "a_sfp", [64, 16, 64], F32)
            sbb = mk.sb(stk2, "a_sbb", [64, 16, 64], BF16)
            ktk = mk.sb(stk2, "a_ktk", [128, 2, 64], BF16)
            zbuf = mk.sb(stk2, "a_zbuf", [128, 2, 2, 128], BF16)
            lbuf = mk.sb(stk2, "a_lbuf", [128, 2, 128], BF16)
            mbuf = mk.sb(stk2, "a_mbuf", [128, 2, 128], BF16)
            uw = mk.sb(stk2, "a_uw", [128, 2, 128], BF16)
            tut = mk.sb(stk2, "a_tut", [128, 2, 128], BF16)
            otok = mk.sb(stk2, "a_otok", [128, 16, 64], BF16)
            ocms = mk.sb(stk2, "a_ocms", [128, 2, 8, 128], BF16)
            stmp = mk.sb(stk2, "a_stmp", [64, 16, 64], F32)
            psG = mk.psum(stk2, "a_psG", [128, 2, 128], F32)
            psZ = mk.psum(stk2, "a_psZ", [128, 2, 128], F32)
            psLM = mk.psum(stk2, "a_psLM", [128, 2, 2, 128], F32)
            ptb = mk.psum(stk2, "a_ptb", [128, 2, 2, 128], BF16)
            psO = mk.psum(stk2, "a_psO", [128, 8, 64], F32)
            psS = mk.psum(stk2, "a_psS", [64, 8, 64], F32)
            psOT = mk.psum(stk2, "a_psOT", [64, 8, 128], BF16)

            def vinit(eng):
                eng.memset(sfp[:, :, :], 0.0)
                mk.dr(eng)
                mk.mark(eng.memset(sbb[:, :, :], 0.0), "Vm")
            mk.v(vinit)

            gfence = [(g, mk.cnt[g]) for g in ("GS0", "GS1")]

            def gfen(eng, gfence=gfence):
                for g, v in gfence:
                    eng.wait_ge(mk.sems[g], v)
            mk.g(gfen)

            mk.barrier()

            S0 = mk.cnt["Sm"]

            def Sb(ci):
                return S0 + ci

            def kv16(ci):
                return 16 * (ci // 2 + 1)

            def kv32(ci):
                return 32 * (ci // 2 + 1)

            def oval(c):
                return 16 * (c // 2 + 1)

            KPAT = [[RKW, 64], [128 * RKW, 8], [1, 128]]
            VPAT = [[D, 128], [1, D]]
            GPAT = [[32, 128], [1, 32]]
            OPAT = [[T, 128], [128 * T, 8], [1, 128]]

            def g_unit(eng, c, ci, loop):
                cp = ci % 2
                if ci >= 2:
                    mk.w(eng, "AT4", 8 * (ci - 2) + 8)
                    mk.w(eng, "AV4", 8 * (ci - 2) + 8)
                    mk.w(eng, "Sm", Sb(ci - 2) + 1)
                kofs = 127 + c * 128
                rofs = 128 + c * 128
                for dst, qof, cof, sem in (
                        (keb, 0, kofs, "FKa" if cp == 0 else "FKb"),
                        (kob, 64 * RKW, kofs, "FKa" if cp == 0 else "FKb"),
                        (reb, 0, rofs, "FRa" if cp == 0 else "FRb"),
                        (rob, 64 * RKW, rofs, "FRa" if cp == 0 else "FRb")):
                    off = eng.compute_val(cof + qof) if loop else cof + qof
                    mk.mark(eng.dma_start(
                        out=dst[:, cp, :, :],
                        in_=bass.AP(rk_hbm.tensor, off, KPAT)), sem, 16)
                voff = c * (128 * D)
                off = eng.compute_val(voff) if loop else voff
                mk.mark(eng.dma_start(
                    out=vld[:, cp, :],
                    in_=bass.AP(v_hbm.tensor, off, VPAT)),
                    "FVa" if cp == 0 else "FVb", 16)
                goff = c * (128 * 32)
                off = eng.compute_val(goff) if loop else goff
                mk.mark(eng.dma_start(
                    out=gld[:, cp, :],
                    in_=bass.AP(g_hbm.tensor, off, GPAT)),
                    "FGa" if cp == 0 else "FGb", 16)
                if ci >= 2:
                    mk.w(eng, "AVC", 2 * ci - 2)
                    ooff = (c - 2) * 128
                    off = eng.compute_val(ooff) if loop else ooff
                    mk.mark(eng.dma_start(
                        out=bass.AP(oattn_hbm.tensor, off, OPAT),
                        in_=ocms[:, cp, :, :]),
                        "FOa" if cp == 0 else "FOb", 16)

            def s_unit(eng, c, ci, loop):
                cp = ci % 2
                mk.w(eng, "FGa" if cp == 0 else "FGb", kv16(ci))
                if ci >= 2:
                    mk.w(eng, "AVO", 2 * ci - 2)
                mk.dr(eng)
                mk.mark(eng.activation(gsig[:, cp, :], gld[:, cp, :],
                                       AF.Sigmoid), "Sm")

            def t_unit(eng, c, ci, loop):
                cp = ci % 2
                mk.w(eng, "FKa" if cp == 0 else "FKb", kv32(ci))
                mk.w(eng, "FRa" if cp == 0 else "FRb", kv32(ci))
                kbs = (keb, kob)
                rbs = (reb, rob)
                for g in range(8):
                    if g > 0 or ci >= 1:
                        mk.w(eng, "AV4", 8 * ci + g)
                    last = None
                    for ii in range(2):
                        eng.matmul(psG[:, ii, :], kbs[ii][:, cp, g, :],
                                   kbs[ii][:, cp, g, :], start=True, stop=True)
                    for ii in range(2):
                        last = eng.transpose(ptb[:, 0, ii, 0:64],
                                             kbs[ii][:, cp, g, :],
                                             idb[0:64, 0:64])
                    mk.mark(last, "AT1")
                    mk.w(eng, "AV1", 8 * ci + g + 1)
                    last = None
                    for ii in range(2):
                        last = eng.transpose(ptb[:, 1, ii, :], lbuf[:, ii, :],
                                             idb[:])
                    mk.mark(last, "AT2")
                    for lev in range(7):
                        mk.w(eng, "AV2" if lev == 0 else "AZ%d" % (lev - 1),
                             8 * ci + g + 1)
                        last = None
                        for ii in range(2):
                            last = eng.matmul(psZ[:, ii, :], lbuf[:, ii, :],
                                              zbuf[:, lev % 2, ii, :],
                                              start=True, stop=True)
                        if lev < 6:
                            for ii in range(2):
                                last = eng.matmul(psLM[:, 0, ii, :],
                                                  mbuf[:, ii, :],
                                                  lbuf[:, ii, :],
                                                  start=True, stop=True)
                            if lev < 5:
                                for ii in range(2):
                                    last = eng.matmul(psLM[:, 1, ii, :],
                                                      lbuf[:, ii, :],
                                                      mbuf[:, ii, :],
                                                      start=True, stop=True)
                        mk.mark(last, "AU%d" % lev)
                    mk.w(eng, "AZ6", 8 * ci + g + 1)
                    last = None
                    for ii in range(2):
                        last = eng.matmul(psG[:, ii, :], kbs[ii][:, cp, g, :],
                                          rbs[ii][:, cp, g, :],
                                          start=True, stop=True)
                    mk.mark(last, "AT3")
                    mk.w(eng, "AV3", 8 * ci + g + 1)
                    last = None
                    for ii in range(2):
                        eng.matmul(psG[:, ii, 0:64], tut[:, ii, :],
                                   uw[:, ii, 0:64], start=True, stop=True)
                        eng.matmul(psZ[0:64, ii, :], uw[:, ii, 64:128],
                                   tut[:, ii, :], start=True, stop=True)
                        eng.matmul(psLM[0:64, 0, ii, 0:64], ktk[:, ii, :],
                                   uw[:, ii, 0:64], start=True, stop=True)
                        last = eng.matmul(psLM[0:64, 0, ii, 64:128],
                                          uw[:, ii, 64:128], ktk[:, ii, :],
                                          start=True, stop=True)
                    mk.mark(last, "AT4")
                if ci >= 1:
                    cb = (ci - 1) % 2
                    for hb in range(2):
                        if hb == 0:
                            if ci == 1:
                                mk.w(eng, "AV4", 8)
                            else:
                                mk.w(eng, "AVS", ci - 1)
                        else:
                            mk.w(eng, "AVO", 2 * ci - 1)
                        last = None
                        for hh in range(8):
                            h = hb * 8 + hh
                            last = eng.matmul(psO[:, hh, :], ecm[:, cb, h, :],
                                              sbb[:, h, :], start=True,
                                              stop=True)
                        for hh in range(8):
                            h = hb * 8 + hh
                            last = eng.matmul(psS[:, hh, :], pts[:, cb, h, :],
                                              sbb[:, h, :], start=True,
                                              stop=True)
                        mk.mark(last, "ATE")
                    for hb in range(2):
                        if hb == 0:
                            mk.w(eng, "AVO", 2 * ci)
                        else:
                            mk.w(eng, "AVC", 2 * ci - 1)
                        last = None
                        for hh in range(8):
                            h = hb * 8 + hh
                            last = eng.transpose(psOT[:, hh, :], otok[:, h, :],
                                                 idb[:])
                        mk.mark(last, "ATO")

            def v_unit(eng, c, ci, loop):
                cp = ci % 2
                kbs = (keb, kob)
                rbs = (reb, rob)
                for g in range(8):
                    if g == 0:
                        mk.w(eng, "FVa" if cp == 0 else "FVb", kv16(ci))
                        mk.w(eng, "Sm", Sb(ci) + 1)
                    mk.w(eng, "AT1", 8 * ci + g + 1)
                    mk.dr(eng)
                    eng.tensor_copy(ktk[:, :, :], ptb[:, 0, :, 0:64])
                    mk.dr(eng)
                    for ii in range(2):
                        eng.tensor_tensor(lbuf[:, ii, :], psG[:, ii, :],
                                          umneg[:, :], OP.mult)
                    mk.dr(eng)
                    for ii in range(2):
                        h = 2 * g + ii
                        eng.tensor_scalar(lbuf[:, ii, :], lbuf[:, ii, :],
                                          gsig[:, cp, 16 + h:17 + h], None,
                                          OP.mult)
                    mk.dr(eng)
                    last = None
                    for ii in range(2):
                        h = 2 * g + ii
                        eng.tensor_copy(zbuf[:, 0, ii, 0:64],
                                        vld[:, cp, h * 64:(h + 1) * 64])
                        last = eng.tensor_copy(zbuf[:, 0, ii, 64:128],
                                               ktk[:, ii, :])
                    mk.mark(last, "AV1")
                    mk.w(eng, "AT2", 8 * ci + g + 1)
                    mk.dr(eng)
                    mk.mark(eng.tensor_copy(mbuf[:, :, :], ptb[:, 1, :, :]),
                            "AV2")
                    for lev in range(7):
                        mk.w(eng, "AU%d" % lev, 8 * ci + g + 1)
                        mk.dr(eng)
                        last = eng.tensor_tensor(
                            zbuf[:, 1 - lev % 2, :, :].rearrange(
                                "p a b -> p (a b)"),
                            zbuf[:, lev % 2, :, :].rearrange(
                                "p a b -> p (a b)"),
                            psZ[:, :, :].rearrange("p a b -> p (a b)"),
                            OP.add)
                        if lev < 6:
                            mk.dr(eng)
                            last = eng.tensor_copy(
                                lbuf[:, :, :].rearrange("p a b -> p (a b)"),
                                psLM[:, 0, :, :].rearrange("p a b -> p (a b)"))
                            if lev < 5:
                                mk.dr(eng)
                                last = eng.tensor_copy(
                                    mbuf[:, :, :].rearrange("p a b -> p (a b)"),
                                    psLM[:, 1, :, :].rearrange(
                                        "p a b -> p (a b)"))
                        mk.mark(last, "AZ%d" % lev)
                    mk.w(eng, "AT3", 8 * ci + g + 1)
                    mk.dr(eng)
                    for ii in range(2):
                        h = 2 * g + ii
                        eng.tensor_scalar(uw[:, ii, :], zbuf[:, 1, ii, :],
                                          gsig[:, cp, 16 + h:17 + h], None,
                                          OP.mult)
                    mk.dr(eng)
                    last = None
                    for ii in range(2):
                        last = eng.tensor_tensor(tut[:, ii, :], psG[:, ii, :],
                                                 udiag[:, :], OP.mult)
                    mk.mark(last, "AV3")
                    mk.w(eng, "AT4", 8 * ci + g + 1)
                    if ci >= 2 and g == 0:
                        mk.w(eng, "ATE", 2 * ci - 2)
                    mk.dr(eng)
                    last = None
                    for ii in range(2):
                        h = 2 * g + ii
                        eng.tensor_tensor(ecm[:, cp, h, :],
                                          rbs[ii][:, cp, g, :],
                                          psZ[0:64, ii, :], OP.subtract)
                        eng.tensor_scalar(fgs[:, cp, h, :], psG[:, ii, 0:64],
                                          gsig[:, cp, h:h + 1], None, OP.mult)
                        eng.tensor_copy(gcs[:, cp, h, :],
                                        psLM[0:64, 0, ii, 0:64])
                        last = eng.tensor_copy(pts[:, cp, h, :],
                                               psLM[0:64, 0, ii, 64:128])
                    mk.mark(last, "AV4")
                if ci >= 1:
                    cb = (ci - 1) % 2
                    for hb in range(2):
                        mk.w(eng, "ATE", 2 * ci - 1 + hb)
                        mk.dr(eng)
                        hs = slice(hb * 8, hb * 8 + 8)
                        for hh in range(8):
                            h = hb * 8 + hh
                            eng.scalar_tensor_tensor(
                                otok[:, h, :], psO[:, hh, :],
                                gsig[:, cb, h:h + 1], fgs[:, cb, h, :],
                                OP.mult, OP.add)
                        mk.dr(eng)
                        eng.tensor_tensor(
                            stmp[:, hs, :].rearrange("p a b -> p (a b)"),
                            gcs[:, cb, hs, :].rearrange("p a b -> p (a b)"),
                            psS[:, :, :].rearrange("p a b -> p (a b)"),
                            OP.subtract)
                        mk.dr(eng)
                        mk.mark(eng.tensor_tensor(
                            sfp[:, hs, :].rearrange("p a b -> p (a b)"),
                            sfp[:, hs, :].rearrange("p a b -> p (a b)"),
                            stmp[:, hs, :].rearrange("p a b -> p (a b)"),
                            OP.add), "AVO")
                    mk.dr(eng)
                    mk.mark(eng.tensor_copy(
                        sbb[:, :, :].rearrange("p a b -> p (a b)"),
                        sfp[:, :, :].rearrange("p a b -> p (a b)")), "AVS")
                    for hb in range(2):
                        mk.w(eng, "ATO", 2 * ci - 1 + hb)
                        if hb == 0 and ci >= 3:
                            mk.w(eng, "FOa" if (ci - 1) % 2 == 0 else "FOb",
                                 oval(ci - 3))
                        mk.dr(eng)
                        last = None
                        for hh in range(8):
                            h = hb * 8 + hh
                            last = eng.tensor_copy(
                                ocms[(h % 2) * 64:(h % 2) * 64 + 64, cb,
                                     h // 2, :], psOT[:, hh, :])
                        mk.mark(last, "AVC")

            for ci in range(4):
                mk.g(lambda eng, ci=ci: g_unit(eng, ci, ci, False))
                mk.s(lambda eng, ci=ci: s_unit(eng, ci, ci, False))
                mk.t(lambda eng, ci=ci: t_unit(eng, ci, ci, False))
                mk.v(lambda eng, ci=ci: v_unit(eng, ci, ci, False))
                for sname in STEP_T + STEP_V:
                    assert mk.cnt[sname] == 8 * (ci + 1), (ci, sname)
                for sname in ("ATE", "ATO", "AVO", "AVC"):
                    assert mk.cnt[sname] == 2 * ci, (ci, sname)
                assert mk.cnt["AVS"] == ci, (ci, "AVS")
                assert mk.cnt["Sm"] == Sb(ci + 1), (ci, mk.cnt["Sm"])

            NI = (NCH - 4) // 2
            rates = {"Sm": 2,
                     "FKa": 32, "FKb": 32, "FRa": 32, "FRb": 32,
                     "FVa": 16, "FVb": 16, "FGa": 16, "FGb": 16,
                     "FOa": 16, "FOb": 16}
            for sname in STEP_T + STEP_V:
                rates[sname] = 16
            for sname in ("ATE", "ATO", "AVO", "AVC"):
                rates[sname] = 4
            rates["AVS"] = 2
            mk.flush()
            snapc = dict(mk.cnt)
            for engname, unit in (("gpsimd", g_unit), ("scalar", s_unit),
                                  ("tensor", t_unit), ("vector", v_unit)):
                mk.cnt = dict(snapc)
                eng = getattr(nc, engname)
                with eng.Fori(0, NI) as it:
                    mk.loop_enter(rates, it)
                    for u in range(2):
                        unit(eng, it * 2 + 4 + u, 4 + u, True)
                    mk.loop_rates = None
            mk.cnt = dict(snapc)
            for k, r in rates.items():
                mk.cnt[k] = snapc[k] + NI * r
            mk.loop_iv = None

            mk.barrier()

            epi = {}

            def e_tes(eng, hb):
                if hb == 1:
                    mk.w(eng, "Vm", epi["vo0"])
                last = None
                for hh in range(8):
                    h = hb * 8 + hh
                    last = eng.matmul(psO[:, hh, :], ecm[:, 1, h, :],
                                      sbb[:, h, :], start=True, stop=True)
                mk.mark(last, "Tm")
                epi["tes%d" % hb] = mk.cnt["Tm"]

            def e_vo(eng, hb):
                mk.w(eng, "Tm", epi["tes%d" % hb])
                mk.dr(eng)
                last = None
                for hh in range(8):
                    h = hb * 8 + hh
                    last = eng.scalar_tensor_tensor(
                        otok[:, h, :], psO[:, hh, :], gsig[:, 1, h:h + 1],
                        fgs[:, 1, h, :], OP.mult, OP.add)
                mk.mark(last, "Vm")
                epi["vo%d" % hb] = mk.cnt["Vm"]

            def e_tot(eng, hb):
                mk.w(eng, "Vm", epi["vo1"] if hb == 0 else epi["voc0"])
                last = None
                for hh in range(8):
                    h = hb * 8 + hh
                    last = eng.transpose(psOT[:, hh, :], otok[:, h, :], idb[:])
                mk.mark(last, "Tm")
                epi["tot%d" % hb] = mk.cnt["Tm"]

            def e_voc(eng, hb):
                mk.w(eng, "Tm", epi["tot%d" % hb])
                if hb == 0:
                    mk.w(eng, "FOb", 240)
                mk.dr(eng)
                last = None
                for hh in range(8):
                    h = hb * 8 + hh
                    last = eng.tensor_copy(
                        ocms[(h % 2) * 64:(h % 2) * 64 + 64, 1, h // 2, :],
                        psOT[:, hh, :])
                mk.mark(last, "Vm")
                epi["voc%d" % hb] = mk.cnt["Vm"]

            mk.t(lambda eng: e_tes(eng, 0))
            mk.v(lambda eng: e_vo(eng, 0))
            mk.t(lambda eng: e_tes(eng, 1))
            mk.v(lambda eng: e_vo(eng, 1))
            mk.t(lambda eng: e_tot(eng, 0))
            mk.v(lambda eng: e_voc(eng, 0))
            mk.t(lambda eng: e_tot(eng, 1))
            mk.v(lambda eng: e_voc(eng, 1))

            oat_r = oattn_hbm.rearrange("(j p) t -> p j t", p=128)

            def e_st(eng):
                eng.wait_ge(mk.sems["FOa"], 240)
                mk.mark(eng.dma_start(out=oat_r[:, :, 30 * 128:31 * 128],
                                      in_=ocms[:, 0, :, :]), "FOa", 16)
                mk.w(eng, "Vm", epi["voc1"])
                mk.mark(eng.dma_start(out=oat_r[:, :, 31 * 128:32 * 128],
                                      in_=ocms[:, 1, :, :]), "FOb", 16)
                eng.wait_ge(mk.sems["FOa"], 256)
                eng.wait_ge(mk.sems["FOb"], 256)
            mk.g(e_st)

        mk.barrier()

        mix_store_marks = {}
        with ExitStack() as stk3:
            emit_outproj(mk, stk3, "a_op", oattn_hbm, W["wo_t"], mixbuf,
                         mxsum, mxsq, {}, mix_store_marks)
        mk.barrier()
        with ExitStack() as stk4:
            emit_residual_update(mk, stk4, "a_up", x_in, mixbuf, x_out,
                                 mxsum, mxsq, LS[0:6], SS[0:3], mix_store_marks)




_PROGS = {}


def _build_a(prep):
    mk = MK()
    nc = mk.nc
    xd = nc.dram_tensor("x", [T, D], F32, kind="ExternalInput").ap()
    yd = nc.dram_tensor("y", [T, D], F32, kind="ExternalOutput").ap()
    oattn = nc.dram_tensor("oattn_scr", [D, T], BF16).ap()
    mixbuf = nc.dram_tensor("mixbuf", [T, D], F32).ap()
    rk = nc.dram_tensor("rk_scr", [D, T + 128], BF16).ap()
    vh = nc.dram_tensor("v_scr", [NT, 128, D], BF16).ap()
    gh = nc.dram_tensor("g_scr", [NT, 128, 32], F32).ap()
    W = {k: nc.dram_tensor("l0a_" + k, list(v.shape), F32, kind="ExternalInput").ap()
         for k, v in prep.items()}
    W["ident"] = nc.dram_tensor("ident", [128, 128], F32, kind="ExternalInput").ap()
    mk.start()
    phase_l0attn_f(mk, xd, yd, W, mixbuf, oattn, rk, vh, gh)
    mk.finish()
    return nc


def _build_b(prep):
    mk = MK()
    nc = mk.nc
    xd = nc.dram_tensor("x", [T, D], F32, kind="ExternalInput").ap()
    yd = nc.dram_tensor("y", [T, D], F32, kind="ExternalOutput").ap()
    mixbuf = nc.dram_tensor("mixbuf", [T, D], F32).ap()
    mixed_hbm = nc.dram_tensor("mixed_scr", [NT, 4096, 128], BF16).ap()
    nx_hbm = nc.dram_tensor("nx_scr", [D, T], BF16).ap()
    W = {k: nc.dram_tensor("l0f_" + k, list(v.shape), F32, kind="ExternalInput").ap()
         for k, v in prep.items()}
    W["ident"] = nc.dram_tensor("ident", [128, 128], F32, kind="ExternalInput").ap()
    mk.start()
    phase_l0ffn(mk, xd, yd, W, mixbuf, mixed_hbm, nx_hbm)
    mk.finish()
    return nc


def _build_c(prep):
    mk = MK()
    nc = mk.nc
    xd = nc.dram_tensor("x", [T, D], F32, kind="ExternalInput").ap()
    yd = nc.dram_tensor("y", [T, D], F32, kind="ExternalOutput").ap()
    mixbuf = nc.dram_tensor("mixbuf", [T, D], F32).ap()
    qk_hbm = nc.dram_tensor("qk_scr", [2048, T], BF16).ap()
    attn_hbm = nc.dram_tensor("attn_scr", [1024, T], BF16).ap()
    W = {k: nc.dram_tensor("l1a_" + k, list(v.shape), F32, kind="ExternalInput").ap()
         for k, v in prep.items()}
    W["ident"] = nc.dram_tensor("ident", [128, 128], F32, kind="ExternalInput").ap()
    W["ones"] = nc.dram_tensor("ones", [128, 128], F32, kind="ExternalInput").ap()
    mk.start()
    phase_l1attn(mk, xd, yd, W, mixbuf, qk_hbm, attn_hbm)
    mk.finish()
    return nc


def _build_d(prep):
    mk = MK()
    nc = mk.nc
    xd = nc.dram_tensor("x", [T, D], F32, kind="ExternalInput").ap()
    yd = nc.dram_tensor("y", [T, D], F32, kind="ExternalOutput").ap()
    mixbuf = nc.dram_tensor("mixbuf", [T, D], F32).ap()
    W = {k: nc.dram_tensor("l1f_" + k, list(v.shape), F32, kind="ExternalInput").ap()
         for k, v in prep.items()}
    W["ident"] = nc.dram_tensor("ident", [128, 128], F32, kind="ExternalInput").ap()
    mk.start()
    phase_l1ffn(mk, xd, yd, W, mixbuf)
    mk.finish()
    return nc


def kernel(L0_attn_x, L0_attn_Wv, L0_attn_Wo, L0_attn_Wg, L0_attn_Wb,
           L0_ffn_norm_w, L0_ffn_w1, L0_ffn_w3, L0_ffn_w2, L0_ffn_gate_w,
           L0_ffn_a1, L0_ffn_b1, L0_ffn_a3, L0_ffn_b3, L0_ffn_a2, L0_ffn_b2,
           L1_attn_w_qkv, L1_attn_w_out, L1_ffn_W_r, L1_ffn_W_k, L1_ffn_W_v,
           L1_ffn_mu_r, L1_ffn_mu_k):
    ncores = 8
    xs = [np.ascontiguousarray(np.asarray(L0_attn_x)[i].astype(np.float32,
                                                               copy=False))
          for i in range(ncores)]
    pa = prep_l0attn(np.asarray(L0_attn_Wv), np.asarray(L0_attn_Wo),
                     np.asarray(L0_attn_Wg), np.asarray(L0_attn_Wb))
    pb = prep_l0ffn(np.asarray(L0_ffn_norm_w), np.asarray(L0_ffn_w1),
                    np.asarray(L0_ffn_w3), np.asarray(L0_ffn_w2),
                    np.asarray(L0_ffn_gate_w), np.asarray(L0_ffn_a1),
                    np.asarray(L0_ffn_b1), np.asarray(L0_ffn_a3),
                    np.asarray(L0_ffn_b3), np.asarray(L0_ffn_a2),
                    np.asarray(L0_ffn_b2))
    pc = prep_l1attn(np.asarray(L1_attn_w_qkv), np.asarray(L1_attn_w_out))
    pd = prep_l1ffn(np.asarray(L1_ffn_W_r), np.asarray(L1_ffn_W_k),
                    np.asarray(L1_ffn_W_v), np.asarray(L1_ffn_mu_r),
                    np.asarray(L1_ffn_mu_k))
    ident = np.eye(128, dtype=np.float32)
    ones = np.ones((128, 128), dtype=np.float32)

    if "a" not in _PROGS:
        _PROGS["a"] = _build_a(pa)
        _PROGS["b"] = _build_b(pb)
        _PROGS["c"] = _build_c(pc)
        _PROGS["d"] = _build_d(pd)

    core_ids = list(range(ncores))

    base_a = {"ident": ident}
    for k, v in pa.items():
        base_a["l0a_" + k] = v
    maps = [dict(base_a, x=xs[i]) for i in range(ncores)]
    r1 = run_bass_kernel_spmd(_PROGS["a"], maps, core_ids=core_ids)
    x1 = [r1.results[i]["y"] for i in range(ncores)]

    base_b = {"ident": ident}
    for k, v in pb.items():
        base_b["l0f_" + k] = v
    maps = [dict(base_b, x=x1[i]) for i in range(ncores)]
    r3 = run_bass_kernel_spmd(_PROGS["b"], maps, core_ids=core_ids)
    x2 = [r3.results[i]["y"] for i in range(ncores)]

    base_c = {"ident": ident, "ones": ones}
    for k, v in pc.items():
        base_c["l1a_" + k] = v
    maps = [dict(base_c, x=x2[i]) for i in range(ncores)]
    r4 = run_bass_kernel_spmd(_PROGS["c"], maps, core_ids=core_ids)
    x3 = [r4.results[i]["y"] for i in range(ncores)]

    base_d = {"ident": ident}
    for k, v in pd.items():
        base_d["l1f_" + k] = v
    maps = [dict(base_d, x=x3[i]) for i in range(ncores)]
    r5 = run_bass_kernel_spmd(_PROGS["d"], maps, core_ids=core_ids)

    out = np.stack([np.asarray(r5.results[i]["y"]) for i in range(ncores)], axis=0)
    return out.astype(np.float32, copy=False)
